# version 01
# baseline (speedup 1.0000x reference)
import math
from contextlib import ExitStack

import numpy as np
import ml_dtypes

import concourse.bass as bass
import concourse.mybir as mybir
from concourse.bass_utils import run_bass_kernel_spmd

F32 = mybir.dt.float32
BF16 = mybir.dt.bfloat16
FP8 = mybir.dt.float8e4
ALU = mybir.AluOpType
ACT = mybir.ActivationFunctionType
AX = mybir.AxisListType

B, S, H = 8, 2048, 2048
NH, HD, R, C, WIN = 16, 128, 16, 128, 128
ROPE, HALF = 64, 32
EPS_RMS, EPS_LN = 1e-6, 1e-5
II = 8192
SCALE = 1.0 / math.sqrt(HD)
NEG = -1.0e9
NT = 16
NSUP, TSUP = 4, 512
NG, HPG = 4, 4
NHC = 16
C224 = float(np.float32(1.0) / np.float32(224.0))
C128 = 1.0 / 128.0
CH = 1.0 / float(H)


def host_consts():
    cs = {}
    cs["c_ident"] = np.eye(128, dtype=np.float32).astype(ml_dtypes.bfloat16)
    cs["c_identf"] = np.eye(128, dtype=np.float32)
    swap = np.zeros((64, 64), np.float32)
    swap[:32, 32:] = np.eye(32)
    swap[32:, :32] = np.eye(32)
    cs["c_swap"] = swap.astype(ml_dtypes.bfloat16)
    inv_freq = 1.0 / 10000.0 ** (np.arange(HALF, dtype=np.float64) * 2.0 / ROPE)
    pos = np.arange(S, dtype=np.float64)
    ang = pos[None, :] * inv_freq[:, None]
    cs["c_cosd"] = np.concatenate([np.cos(ang)] * 2, 0).astype(np.float32).astype(ml_dtypes.bfloat16)
    cs["c_sins"] = np.concatenate([-np.sin(ang), np.sin(ang)], 0).astype(np.float32).astype(ml_dtypes.bfloat16)
    pc = np.arange(C, dtype=np.float64) * R + (R - 1)
    angc = pc[None, :] * inv_freq[:, None]
    cs["c_cosdc"] = np.concatenate([np.cos(angc)] * 2, 0).astype(np.float32).astype(ml_dtypes.bfloat16)
    cs["c_sinsc"] = np.concatenate([-np.sin(angc), np.sin(angc)], 0).astype(np.float32).astype(ml_dtypes.bfloat16)
    qi = np.arange(128)[:, None]
    j = np.arange(256)[None, :]
    gen = np.where((j >= qi) & (j <= qi + WIN), 0.0, NEG)
    t0 = np.where((j >= 128) & (j - 128 <= qi), 0.0, NEG)
    cs["c_lmask"] = np.concatenate([t0, gen], 1).astype(np.float32)
    cm = np.zeros((128, NT, 129), np.float32)
    cidx = np.arange(C)[None, :]
    for tt in range(NT):
        cm[:, tt, 1:] = np.where(cidx * R + (R - 1) <= tt * 128 + qi, 0.0, NEG)
    cs["c_cmask"] = cm.reshape(128, NT * 129)
    g = np.zeros((NT, 128, 128), np.float32)
    for tt in range(NT):
        for lt in range(128):
            g[tt, lt, 8 * tt + lt // 16] = 1.0
    cs["c_g"] = g.transpose(1, 0, 2).reshape(128, NT * 128).astype(ml_dtypes.bfloat16)
    e = np.zeros((128, 8, 8), np.float32)
    for rr in range(8):
        e[:, rr, rr] = 1.0
    cs["c_e"] = e.reshape(128, 64).astype(ml_dtypes.bfloat16)
    f = np.zeros((8, 8, 128), np.float32)
    for rr in range(8):
        f[rr, rr, :] = 1.0
    cs["c_f"] = f.reshape(8, 8 * 128).astype(ml_dtypes.bfloat16)
    return cs


class Sched:
    ENGINES = ("tensor", "vector", "scalar", "gpsimd", "sync")

    def __init__(self):
        self.steps = {e: [] for e in self.ENGINES}
        self.counts = {}

    def emit(self, engine, fn, waits=(), inc=None, inc_n=1):
        incs = []
        if inc is not None:
            self.counts[inc] = self.counts.get(inc, 0) + inc_n
            incs = [(inc, inc_n)]
        self.steps[engine].append(
            (fn, [(s, v) for (s, v) in waits if v > 0], incs))

    def dma(self, engine, fn, waits=(), inc="dS"):
        self.emit(engine, fn, waits, inc, inc_n=16)

    def now(self, sem):
        return self.counts.get(sem, 0)

    def build(self, block, semmap):
        for ename in self.ENGINES:
            steps = self.steps[ename]
            if not steps:
                continue

            def make_body(steps=list(steps)):
                def body(eng):
                    for fn, waits, incs in steps:
                        for s, v in waits:
                            eng.wait_ge(semmap[s], v)
                        inst = fn(eng)
                        for s, n in incs:
                            assert inst is not None
                            inst.then_inc(semmap[s], n)
                return body
            getattr(block, ename)(make_body())


class KB:
    def __init__(self, debug=(), phases=("attn", "ffn"), paranoid=False):
        self.debug = set(debug)
        self.phases = set(phases)
        self.paranoid = paranoid
        nc = self.nc = bass.Bass("TRN2", target_bir_lowering=False)
        self.sch = Sched()
        if paranoid:
            _orig_emit = self.sch.emit
            def _pemit(engine, fn, waits=(), inc=None, inc_n=1):
                if engine == "vector" and inc_n == 1:
                    _orig_emit("vector", lambda e: e.tensor_copy(
                        out=self.junk[:, 0:1], in_=self.junk[:, 1:2]))
                _orig_emit(engine, fn, waits, inc, inc_n)
            self.sch.emit = _pemit
        self.stk = ExitStack()
        self.dbg = {}

        D = lambda n, sh: nc.dram_tensor(n, sh, F32, kind="ExternalInput")
        self.i_x = D("i_x", [S, H])
        self.i_wq, self.i_wlk = D("i_wq", [H, H]), D("i_wlk", [H, H])
        self.i_wlv, self.i_wo = D("i_wlv", [H, H]), D("i_wo", [H, H])
        self.i_wk, self.i_wv = D("i_wk", [H, HD]), D("i_wv", [H, HD])
        self.i_wc = D("i_wc", [128, NHC])
        self.i_bq, self.i_blk = D("i_bq", [128, NHC]), D("i_blk", [128, NHC])
        self.i_blv, self.i_bo = D("i_blv", [H]), D("i_bo", [H])
        self.i_bk, self.i_bv, self.i_bc = D("i_bk", [HD]), D("i_bv", [HD]), D("i_bc", [1])
        self.i_qnw, self.i_knw = D("i_qnw", [HD]), D("i_knw", [HD])
        self.i_sk, self.i_sv = D("i_sk", [128, NH]), D("i_sv", [NH, HD])
        self.i_gw, self.i_uw = D("i_gw", [II, H]), D("i_uw", [II, H])
        self.i_dw = D("i_dw", [H, II])
        self.cs = host_consts()
        self.ch = {}
        for k, v in self.cs.items():
            dt = BF16 if v.dtype == ml_dtypes.bfloat16 else F32
            self.ch[k] = nc.dram_tensor(k, list(v.shape), dt, kind="ExternalInput")

        self.o_y = nc.dram_tensor("o_y", [S, H], F32, kind="ExternalOutput")
        DI = lambda n, sh, dt: nc.dram_tensor(n, sh, dt, kind="Internal")
        self.d_xbf = DI("d_xbf", [S, H], BF16)
        self.d_wrow = DI("d_wrow", [S], F32)
        self.d_ctx = DI("d_ctx", [H, S], BF16)
        self.d_x1 = DI("d_x1", [S, H], F32)
        self.d_xq = DI("d_xq", [S, H], BF16)
        self.d_r2 = DI("d_r2", [S, H], F32)
        self.d_wqb = DI("d_wqb", [H, H], BF16)
        self.d_wlkb = DI("d_wlkb", [H, H], BF16)
        self.d_wlvb = DI("d_wlvb", [H, H], BF16)
        self.d_wob = DI("d_wob", [H, H], BF16)
        self.d_wkb = DI("d_wkb", [H, HD], BF16)
        self.d_wvb = DI("d_wvb", [H, HD], BF16)
        self.d_rows = DI("d_rows", [4 * H], BF16)
        self.d_gw = DI("d_gw", [II, H], BF16)
        self.d_uw = DI("d_uw", [II, H], BF16)
        self.d_dw = DI("d_dw", [H, II], BF16)

        self.semmap = {n: self.stk.enter_context(nc.semaphore("sem_" + n))
                       for n in ("t", "v", "s", "g", "dS", "dG")}

        PS = lambda n, sh, dt: self.stk.enter_context(nc.psum_tensor(n, sh, dt))
        self.ps_big = PS("ps_big", [128, 4, 512], F32)
        self.pb4 = PS("pb4", [128, 512], F32)
        self.pb5 = PS("pb5", [128, 512], F32)
        self.pt0 = PS("pt0", [128, 1024], BF16)
        self.pb7 = PS("pb7", [128, 512], F32)

        SBs = self.SBs = lambda n, sh, dt: self.stk.enter_context(
            nc.sbuf_tensor(n, sh, dt))
        self.ident = SBs("ident", [128, 128], BF16)
        self.ones_row = SBs("ones_row", [1, 512], BF16)
        self.rows_b = SBs("rows_b", [1, 4 * H], BF16)
        self.xt = [SBs("xt_a", [128, H], F32), SBs("xt_b", [128, H], F32)]
        self.x1_t = SBs("x1_t", [128, H], F32)
        self.cols = SBs("cols", [128, 16], F32)
        self.ln_s = SBs("ln_s", [128, NT], F32)
        self.ln_q = SBs("ln_q", [128, NT], F32)
        self.junk = SBs("junk", [128, 4], F32)
        self.xfm = SBs("xfm", [128, NHC, TSUP], BF16)
        self.wcol = SBs("wcol", [128, NT], F32)
        self.sc_attn = ExitStack()
        SBa = self.SBa = lambda n, sh, dt: self.sc_attn.enter_context(
            nc.sbuf_tensor(n, sh, dt))
        self.swapm = SBa("swapm", [64, 64], BF16)
        self.cosd = SBa("cosd", [64, S], BF16)
        self.sins = SBa("sins", [64, S], BF16)
        self.cosdc = SBa("cosdc", [64, C], BF16)
        self.sinsc = SBa("sinsc", [64, C], BF16)
        self.lmask = SBa("lmask", [128, 512], F32)
        self.cmask = SBa("cmask", [128, NT * 129], F32)
        self.emat = SBa("emat", [128, 64], BF16)
        self.fmat = SBa("fmat", [8, 8 * 128], BF16)
        self.wc_bf = SBa("wc_bf", [128, NHC], BF16)
        self.wc_f = SBa("wc_f", [128, NHC], F32)
        self.bq_c = SBa("bq_c", [128, NHC], F32)
        self.blk_c = SBa("blk_c", [128, NHC], F32)
        self.bk_c = SBa("bk_c", [128, 1], F32)
        self.qnw_c = SBa("qnw_c", [128, 1], F32)
        self.knw_c = SBa("knw_c", [128, 1], F32)
        self.bc_c = SBa("bc_c", [1, 1], F32)
        self.bc_bf = SBa("bc_bf", [1, 1], BF16)
        self.bc128 = SBa("bc128", [128, 1], F32)
        self.sink_f = SBa("sink_f", [128, NH], F32)
        self.sinkb = SBa("sinkb", [128, NH], BF16)
        self.scb = SBa("scb", [128, NH, 129], BF16)
        self.cvb = SBa("cvb", [128, HD], BF16)
        self.lcol = SBa("lcol", [128, NT], F32)
        self.ident_f = SBa("ident_f", [128, 128], F32)
        self.lT = SBa("lT", [16, 128], F32)
        self.lm8 = SBa("lm8", [16, 8], F32)
        self.ls8 = SBa("ls8", [16, 8], F32)
        if "wrow" in self.debug:
            self.wrow_st = SBa("wrow_st", [1, S], F32)

        self.W = lambda *ss: [(s, self.sch.now(s)) for s in ss]
        self.a4_store_ds = []

    def mkdbg(self, name, shape):
        if name in self.debug:
            self.dbg[name] = self.nc.dram_tensor("dbg_" + name, shape, F32,
                                                 kind="ExternalOutput")
            return True
        return False

    def gap(self, n=3):
        for _ in range(n):
            self.sch.emit("vector", lambda e: e.tensor_copy(
                out=self.junk[:, 0:1], in_=self.junk[:, 1:2]))

    def build(self):
        sch, W = self.sch, self.W
        self.phase0_consts()
        self.sc_f0 = ExitStack()
        if "ffn" in self.phases:
            self.phase_f0()
        self.phase0_logits()
        with ExitStack() as sc0:
            self.phase0b(sc0)
        with ExitStack() as scA:
            self.phase_groups(scA)
        with ExitStack() as scA4:
            self.phase_a4(scA4)
        self.sc_f0.close()
        self.sc_attn.close()
        if "ffn" in self.phases:
            with ExitStack() as scF:
                self.phase_f(scF)
        if "xqd" in self.debug:
            self.dbg["xqd"] = self.nc.dram_tensor("dbg_xqd", [S, H], F32,
                                                  kind="ExternalOutput")
            sch.dma("gpsimd", lambda e: e.dma_start(out=self.dbg["xqd"].ap(),
                                                    in_=self.d_xq.ap()),
                    waits=W("dS", "v"), inc="dG")
        if "r2d" in self.debug:
            self.dbg["r2d"] = self.nc.dram_tensor("dbg_r2d", [S, H], F32,
                                                  kind="ExternalOutput")
            sch.dma("gpsimd", lambda e: e.dma_start(out=self.dbg["r2d"].ap(),
                                                    in_=self.d_r2.ap()),
                    waits=W("dS", "v"), inc="dG")
        if "lns" in self.debug:
            self.dbg["lns"] = self.nc.dram_tensor("dbg_lns", [128, NT], F32,
                                                  kind="ExternalOutput")
            self.dbg["lnq"] = self.nc.dram_tensor("dbg_lnq", [128, NT], F32,
                                                  kind="ExternalOutput")
            sch.dma("sync", lambda e: e.dma_start(out=self.dbg["lns"].ap(),
                                                  in_=self.ln_s[:]),
                    waits=W("v"))
            sch.dma("sync", lambda e: e.dma_start(out=self.dbg["lnq"].ap(),
                                                  in_=self.ln_q[:]),
                    waits=W("v"))
        sch.emit("gpsimd", lambda e: e.nop(),
                 waits=W("dS", "dG", "v", "t", "s", "g"))
        with self.stk:
            blk = self.stk.enter_context(self.nc.Block())
            sch.build(blk, self.semmap)
        return self.nc

    def phase0_consts(self):
        sch = self.sch
        ld = lambda out, in_: sch.dma("sync", lambda e, o=out, i=in_: e.dma_start(
            out=o, in_=i))
        ch = self.ch
        ld(self.ident[:], ch["c_ident"].ap())
        ld(self.swapm[:], ch["c_swap"].ap())
        ld(self.cosd[:], ch["c_cosd"].ap())
        ld(self.sins[:], ch["c_sins"].ap())
        ld(self.cosdc[:], ch["c_cosdc"].ap())
        ld(self.sinsc[:], ch["c_sinsc"].ap())
        ld(self.lmask[:], ch["c_lmask"].ap())
        ld(self.cmask[:], ch["c_cmask"].ap())
        ld(self.emat[:], ch["c_e"].ap())
        ld(self.ident_f[:], ch["c_identf"].ap())
        ld(self.fmat[:], ch["c_f"].ap())
        ld(self.wc_f[:], self.i_wc.ap())
        ld(self.bq_c[:], self.i_bq.ap())
        ld(self.blk_c[:], self.i_blk.ap())
        ld(self.bk_c[:], self.i_bk.ap().rearrange("(o p) -> p o", p=128))
        ld(self.qnw_c[:], self.i_qnw.ap().rearrange("(o p) -> p o", p=128))
        ld(self.knw_c[:], self.i_knw.ap().rearrange("(o p) -> p o", p=128))
        ld(self.bc_c[:], self.i_bc.ap().rearrange("(o p) -> p o", p=1))
        ld(self.sink_f[:], self.i_sk.ap())
        self.CONST_DS = sch.now("dS")
        lg = lambda out, in_: sch.dma("gpsimd", lambda e, o=out, i=in_:
                                      e.dma_start(out=o, in_=i), inc="dG")
        lg(self.d_rows.ap()[0:H], self.i_blv.ap())
        lg(self.d_rows.ap()[H:2 * H], self.i_bo.ap())
        lg(self.d_rows.ap()[2 * H:2 * H + HD], self.i_bv.ap())
        lg(self.d_rows.ap()[3 * H:4 * H], self.i_sv.ap().rearrange("h d -> (h d)"))
        rg = sch.now("dG")
        sch.dma("sync", lambda e: e.dma_start(
            out=self.rows_b[:], in_=self.d_rows.ap().rearrange("(o h) -> o h", o=1)),
            waits=[("dG", rg)])
        lg(self.d_wkb.ap(), self.i_wk.ap())
        lg(self.d_wvb.ap(), self.i_wv.ap())
        self.WKVMIR_DG = sch.now("dG")
        lg(self.d_wqb.ap(), self.i_wq.ap())
        lg(self.d_wlkb.ap(), self.i_wlk.ap())
        lg(self.d_wlvb.ap(), self.i_wlv.ap())
        self.WQKV_DG = sch.now("dG")
        lg(self.d_wob.ap(), self.i_wo.ap())
        self.WO_DG = sch.now("dG")
        self.ROWS_DG = sch.now("dG")

        sch.dma("gpsimd", lambda e: e.dma_start(out=self.d_xbf.ap(),
                                                in_=self.i_x.ap()), inc="dG")
        self.XBF_DG = sch.now("dG")

        sch.emit("vector", lambda e: e.memset(self.ones_row[:], 1.0))
        sch.emit("vector", lambda e: e.memset(self.junk[:], 0.0))
        sch.emit("vector", lambda e: e.memset(self.ln_s[:], 0.0))
        sch.emit("vector", lambda e: e.memset(self.ln_q[:], 0.0))
        sch.emit("vector", lambda e: e.tensor_copy(out=self.sinkb[:],
                                                   in_=self.sink_f[:]),
                 waits=[("dS", self.CONST_DS)])
        sch.emit("vector", lambda e: e.tensor_copy(out=self.wc_bf[:],
                                                   in_=self.wc_f[:]))
        sch.emit("vector", lambda e: e.tensor_copy(out=self.bc_bf[:],
                                                   in_=self.bc_c[:]), inc="v")
        self.V_SMALL = sch.now("v")

    def phase_f0(self):
        sch = self.sch
        SB = lambda n, sh, dt: self.sc_f0.enter_context(
            self.nc.sbuf_tensor(n, sh, dt))
        FW = 1024
        NBW = FW // 32
        wtile = [SB("f0_wa", [128, FW], F32), SB("f0_wb", [128, FW], F32)]
        w8 = SB("f0_q8", [128, FW], FP8)
        wqd = SB("f0_qd", [128, FW], BF16)
        mw = SB("f0_m", [128, NBW], F32)
        s2w = SB("f0_s2", [128, NBW], F32)
        rrw = SB("f0_rr", [128, NBW], F32)

        jobs = []
        for src_, dst, nrow in ((self.i_gw, self.d_gw, II),
                                (self.i_uw, self.d_uw, II),
                                (self.i_dw, self.d_dw, H)):
            ncol = H if nrow == II else II
            for r0 in range(0, nrow, 128):
                for c0 in range(0, ncol, FW):
                    jobs.append((src_, dst, r0, c0))

        st_g = {}
        for k, (src_, dst, r0, c0) in enumerate(jobs):
            buf = wtile[k % 2]
            w = [("g", st_g[k - 2])] if k - 2 in st_g else []
            sch.dma("gpsimd", lambda e, buf=buf, src_=src_, r0=r0, c0=c0:
                    e.dma_start(out=buf[:],
                                in_=src_.ap()[r0:r0 + 128, c0:c0 + FW]),
                    waits=w, inc="dG")
            ldn = sch.now("dG")
            b3 = buf[:].rearrange("p (b k) -> p b k", k=32)
            sch.emit("vector", lambda e, b3=b3: e.tensor_reduce(
                out=mw[:], in_=b3, axis=AX.X, op=ALU.max,
                apply_absolute_value=True),
                waits=[("dG", ldn), ("g", sch.now("g"))])
            sch.emit("vector", lambda e: e.tensor_scalar(
                out=s2w[:], in0=mw[:], scalar1=C224, scalar2=2e-30,
                op0=ALU.mult, op1=ALU.max))
            sch.emit("vector", lambda e: e.reciprocal(out=rrw[:], in_=s2w[:]),
                     inc="v")
            vn = sch.now("v")
            rrb = rrw[:].unsqueeze(2).broadcast_to([128, NBW, 32])
            s2b = s2w[:].unsqueeze(2).broadcast_to([128, NBW, 32])
            sch.emit("gpsimd", lambda e, b3=b3, rrb=rrb: e.tensor_tensor(
                out=w8[:], in0=b3, in1=rrb, op=ALU.mult), waits=[("v", vn)])
            sch.emit("gpsimd", lambda e, s2b=s2b: e.tensor_tensor(
                out=wqd[:], in0=w8[:], in1=s2b, op=ALU.mult), inc="g")
            st_g[k] = sch.now("g")
            sch.dma("gpsimd", lambda e, dst=dst, r0=r0, c0=c0: e.dma_start(
                out=dst.ap()[r0:r0 + 128, c0:c0 + FW], in_=wqd[:]), inc="dG")
        self.F0_DG = sch.now("dG")
        self.F0_G = sch.now("g")

    def phase0_logits(self):
        sch, W = self.sch, self.W
        lcol = self.lcol

        for sup in range(NSUP):
            tprev = sch.now("t")
            for hc in range(NHC):
                out = self.xfm[:, hc, :]
                in_ = self.d_xbf.ap()[sup * TSUP:(sup + 1) * TSUP,
                                      hc * 128:(hc + 1) * 128]
                w = [("dG", self.XBF_DG)]
                if sup > 0 and hc == 0:
                    w.append(("t", tprev))
                sch.dma("sync", lambda e, o=out, i=in_: e.dma_start_transpose(
                    out=o, in_=i), waits=w)
            dsn = sch.now("dS")

            def fn_t(e, sup=sup):
                last = None
                for t4 in range(4):
                    tt = sup * 4 + t4
                    for hc in range(NHC):
                        last = e.matmul(
                            self.pb7[:, tt:tt + 1],
                            self.xfm[:, hc, t4 * 128:(t4 + 1) * 128],
                            self.wc_bf[:, hc:hc + 1], start=(hc == 0),
                            stop=(hc == NHC - 1))
                return last
            sch.emit("tensor", fn_t,
                     waits=[("dS", dsn),
                            ("v", self.V_SMALL if sup == 0 else sch.now("v"))],
                     inc="t")
        tn = sch.now("t")
        sch.emit("tensor", lambda e: e.matmul(
            self.pb5[:, 0:1], self.ones_row[0:1, 0:128], self.bc_bf[:],
            start=True, stop=True), waits=[("v", self.V_SMALL)], inc="t")
        tbc = sch.now("t")
        sch.emit("vector", lambda e: e.tensor_copy(out=self.bc128[:],
                                                   in_=self.pb5[:, 0:1]),
                 waits=[("t", tbc)])
        sch.emit("vector", lambda e: e.tensor_tensor(
            out=lcol[:], in0=self.pb7[:, 0:NT],
            in1=self.bc128[:].broadcast_to([128, NT]), op=ALU.add),
            waits=[("t", tn)], inc="v")
        vl = sch.now("v")
        if self.mkdbg("lcol", [128, NT]):
            sch.dma("sync", lambda e: e.dma_start(out=self.dbg["lcol"].ap(),
                                                  in_=lcol[:]),
                    waits=[("v", vl)])
        if "xfm" in self.debug:
            self.dbg["xfm"] = self.nc.dram_tensor("dbg_xfm", [128, NHC * TSUP],
                                                  BF16, kind="ExternalOutput")
            dsx = sch.now("dS")
            sch.dma("sync", lambda e: e.dma_start(
                out=self.dbg["xfm"].ap(),
                in_=self.xfm[:].rearrange("p c t -> p (c t)")),
                waits=[("dS", dsx)])
        sch.emit("tensor", lambda e: e.transpose(
            self.pb5[0:16, 0:128], lcol[:], self.ident_f[:]),
            waits=[("v", vl)], inc="t")
        t1 = sch.now("t")
        sch.emit("vector", lambda e: e.tensor_copy(out=self.lT[:],
                                                   in_=self.pb5[0:16, 0:128]),
                 waits=[("t", t1)], inc="v")
        l3 = self.lT[:].rearrange("p (b r) -> p b r", r=R)
        sch.emit("vector", lambda e, l3=l3: e.tensor_reduce(
            out=self.lm8[:], in_=l3, axis=AX.X, op=ALU.max))
        sch.emit("vector", lambda e, l3=l3: e.tensor_tensor(
            out=l3, in0=l3,
            in1=self.lm8[:].unsqueeze(2).broadcast_to([16, 8, R]),
            op=ALU.subtract), inc="v")
        sch.emit("scalar", lambda e: e.activation(
            out=self.lT[:], in_=self.lT[:], func=ACT.Exp), waits=W("v"),
            inc="s")
        sch.emit("vector", lambda e, l3=l3: e.tensor_reduce(
            out=self.ls8[:], in_=l3, axis=AX.X, op=ALU.add), waits=W("s"))
        sch.emit("vector", lambda e: e.reciprocal(out=self.ls8[:],
                                                  in_=self.ls8[:]))
        sch.emit("vector", lambda e, l3=l3: e.tensor_tensor(
            out=l3, in0=l3,
            in1=self.ls8[:].unsqueeze(2).broadcast_to([16, 8, R]),
            op=ALU.mult), inc="v")
        vw = sch.now("v")
        sch.emit("tensor", lambda e: e.transpose(
            self.pb5[0:128, 0:16], self.lT[:], self.ident_f[0:16, 0:16]),
            waits=[("v", vw)], inc="t")
        t2 = sch.now("t")
        sch.emit("vector", lambda e: e.tensor_copy(out=self.wcol[:],
                                                   in_=self.pb5[0:128, 0:16]),
                 waits=[("t", t2)], inc="v")
        self.V_WCOL = sch.now("v")
        if self.mkdbg("wrow", [128, NT]):
            sch.dma("sync", lambda e: e.dma_start(out=self.dbg["wrow"].ap(),
                                                  in_=self.wcol[:]),
                    waits=[("v", self.V_WCOL)])

    def phase0b(self, sc):
        sch, W, nc = self.sch, self.W, self.nc
        SB = lambda n, sh, dt: sc.enter_context(nc.sbuf_tensor(n, sh, dt))
        gmat = SB("gmat", [128, NT * 128], BF16)
        sch.dma("sync", lambda e: e.dma_start(out=gmat[:],
                                              in_=self.ch["c_g"].ap()))
        xw_bf = SB("xw_bf", [128, H], BF16)
        entries_b = SB("entries_b", [128, H], BF16)
        entries_fm = SB("entries_fm", [128, NHC, 128], BF16)
        wkb = SB("wkb", [128, NHC, HD], BF16)
        wvb = SB("wvb", [128, NHC, HD], BF16)
        ck_sb = SB("ck_sb", [128, C], F32)
        sq_ck = SB("sq_ck", [128, C], BF16)
        rs8 = SB("rs8", [8, 128], F32)
        rs8b = SB("rs8b", [8, 128], BF16)
        ckb2 = SB("ckb2", [128, C], BF16)
        cksw = SB("cksw", [64, C], BF16)
        rp1 = SB("rp1", [64, 2 * C], BF16)

        WKVm = self.WKVMIR_DG
        sch.dma("sync", lambda e: e.dma_start(
            out=wkb[:], in_=self.d_wkb.ap().rearrange("(c p) d -> p c d", p=128)),
            waits=[("dG", WKVm)])
        sch.dma("sync", lambda e: e.dma_start(
            out=wvb[:], in_=self.d_wvb.ap().rearrange("(c p) d -> p c d", p=128)),
            waits=[("dG", WKVm)])
        WKV = sch.now("dS")
        if "wkb" in self.debug:
            self.dbg["wkb"] = self.nc.dram_tensor("dbg_wkb", [128, NHC * HD],
                                                  BF16, kind="ExternalOutput")
            sch.dma("sync", lambda e: e.dma_start(
                out=self.dbg["wkb"].ap(),
                in_=wkb[:].rearrange("p c d -> p (c d)")), waits=[("dS", WKV)])

        for tt in range(NT):
            xt = self.xt[tt % 2]
            wv = [("v", sch.now("v"))] if tt >= 2 else []
            sch.dma("sync", lambda e, xt=xt, tt=tt: e.dma_start(
                out=xt[:], in_=self.i_x.ap()[tt * 128:(tt + 1) * 128, :]),
                waits=wv)
            dsn = sch.now("dS")
            sch.emit("vector", lambda e, xt=xt, tt=tt: e.tensor_scalar(
                out=xw_bf[:], in0=xt[:], scalar1=self.wcol[:, tt:tt + 1],
                scalar2=None, op0=ALU.mult),
                waits=[("dS", dsn), ("v", self.V_WCOL), ("t", sch.now("t"))],
                inc="v")
            vn = sch.now("v")

            def fn(e, tt=tt):
                last = None
                for j in range(4):
                    last = e.matmul(self.ps_big[:, j, :],
                                    gmat[:, tt * 128:(tt + 1) * 128],
                                    xw_bf[:, j * 512:(j + 1) * 512],
                                    start=(tt == 0), stop=(tt == NT - 1))
                return last
            sch.emit("tensor", fn, waits=[("v", vn)], inc="t")
        t_ent = sch.now("t")
        sch.emit("vector", lambda e: e.tensor_copy(out=entries_b[:],
                                                   in_=self.ps_big[:, :, :]),
                 waits=[("t", t_ent)], inc="v")
        ven = sch.now("v")
        if self.mkdbg("entries", [C, H]):
            ent_f = SB("ent_f", [128, H], F32)
            sch.emit("vector", lambda e: e.tensor_copy(out=ent_f[:],
                                                       in_=self.ps_big[:, :, :]),
                     waits=[("t", t_ent)], inc="v")
            sch.dma("sync", lambda e: e.dma_start(out=self.dbg["entries"].ap(),
                                                  in_=ent_f[:]), waits=W("v"))
        for hc in range(NHC):
            w = [("v", ven)]
            if hc >= 2:
                w = [("v", sch.now("v"))]
            sch.emit("tensor", lambda e, hc=hc: e.transpose(
                self.pt0[:, (hc % 2) * 512:(hc % 2) * 512 + 128],
                entries_b[:, hc * 128:(hc + 1) * 128], self.ident[:]),
                waits=w, inc="t")
            tn = sch.now("t")
            sch.emit("vector", lambda e, hc=hc: e.tensor_copy(
                out=entries_fm[:, hc, :],
                in_=self.pt0[:, (hc % 2) * 512:(hc % 2) * 512 + 128]),
                waits=[("t", tn)], inc="v")
        vfm = sch.now("v")
        if "efm" in self.debug:
            self.dbg["efm"] = self.nc.dram_tensor("dbg_efm", [128, NHC * 128],
                                                  BF16, kind="ExternalOutput")
            sch.dma("sync", lambda e: e.dma_start(
                out=self.dbg["efm"].ap(),
                in_=entries_fm[:].rearrange("p c t -> p (c t)")),
                waits=[("v", vfm)])

        def fn_ck(e):
            last = None
            for hc in range(NHC):
                last = e.matmul(self.pb4[:, 0:128], wkb[:, hc, :],
                                entries_fm[:, hc, :], start=(hc == 0),
                                stop=(hc == NHC - 1))
            return last
        sch.emit("tensor", fn_ck, waits=[("v", vfm), ("dS", WKV)], inc="t")
        tck = sch.now("t")

        def fn_cv(e):
            for hc in range(NHC):
                e.matmul(self.pb5[:, 0:128], entries_fm[:, hc, :], wvb[:, hc, :],
                         start=(hc == 0), stop=False)
            return e.matmul(self.pb5[:, 0:128], self.ones_row[0:1, 0:128],
                            self.rows_b[0:1, 2 * H:2 * H + HD], start=False,
                            stop=True)
        sch.emit("tensor", fn_cv, waits=[("v", vfm), ("dS", WKV)], inc="t")
        tcv = sch.now("t")
        sch.emit("vector", lambda e: e.tensor_copy(out=self.cvb[:],
                                                   in_=self.pb5[:, 0:128]),
                 waits=[("t", tcv)], inc="v")
        if self.mkdbg("cv", [C, HD]):
            cv_f = SB("cv_f", [128, HD], F32)
            sch.emit("vector", lambda e: e.tensor_copy(out=cv_f[:],
                                                       in_=self.pb5[:, 0:128]),
                     inc="v")
            sch.dma("sync", lambda e: e.dma_start(out=self.dbg["cv"].ap(),
                                                  in_=cv_f[:]), waits=W("v"))

        sch.emit("vector", lambda e: e.tensor_scalar(
            out=ck_sb[:], in0=self.pb4[:, 0:128], scalar1=self.bk_c[:],
            scalar2=None, op0=ALU.add), waits=[("t", tck)], inc="v")
        if self.mkdbg("cklin", [HD, C]):
            sch.dma("sync", lambda e: e.dma_start(out=self.dbg["cklin"].ap(),
                                                  in_=ck_sb[:]), waits=W("v"))
        sch.emit("scalar", lambda e: e.activation(out=sq_ck[:], in_=ck_sb[:],
                                                  func=ACT.Square),
                 waits=W("v"), inc="s")
        sch.emit("tensor", lambda e: e.matmul(
            self.pb4[0:8, 128:256], self.emat[:, 0:8], sq_ck[:], start=True,
            stop=True), waits=W("s"), inc="t")
        sch.emit("vector", lambda e: e.tensor_scalar(
            out=rs8[0:8, :], in0=self.pb4[0:8, 128:256], scalar1=C128,
            scalar2=EPS_RMS, op0=ALU.mult, op1=ALU.add), waits=W("t"), inc="v")
        sch.emit("scalar", lambda e: e.activation(out=rs8[0:8, :],
                                                  in_=rs8[0:8, :], func=ACT.Sqrt),
                 waits=W("v"), inc="s")
        sch.emit("vector", lambda e: e.reciprocal(out=rs8[0:8, :],
                                                  in_=rs8[0:8, :]), waits=W("s"))
        sch.emit("vector", lambda e: e.tensor_copy(out=rs8b[0:8, :],
                                                   in_=rs8[0:8, :]), inc="v")
        sch.emit("tensor", lambda e: e.matmul(
            self.pb5[:, 128:256], self.fmat[0:8, 0:128], rs8b[0:8, :],
            start=True, stop=True), waits=W("v"), inc="t")
        sch.emit("vector", lambda e: e.scalar_tensor_tensor(
            out=ck_sb[:], in0=ck_sb[:], scalar=self.knw_c[:],
            in1=self.pb5[:, 128:256], op0=ALU.mult, op1=ALU.mult),
            waits=W("t"), inc="v")
        sch.emit("vector", lambda e: e.tensor_copy(out=ckb2[:], in_=ck_sb[:]),
                 inc="v")
        sch.emit("tensor", lambda e: e.matmul(
            self.pb7[0:64, 0:128], self.swapm[:], ckb2[0:64, :], start=True,
            stop=True), waits=W("v"), inc="t")
        sch.emit("vector", lambda e: e.tensor_copy(
            out=cksw[0:64, :], in_=self.pb7[0:64, 0:128]), waits=W("t"))
        sch.emit("vector", lambda e: e.tensor_tensor(
            out=rp1[0:64, 0:C], in0=cksw[0:64, :], in1=self.sinsc[:],
            op=ALU.mult))
        sch.emit("vector", lambda e: e.tensor_tensor(
            out=rp1[0:64, C:2 * C], in0=ckb2[0:64, :], in1=self.cosdc[:],
            op=ALU.mult))
        sch.emit("vector", lambda e: e.tensor_tensor(
            out=ckb2[0:64, :], in0=rp1[0:64, C:2 * C], in1=rp1[0:64, 0:C],
            op=ALU.add), inc="v")
        vck = sch.now("v")
        if self.mkdbg("ck", [HD, C]):
            ck_f = SB("ck_f", [128, C], F32)
            sch.emit("vector", lambda e: e.tensor_copy(out=ck_f[:], in_=ckb2[:]),
                     waits=[("v", vck)], inc="v")
            sch.dma("sync", lambda e: e.dma_start(out=self.dbg["ck"].ap(),
                                                  in_=ck_f[:]), waits=W("v"))

        def fn_scb(e):
            last = None
            for h in range(NH):
                e.tensor_copy(out=self.scb[:, h, 0:1],
                              in_=self.sinkb[:, h:h + 1])
                last = e.tensor_copy(out=self.scb[:, h, 1:129], in_=ckb2[:])
            return last
        sch.emit("vector", fn_scb, waits=[("v", vck)], inc="v")

    def phase_groups(self, sc):
        sch, W, nc = self.sch, self.W, self.nc
        SB = lambda n, sh, dt: sc.enter_context(nc.sbuf_tensor(n, sh, dt))
        wq_g = SB("wq_g", [128, NHC, 512], BF16)
        wlk_g = SB("wlk_g", [128, NHC, 512], BF16)
        wlv_g = SB("wlv_g", [128, NHC, 512], BF16)
        qn_sb = SB("qn_sb", [128, HPG, TSUP], F32)
        kn_sb = SB("kn_sb", [128, HPG, TSUP], F32)
        sq_bf = SB("sq_bf", [128, 2 * HPG, TSUP], BF16)
        q_sup = SB("q_sup", [128, HPG, TSUP], BF16)
        lk_sup = [SB("lk_supA", [128, HPG, TSUP], BF16),
                  SB("lk_supB", [128, HPG, TSUP], BF16)]
        lv_sup = [SB("lv_supA", [128, 4, HPG * 128], BF16),
                  SB("lv_supB", [128, 4, HPG * 128], BF16)]
        rs8 = SB("rs8g", [8, TSUP], F32)
        rs8b = SB("rs8gb", [8, TSUP], BF16)
        rope_sc = SB("rope_sc", [64, 4, TSUP], BF16)
        ms_c = SB("ms_c", [128, 2, 129], F32)
        ms_l = SB("ms_l", [128, 2, 256], F32)
        es_c = SB("es_c", [128, 2, 129], F32)
        es_l = SB("es_l", [128, 2, 256], F32)
        nm_c = SB("nm_c", [128, 4], F32)
        nm_l = SB("nm_l", [128, 4], F32)
        dn_c = SB("dn_c", [128, 4], F32)
        dn_l = SB("dn_l", [128, 4], F32)
        rc_c = SB("rc_c", [128, 4], F32)
        rc_l = SB("rc_l", [128, 4], F32)
        p_bf = SB("p_bf", [128, 2, 385], BF16)
        pT_sb = SB("pT_sb", [128, 2, 512], BF16)
        ctx_st = SB("ctx_st", [128, 2, HPG, 128], BF16)

        self.ctx_ds = []

        for g in range(NG):
            gsl = slice(g * 512, (g + 1) * 512)
            wload = [("dG", self.WQKV_DG), ("t", sch.now("t")),
                     ("v", sch.now("v"))]
            sch.dma("sync", lambda e, gsl=gsl: e.dma_start(
                out=wq_g[:], in_=self.d_wqb.ap()[:, gsl].rearrange(
                    "(c p) m -> p c m", p=128)), waits=wload)
            sch.dma("sync", lambda e, gsl=gsl: e.dma_start(
                out=wlk_g[:], in_=self.d_wlkb.ap()[:, gsl].rearrange(
                    "(c p) m -> p c m", p=128)))
            sch.dma("sync", lambda e, gsl=gsl: e.dma_start(
                out=wlv_g[:], in_=self.d_wlvb.ap()[:, gsl].rearrange(
                    "(c p) m -> p c m", p=128)))
            WG = sch.now("dS")
            sch.emit("vector", lambda e: e.memset(
                lk_sup[1][:, :, TSUP - 128:TSUP], 0.0),
                waits=[("t", sch.now("t"))])
            sch.emit("vector", lambda e: e.memset(lv_sup[1][:, 3, :], 0.0),
                     inc="v")

            for sup in range(NSUP):
                par = sup % 2
                tprev = sch.now("t")
                for hc in range(NHC):
                    out = self.xfm[:, hc, :]
                    in_ = self.d_xbf.ap()[sup * TSUP:(sup + 1) * TSUP,
                                          hc * 128:(hc + 1) * 128]
                    sch.dma("sync", lambda e, o=out, i=in_:
                            e.dma_start_transpose(out=o, in_=i),
                            waits=[("t", tprev)] if hc == 0 else ())
                XFM = sch.now("dS")

                vprev = sch.now("v")

                def fn_q(e):
                    last = None
                    for h in range(HPG):
                        for hc in range(NHC):
                            last = e.matmul(
                                self.ps_big[:, h, :],
                                wq_g[:, hc, h * 128:(h + 1) * 128],
                                self.xfm[:, hc, :], start=(hc == 0),
                                stop=(hc == NHC - 1))
                    return last
                sch.emit("tensor", fn_q,
                         waits=[("dS", max(XFM, WG)), ("v", vprev)], inc="t")
                tq = sch.now("t")
                for h in range(HPG):
                    hg = g * HPG + h
                    sch.emit("vector", lambda e, h=h, hg=hg: e.tensor_scalar(
                        out=qn_sb[:, h, :], in0=self.ps_big[:, h, :],
                        scalar1=self.bq_c[:, hg:hg + 1], scalar2=None,
                        op0=ALU.add), waits=[("t", tq)], inc="v")
                    sch.emit("scalar", lambda e, h=h: e.activation(
                        out=sq_bf[:, h, :], in_=qn_sb[:, h, :],
                        func=ACT.Square), waits=W("v"), inc="s")
                vq_evac = sch.now("v")

                def fn_lk(e):
                    last = None
                    for h in range(HPG):
                        for hc in range(NHC):
                            last = e.matmul(
                                self.ps_big[:, h, :],
                                wlk_g[:, hc, h * 128:(h + 1) * 128],
                                self.xfm[:, hc, :], start=(hc == 0),
                                stop=(hc == NHC - 1))
                    return last
                sch.emit("tensor", fn_lk, waits=[("v", vq_evac)], inc="t")
                tlk = sch.now("t")
                for h in range(HPG):
                    hg = g * HPG + h
                    sch.emit("vector", lambda e, h=h, hg=hg: e.tensor_scalar(
                        out=kn_sb[:, h, :], in0=self.ps_big[:, h, :],
                        scalar1=self.blk_c[:, hg:hg + 1], scalar2=None,
                        op0=ALU.add), waits=[("t", tlk)], inc="v")
                    sch.emit("scalar", lambda e, h=h: e.activation(
                        out=sq_bf[:, HPG + h, :], in_=kn_sb[:, h, :],
                        func=ACT.Square), waits=W("v"), inc="s")
                vlk_evac = sch.now("v")
                sq_all = sch.now("s")

                def fn_e(e):
                    last = None
                    for r in range(8):
                        last = e.matmul(self.pb4[0:8, 0:TSUP],
                                        self.emat[:, r * 8:(r + 1) * 8],
                                        sq_bf[:, r, :], start=(r == 0),
                                        stop=(r == 7))
                    return last
                sch.emit("tensor", fn_e,
                         waits=[("s", sq_all), ("v", sch.now("v"))], inc="t")
                te = sch.now("t")

                def fn_lv(e, g=g):
                    last = None
                    for t4 in range(4):
                        for hc in range(NHC):
                            e.matmul(self.ps_big[:, t4, :],
                                     self.xfm[:, hc, t4 * 128:(t4 + 1) * 128],
                                     wlv_g[:, hc, :], start=(hc == 0),
                                     stop=False)
                        last = e.matmul(self.ps_big[:, t4, :],
                                        self.ones_row[0:1, 0:128],
                                        self.rows_b[0:1, g * 512:(g + 1) * 512],
                                        start=False, stop=True)
                    return last
                sch.emit("tensor", fn_lv, waits=[("v", vlk_evac)], inc="t")
                tlv = sch.now("t")

                sch.emit("vector", lambda e: e.tensor_scalar(
                    out=rs8[0:8, :], in0=self.pb4[0:8, 0:TSUP], scalar1=C128,
                    scalar2=EPS_RMS, op0=ALU.mult, op1=ALU.add),
                    waits=[("t", te)], inc="v")
                sch.emit("scalar", lambda e: e.activation(
                    out=rs8[0:8, :], in_=rs8[0:8, :], func=ACT.Sqrt),
                    waits=W("v"), inc="s")
                sch.emit("vector", lambda e: e.reciprocal(
                    out=rs8[0:8, :], in_=rs8[0:8, :]), waits=W("s"))
                sch.emit("vector", lambda e: e.tensor_copy(
                    out=rs8b[0:8, :], in_=rs8[0:8, :]), inc="v")
                vrs = sch.now("v")

                for t4 in range(4):
                    sch.emit("vector", lambda e, t4=t4, par=par: e.tensor_copy(
                        out=lv_sup[par][:, t4, :], in_=self.ps_big[:, t4, :]),
                        waits=[("t", tlv)], inc="v")
                vlv = sch.now("v")

                cs_sl = self.cosd[:, sup * TSUP:(sup + 1) * TSUP]
                sn_sl = self.sins[:, sup * TSUP:(sup + 1) * TSUP]
                for r in range(8):
                    bank = [self.pb5, self.pb7][r % 2]
                    is_q = r < HPG
                    h = r % HPG
                    dst = q_sup if is_q else lk_sup[par]
                    src = qn_sb if is_q else kn_sb
                    wcol_ = self.qnw_c if is_q else self.knw_c
                    sch.emit("tensor", lambda e, r=r, bank=bank: e.matmul(
                        bank[:, 0:TSUP], self.fmat[0:8, r * 128:(r + 1) * 128],
                        rs8b[0:8, :], start=True, stop=True),
                        waits=[("v", sch.now("v"))], inc="t")
                    tb = sch.now("t")
                    sch.emit("vector", lambda e, h=h, dst=dst, src=src,
                             wcol_=wcol_, bank=bank: e.scalar_tensor_tensor(
                        out=dst[:, h, :], in0=src[:, h, :], scalar=wcol_[:],
                        in1=bank[:, 0:TSUP], op0=ALU.mult, op1=ALU.mult),
                        waits=[("t", tb)], inc="v")
                    vstt = sch.now("v")
                    obank = [self.pb5, self.pb7][(r + 1) % 2]
                    sch.emit("tensor", lambda e, h=h, dst=dst, obank=obank:
                             e.matmul(obank[0:64, 0:TSUP], self.swapm[:],
                                      dst[0:64, h, :], start=True, stop=True),
                             waits=[("v", vstt)], inc="t")
                    tsw = sch.now("t")
                    qsh = rope_sc[0:64, 2, :]
                    sa = rope_sc[0:64, 0, :]
                    sb_ = rope_sc[0:64, 1, :]
                    sch.emit("vector", lambda e, qsh=qsh, obank=obank:
                             e.tensor_copy(out=qsh, in_=obank[0:64, 0:TSUP]),
                             waits=[("t", tsw)], inc="v")
                    sch.emit("vector", lambda e, qsh=qsh, sa=sa, sn_sl=sn_sl:
                             e.tensor_tensor(out=sa, in0=qsh, in1=sn_sl,
                                             op=ALU.mult))
                    sch.emit("vector", lambda e, h=h, dst=dst, sb_=sb_,
                             cs_sl=cs_sl: e.tensor_tensor(
                        out=sb_, in0=dst[0:64, h, :], in1=cs_sl, op=ALU.mult))
                    sch.emit("vector", lambda e, h=h, dst=dst, sa=sa, sb_=sb_:
                             e.tensor_tensor(out=dst[0:64, h, :], in0=sb_,
                                             in1=sa, op=ALU.add), inc="v")
                vrope = sch.now("v")

                if self.mkdbg("q", [H, S]) and g == 0:
                    pass

                units = [(t4, h) for t4 in range(4) for h in range(HPG)]
                NU = len(units)
                st_mask_v = {}
                st_ts_v = {}
                st_ptev_v = {}
                st_ctx_v = {}
                st_qk_t = {}
                st_tr_t = {}
                st_pv_t = {}
                st_exp_s = {}
                ctx_dma_ds = {}

                def em_qk(u):
                    t4, h = units[u]
                    hg = g * HPG + h
                    bank = [self.pb4, self.pb5][u % 2]
                    tt = sup * 4 + t4
                    if t4 > 0:
                        lkp = lk_sup[par][:, h, (t4 - 1) * 128:t4 * 128]
                    else:
                        lkp = lk_sup[1 - par][:, h, 3 * 128:4 * 128]
                    lkc = lk_sup[par][:, h, t4 * 128:(t4 + 1) * 128]
                    qs = q_sup[:, h, t4 * 128:(t4 + 1) * 128]
                    w = [("v", st_mask_v[u - 2] if u - 2 >= 0 else vrope)]
                    if u == 0:
                        w.append(("v", vlv))

                    def fn(e, qs=qs, bank=bank, hg=hg, lkp=lkp, lkc=lkc):
                        e.matmul(bank[:, 0:129], qs, self.scb[:, hg, :],
                                 start=True, stop=True)
                        e.matmul(bank[:, 129:257], qs, lkp, start=True, stop=True)
                        return e.matmul(bank[:, 257:385], qs, lkc, start=True,
                                        stop=True)
                    sch.emit("tensor", fn, waits=w, inc="t")
                    st_qk_t[u] = sch.now("t")

                def em_mask(u):
                    t4, h = units[u]
                    tt = sup * 4 + t4
                    sl = u % 2
                    bank = [self.pb4, self.pb5][u % 2]
                    lm = self.lmask[:, 0:256] if tt == 0 else self.lmask[:, 256:512]
                    cm = self.cmask[:, tt * 129:(tt + 1) * 129]
                    wm = [("t", st_qk_t[u])]
                    if u - 2 in st_exp_s:
                        wm.append(("s", st_exp_s[u - 2]))
                    sch.emit("vector", lambda e, bank=bank, sl=sl, cm=cm:
                             e.scalar_tensor_tensor(
                        out=ms_c[:, sl, :], in0=bank[:, 0:129], scalar=SCALE,
                        in1=cm, op0=ALU.mult, op1=ALU.add),
                        waits=wm)
                    sch.emit("vector", lambda e, sl=sl: e.tensor_reduce(
                        out=nm_c[:, sl:sl + 1], in_=ms_c[:, sl, :], axis=AX.X,
                        op=ALU.max, negate=True))
                    sch.emit("vector", lambda e, bank=bank, sl=sl, lm=lm:
                             e.scalar_tensor_tensor(
                        out=ms_l[:, sl, :], in0=bank[:, 129:385], scalar=SCALE,
                        in1=lm, op0=ALU.mult, op1=ALU.add))
                    sch.emit("vector", lambda e, sl=sl: e.tensor_reduce(
                        out=nm_l[:, sl:sl + 1], in_=ms_l[:, sl, :], axis=AX.X,
                        op=ALU.max, negate=True), inc="v")
                    st_mask_v[u] = sch.now("v")

                def em_exp(u):
                    sl = u % 2

                    def fn(e, sl=sl):
                        e.activation(out=es_c[:, sl, :], in_=ms_c[:, sl, :],
                                     func=ACT.Exp, bias=nm_c[:, sl:sl + 1],
                                     scale=1.0, accum_out=dn_c[:, sl:sl + 1])
                        return e.activation(out=es_l[:, sl, :], in_=ms_l[:, sl, :],
                                            func=ACT.Exp, bias=nm_l[:, sl:sl + 1],
                                            scale=1.0,
                                            accum_out=dn_l[:, sl:sl + 1])
                    sch.emit("scalar", fn, waits=[("v", st_mask_v[u])], inc="s")
                    st_exp_s[u] = sch.now("s")

                def em_recips(u):
                    sl = u % 2
                    sch.emit("vector", lambda e, sl=sl: e.reciprocal(
                        out=rc_c[:, sl:sl + 1], in_=dn_c[:, sl:sl + 1]),
                        waits=[("s", st_exp_s[u])])
                    sch.emit("vector", lambda e, sl=sl: e.reciprocal(
                        out=rc_l[:, sl:sl + 1], in_=dn_l[:, sl:sl + 1]))

                def em_ts(u):
                    sl = u % 2
                    sch.emit("vector", lambda e, sl=sl: e.tensor_scalar(
                        out=p_bf[:, sl, 0:129], in0=es_c[:, sl, :],
                        scalar1=rc_c[:, sl:sl + 1], scalar2=0.5, op0=ALU.mult,
                        op1=ALU.mult))
                    sch.emit("vector", lambda e, sl=sl: e.tensor_scalar(
                        out=p_bf[:, sl, 129:385], in0=es_l[:, sl, :],
                        scalar1=rc_l[:, sl:sl + 1], scalar2=0.5, op0=ALU.mult,
                        op1=ALU.mult), inc="v")
                    st_ts_v[u] = sch.now("v")

                def em_tr(u):
                    sl = u % 2
                    w = [("v", st_ts_v[u])]
                    if u - 1 >= 0:
                        w.append(("v", st_ptev_v[u - 1]))

                    def fn(e, sl=sl):
                        e.transpose(self.pt0[:, 0:128], p_bf[:, sl, 1:129],
                                    self.ident[:])
                        e.transpose(self.pt0[:, 128:256], p_bf[:, sl, 129:257],
                                    self.ident[:])
                        e.transpose(self.pt0[:, 256:384], p_bf[:, sl, 257:385],
                                    self.ident[:])
                        return e.transpose(self.pt0[0:1, 384:512],
                                           p_bf[:, sl, 0:1], self.ident[:])
                    sch.emit("tensor", fn, waits=w, inc="t")
                    st_tr_t[u] = sch.now("t")

                def em_ptev(u):
                    sl = u % 2

                    def fn(e, sl=sl):
                        e.tensor_copy(out=pT_sb[:, sl, 0:384],
                                      in_=self.pt0[:, 0:384])
                        return e.tensor_copy(out=pT_sb[0:1, sl, 384:512],
                                             in_=self.pt0[0:1, 384:512])
                    sch.emit("vector", fn, waits=[("t", st_tr_t[u])], inc="v")
                    st_ptev_v[u] = sch.now("v")

                def em_pv(u):
                    t4, h = units[u]
                    hg = g * HPG + h
                    sl = u % 2
                    if t4 > 0:
                        lvp = lv_sup[par][:, t4 - 1, h * 128:(h + 1) * 128]
                    else:
                        lvp = lv_sup[1 - par][:, 3, h * 128:(h + 1) * 128]
                    lvc = lv_sup[par][:, t4, h * 128:(h + 1) * 128]
                    w = [("v", st_ptev_v[u])]
                    if u - 2 >= 0:
                        w.append(("v", st_ctx_v[u - 2]))

                    def fn(e, sl=sl, hg=hg, lvp=lvp, lvc=lvc):
                        e.matmul(self.ps_big[:, sl, 0:128], self.cvb[:],
                                 pT_sb[:, sl, 0:128], start=True, stop=False)
                        e.matmul(self.ps_big[:, sl, 0:128],
                                 self.rows_b[0:1, 3 * H + hg * 128:3 * H + (hg + 1) * 128],
                                 pT_sb[0:1, sl, 384:512], start=False, stop=False)
                        e.matmul(self.ps_big[:, sl, 0:128], lvp,
                                 pT_sb[:, sl, 128:256], start=False, stop=False)
                        return e.matmul(self.ps_big[:, sl, 0:128], lvc,
                                        pT_sb[:, sl, 256:384], start=False,
                                        stop=True)
                    sch.emit("tensor", fn, waits=w, inc="t")
                    st_pv_t[u] = sch.now("t")

                def em_ctx(u):
                    t4, h = units[u]
                    sl = u % 2
                    w = [("t", st_pv_t[u])]
                    if u - 2 * HPG >= 0:
                        w.append(("dS", ctx_dma_ds[u - 2 * HPG]))

                    sch.emit("vector", lambda e, t4=t4, h=h, sl=sl: e.tensor_copy(
                        out=ctx_st[:, t4 % 2, h, :], in_=self.ps_big[:, sl, 0:128]),
                        waits=w, inc="v")
                    st_ctx_v[u] = sch.now("v")
                    if h == HPG - 1:
                        tt = sup * 4 + t4
                        out = self.d_ctx.ap()[g * 512:(g + 1) * 512,
                                              tt * 128:(tt + 1) * 128].rearrange(
                                                  "(h p) t -> p h t", p=128)
                        sch.dma("sync", lambda e, out=out, t4=t4: e.dma_start(
                            out=out, in_=ctx_st[:, t4 % 2, :, :]),
                            waits=[("v", st_ctx_v[u])])
                        for hh in range(HPG):
                            ctx_dma_ds[u - hh] = sch.now("dS")
                        self.ctx_ds.append(sch.now("dS"))

                for u in range(NU + 1):
                    if u < NU:
                        em_qk(u)
                        em_mask(u)
                        em_exp(u)
                        em_recips(u)
                    if 0 <= u - 1:
                        em_ts(u - 1)
                        em_tr(u - 1)
                        em_ptev(u - 1)
                        em_pv(u - 1)
                        em_ctx(u - 1)

    def phase_a4(self, sc):
        sch, W, nc = self.sch, self.W, self.nc
        SB = lambda n, sh, dt: sc.enter_context(nc.sbuf_tensor(n, sh, dt))
        wo_sb = SB("wo_sb", [128, NHC, H], BF16)
        ctx_t = SB("ctx_t", [128, 2, NHC, 128], BF16)
        r1_sb = SB("r1_sb", [128, H], F32)
        sq_t = SB("sq_t", [128, H], F32)
        mq = SB("mq", [128, 64], F32)
        s2q = SB("s2q", [128, 64], F32)
        rrq = SB("rrq", [128, 64], F32)
        xq8 = SB("xq8", [128, H], FP8)
        xq_bf = SB("xq_bf", [128, H], BF16)

        sch.dma("sync", lambda e: e.dma_start(
            out=wo_sb[:], in_=self.d_wob.ap().rearrange("(c p) m -> p c m", p=128)),
            waits=[("dG", self.WO_DG), ("t", sch.now("t")), ("v", sch.now("v"))])
        WO = sch.now("dS")
        ctx_all = self.ctx_ds[-1] if self.ctx_ds else 0

        dbg_x1 = self.mkdbg("x1", [S, H])

        for tt in range(NT):
            xt = self.xt[tt % 2]
            par = tt % 2
            inctx = self.d_ctx.ap()[:, tt * 128:(tt + 1) * 128].rearrange(
                "(c p) t -> p c t", p=128)
            wv = [("t", sch.now("t"))] if tt >= 2 else [("dS", ctx_all)]
            sch.dma("sync", lambda e, inctx=inctx, par=par: e.dma_start(
                out=ctx_t[:, par, :, :], in_=inctx), waits=wv)
            sch.dma("sync", lambda e, xt=xt, tt=tt: e.dma_start(
                out=xt[:], in_=self.i_x.ap()[tt * 128:(tt + 1) * 128, :]),
                waits=[("v", sch.now("v"))] if tt >= 2 else ())
            dsn = sch.now("dS")

            def fn(e, par=par):
                for hc in range(NHC):
                    for j in range(4):
                        e.matmul(self.ps_big[:, j, :], ctx_t[:, par, hc, :],
                                 wo_sb[:, hc, j * 512:(j + 1) * 512],
                                 start=(hc == 0), stop=False)
                last = None
                for j in range(4):
                    last = e.matmul(self.ps_big[:, j, :],
                                    self.ones_row[0:1, 0:128],
                                    self.rows_b[0:1, H + j * 512:H + (j + 1) * 512],
                                    start=False, stop=True)
                return last
            sch.emit("tensor", fn,
                     waits=[("dS", max(dsn, WO)), ("v", sch.now("v"))],
                     inc="t")
            tn = sch.now("t")

            sch.emit("vector", lambda e, xt=xt: e.tensor_tensor(
                out=r1_sb[:], in0=self.ps_big[:, :, :], in1=xt[:],
                op=ALU.add), waits=[("t", tn)], inc="v")
            sch.emit("vector", lambda e: e.tensor_reduce(
                out=self.cols[:, 0:1], in_=r1_sb[:], axis=AX.X, op=ALU.add),
                inc="v")
            sch.emit("scalar", lambda e: e.activation(
                out=sq_t[:], in_=r1_sb[:], func=ACT.Square,
                accum_out=self.cols[:, 1:2]), waits=W("v"), inc="s")
            sn = sch.now("s")
            sch.emit("vector", lambda e: e.tensor_scalar(
                out=self.cols[:, 2:3], in0=self.cols[:, 0:1], scalar1=CH,
                scalar2=None, op0=ALU.mult), waits=[("s", sn)])
            sch.emit("vector", lambda e: e.tensor_scalar(
                out=self.cols[:, 4:5], in0=self.cols[:, 1:2], scalar1=CH,
                scalar2=None, op0=ALU.mult))
            sch.emit("vector", lambda e: e.tensor_tensor(
                out=self.cols[:, 3:4], in0=self.cols[:, 2:3],
                in1=self.cols[:, 2:3], op=ALU.mult))
            sch.emit("vector", lambda e: e.tensor_tensor(
                out=self.cols[:, 5:6], in0=self.cols[:, 4:5],
                in1=self.cols[:, 3:4], op=ALU.subtract))
            sch.emit("vector", lambda e: e.tensor_scalar(
                out=self.cols[:, 6:7], in0=self.cols[:, 5:6], scalar1=EPS_LN,
                scalar2=None, op0=ALU.add), inc="v")
            sch.emit("scalar", lambda e: e.activation(
                out=self.cols[:, 6:7], in_=self.cols[:, 6:7], func=ACT.Sqrt),
                waits=W("v"), inc="s")
            sn = sch.now("s")
            sch.emit("vector", lambda e: e.reciprocal(
                out=self.cols[:, 7:8], in_=self.cols[:, 6:7]), waits=[("s", sn)])
            self.gap(3)
            sch.emit("vector", lambda e: e.tensor_scalar(
                out=self.x1_t[:], in0=r1_sb[:], scalar1=self.cols[:, 2:3],
                scalar2=self.cols[:, 7:8], op0=ALU.subtract, op1=ALU.mult),
                inc="v")
            x13 = self.x1_t[:].rearrange("p (b k) -> p b k", k=32)
            sch.emit("vector", lambda e, x13=x13: e.tensor_reduce(
                out=mq[:], in_=x13, axis=AX.X, op=ALU.max,
                apply_absolute_value=True))
            sch.emit("vector", lambda e: e.tensor_scalar(
                out=s2q[:], in0=mq[:], scalar1=C224, scalar2=2e-30,
                op0=ALU.mult, op1=ALU.max))
            sch.emit("vector", lambda e: e.reciprocal(out=rrq[:],
                                                      in_=s2q[:]))
            self.gap(3)
            rrb = rrq[:].unsqueeze(2).broadcast_to([128, 64, 32])
            s2b = s2q[:].unsqueeze(2).broadcast_to([128, 64, 32])
            sch.emit("vector", lambda e, x13=x13, rrb=rrb: e.tensor_tensor(
                out=xq8[:], in0=x13, in1=rrb, op=ALU.mult))
            sch.emit("vector", lambda e, s2b=s2b: e.tensor_tensor(
                out=xq_bf[:], in0=xq8[:], in1=s2b, op=ALU.mult),
                inc="v")
            vq = sch.now("v")
            sch.dma("sync", lambda e, tt=tt: e.dma_start(
                out=self.d_x1.ap()[tt * 128:(tt + 1) * 128, :], in_=self.x1_t[:]),
                waits=[("v", vq)])
            sch.dma("sync", lambda e, tt=tt: e.dma_start(
                out=self.d_xq.ap()[tt * 128:(tt + 1) * 128, :], in_=xq_bf[:]),
                waits=[("v", vq)])
            if dbg_x1:
                sch.dma("sync", lambda e, tt=tt: e.dma_start(
                    out=self.dbg["x1"].ap()[tt * 128:(tt + 1) * 128, :],
                    in_=self.x1_t[:]), waits=[("v", vq)])
            self.a4_store_ds.append(sch.now("dS"))
            if "ffn" not in self.phases:
                sch.dma("sync", lambda e, tt=tt: e.dma_start(
                    out=self.o_y.ap()[tt * 128:(tt + 1) * 128, :],
                    in_=self.x1_t[:]), waits=[("v", vq)])

    def phase_f(self, sc):
        sch, W, nc = self.sch, self.W, self.nc
        SB = lambda n, sh, dt: sc.enter_context(nc.sbuf_tensor(n, sh, dt))
        xfm_q = self.xfm
        wbig = SB("wbig", [128, 64, 512], BF16)
        gwT = wbig[:, 0:NHC, :]
        uwT = wbig[:, NHC:2 * NHC, :]
        dwT = wbig
        hqT = SB("hqT", [128, 64, TSUP], BF16)
        sil = SB("sil", [128, 512], F32)
        h_sb = SB("h_sb", [128, 512], F32)
        mh = SB("mh", [128, 16], F32)
        s2h = SB("s2h", [128, 16], F32)
        rrh = SB("rrh", [128, 16], F32)
        h8 = SB("h8", [128, 512], FP8)
        hq = SB("hq", [128, 2, 512], BF16)
        r2c = SB("r2c", [128, 2, 512], F32)
        sqf = SB("sqf", [128, 512], F32)

        for sup in range(NSUP):
            a4ds = self.a4_store_ds[sup * 4 + 3]
            tprev = sch.now("t")
            for hc in range(NHC):
                out = xfm_q[:, hc, :]
                in_ = self.d_xq.ap()[sup * TSUP:(sup + 1) * TSUP,
                                     hc * 128:(hc + 1) * 128]
                w = [("dS", a4ds)]
                if sup > 0 and hc == 0:
                    w.append(("t", tprev))
                sch.dma("sync", lambda e, o=out, i=in_: e.dma_start_transpose(
                    out=o, in_=i), waits=w)
            XQ = sch.now("dS")

            for ic in range(16):
                tload = sch.now("t")
                for hc in range(NHC):
                    w = [("dG", self.F0_DG)]
                    if hc == 0:
                        w.append(("t", tload))
                        if sup == 0 and ic == 0:
                            w.append(("v", sch.now("v")))
                            w.append(("s", sch.now("s")))
                    sch.dma("sync", lambda e, hc=hc, ic=ic: e.dma_start_transpose(
                        out=wbig[:, hc, :],
                        in_=self.d_gw.ap()[ic * 512:(ic + 1) * 512,
                                           hc * 128:(hc + 1) * 128]), waits=w)
                    sch.dma("sync", lambda e, hc=hc, ic=ic: e.dma_start_transpose(
                        out=wbig[:, NHC + hc, :],
                        in_=self.d_uw.ap()[ic * 512:(ic + 1) * 512,
                                           hc * 128:(hc + 1) * 128]))
                GW = sch.now("dS")
                for t4 in range(4):
                    pg = self.ps_big[:, (t4 % 2) * 2, :]
                    pu = self.ps_big[:, (t4 % 2) * 2 + 1, :]
                    w = [("dS", max(GW, XQ)), ("v", sch.now("v")),
                         ("s", sch.now("s"))]

                    def fn(e, t4=t4, pg=pg, pu=pu):
                        last = None
                        for hc in range(NHC):
                            lh = xfm_q[:, hc, t4 * 128:(t4 + 1) * 128]
                            e.matmul(pg, lh, wbig[:, hc, :], start=(hc == 0),
                                     stop=(hc == NHC - 1))
                            last = e.matmul(pu, lh, wbig[:, NHC + hc, :],
                                            start=(hc == 0), stop=(hc == NHC - 1))
                        return last
                    sch.emit("tensor", fn, waits=w, inc="t")
                    tgu = sch.now("t")
                    sch.emit("scalar", lambda e, pg=pg: e.activation(
                        out=sil[:], in_=pg, func=ACT.Silu),
                        waits=[("t", tgu), ("v", sch.now("v"))], inc="s")
                    sn = sch.now("s")
                    sch.emit("vector", lambda e, pu=pu: e.tensor_tensor(
                        out=h_sb[:], in0=sil[:], in1=pu, op=ALU.mult),
                        waits=[("s", sn)], inc="v")
                    h3 = h_sb[:].rearrange("p (b k) -> p b k", k=32)
                    sch.emit("vector", lambda e, h3=h3: e.tensor_reduce(
                        out=mh[:], in_=h3, axis=AX.X, op=ALU.max,
                        apply_absolute_value=True))
                    sch.emit("vector", lambda e: e.tensor_scalar(
                        out=s2h[:], in0=mh[:], scalar1=C224, scalar2=2e-30,
                        op0=ALU.mult, op1=ALU.max))
                    sch.emit("vector", lambda e: e.reciprocal(out=rrh[:],
                                                              in_=s2h[:]))
                    self.gap(3)
                    rrb = rrh[:].unsqueeze(2).broadcast_to([128, 16, 32])
                    s2b = s2h[:].unsqueeze(2).broadcast_to([128, 16, 32])
                    sch.emit("vector", lambda e, h3=h3, rrb=rrb: e.tensor_tensor(
                        out=h8[:], in0=h3, in1=rrb, op=ALU.mult))
                    sch.emit("vector", lambda e, s2b=s2b, t4=t4: e.tensor_tensor(
                        out=hq[:, t4 % 2, :], in0=h8[:], in1=s2b, op=ALU.mult),
                        inc="v")
                    vh = sch.now("v")
                    w = [("v", vh), ("s", sch.now("s"))]

                    def ftr(e, t4=t4):
                        last = None
                        for isub in range(4):
                            last = e.transpose(
                                self.pt0[:, isub * 128:(isub + 1) * 128],
                                hq[:, t4 % 2, isub * 128:(isub + 1) * 128],
                                self.ident[:])
                        return last
                    sch.emit("tensor", ftr, waits=w, inc="t")
                    ttr = sch.now("t")

                    def fev(e, ic=ic, t4=t4):
                        last = None
                        for isub in range(4):
                            last = e.copy(
                                out=hqT[:, ic * 4 + isub, t4 * 128:(t4 + 1) * 128],
                                in_=self.pt0[:, isub * 128:(isub + 1) * 128])
                        return last
                    sch.emit("scalar", fev, waits=[("t", ttr)], inc="s")

            SEVAC = sch.now("s")
            TGU_ALL = sch.now("t")
            for hcd in range(4):
                tload = sch.now("t")
                for icd in range(64):
                    w = [("dG", self.F0_DG)]
                    if icd == 0:
                        w.append(("t", tload if hcd >= 1 else TGU_ALL))
                    sch.dma("sync", lambda e, hcd=hcd, icd=icd:
                            e.dma_start_transpose(
                        out=wbig[:, icd, :],
                        in_=self.d_dw.ap()[hcd * 512:(hcd + 1) * 512,
                                           icd * 128:(icd + 1) * 128]), waits=w)
                DW = sch.now("dS")
                for t4 in range(4):
                    tt = sup * 4 + t4
                    xt = self.xt[(hcd * 4 + t4) % 2]
                    sch.dma("sync", lambda e, tt=tt, xt=xt, hcd=hcd: e.dma_start(
                        out=xt[:, hcd * 512:(hcd + 1) * 512],
                        in_=self.d_x1.ap()[tt * 128:(tt + 1) * 128,
                                           hcd * 512:(hcd + 1) * 512]),
                        waits=[("v", sch.now("v"))])
                    X1 = sch.now("dS")
                    pbank = self.ps_big[:, t4, :]
                    w = [("dS", DW), ("s", SEVAC if (hcd == 0 and t4 == 0) else sch.now("s")),
                         ("v", sch.now("v"))]

                    def fn(e, t4=t4, pbank=pbank):
                        last = None
                        for icd in range(64):
                            last = e.matmul(pbank,
                                            hqT[:, icd, t4 * 128:(t4 + 1) * 128],
                                            wbig[:, icd, :], start=(icd == 0),
                                            stop=(icd == 63))
                        return last
                    sch.emit("tensor", fn, waits=w, inc="t")
                    tdn = sch.now("t")
                    sl = t4 % 2
                    sch.emit("vector", lambda e, pbank=pbank, sl=sl, xt=xt,
                             hcd=hcd: e.tensor_tensor(
                        out=r2c[:, sl, :], in0=pbank,
                        in1=xt[:, hcd * 512:(hcd + 1) * 512], op=ALU.add),
                        waits=[("t", tdn), ("dS", X1)], inc="v")
                    sch.emit("vector", lambda e, sl=sl: e.tensor_reduce(
                        out=self.cols[:, 8:9], in_=r2c[:, sl, :], axis=AX.X,
                        op=ALU.add))
                    sch.emit("vector", lambda e, tt=tt: e.tensor_tensor(
                        out=self.ln_s[:, tt:tt + 1], in0=self.ln_s[:, tt:tt + 1],
                        in1=self.cols[:, 8:9], op=ALU.add))
                    sch.emit("scalar", lambda e, sl=sl: e.activation(
                        out=sqf[:], in_=r2c[:, sl, :],
                        func=ACT.Square, accum_out=self.cols[:, 9:10]),
                        waits=W("v"), inc="s")
                    sn2 = sch.now("s")
                    sch.emit("vector", lambda e, tt=tt: e.tensor_tensor(
                        out=self.ln_q[:, tt:tt + 1], in0=self.ln_q[:, tt:tt + 1],
                        in1=self.cols[:, 9:10], op=ALU.add),
                        waits=[("s", sn2)], inc="v")
                    vr2 = sch.now("v")
                    sch.dma("sync", lambda e, tt=tt, sl=sl, hcd=hcd: e.dma_start(
                        out=self.d_r2.ap()[tt * 128:(tt + 1) * 128,
                                           hcd * 512:(hcd + 1) * 512],
                        in_=r2c[:, sl, :]), waits=[("v", vr2)])
            R2D = sch.now("dS")

            for t4 in range(4):
                tt = sup * 4 + t4
                xt = self.xt[t4 % 2]
                sch.dma("sync", lambda e, tt=tt, xt=xt: e.dma_start(
                    out=xt[:], in_=self.d_r2.ap()[tt * 128:(tt + 1) * 128, :]),
                    waits=[("dS", R2D), ("v", sch.now("v"))])
                dsn = sch.now("dS")
                sch.emit("vector", lambda e, tt=tt: e.tensor_scalar(
                    out=self.cols[:, 2:3], in0=self.ln_s[:, tt:tt + 1],
                    scalar1=CH, scalar2=None, op0=ALU.mult),
                    waits=[("dS", dsn)])
                sch.emit("vector", lambda e, tt=tt: e.tensor_scalar(
                    out=self.cols[:, 4:5], in0=self.ln_q[:, tt:tt + 1],
                    scalar1=CH, scalar2=None, op0=ALU.mult))
                sch.emit("vector", lambda e: e.tensor_tensor(
                    out=self.cols[:, 3:4], in0=self.cols[:, 2:3],
                    in1=self.cols[:, 2:3], op=ALU.mult))
                sch.emit("vector", lambda e: e.tensor_tensor(
                    out=self.cols[:, 5:6], in0=self.cols[:, 4:5],
                    in1=self.cols[:, 3:4], op=ALU.subtract))
                sch.emit("vector", lambda e: e.tensor_scalar(
                    out=self.cols[:, 6:7], in0=self.cols[:, 5:6],
                    scalar1=EPS_LN, scalar2=None, op0=ALU.add), inc="v")
                sch.emit("scalar", lambda e: e.activation(
                    out=self.cols[:, 6:7], in_=self.cols[:, 6:7],
                    func=ACT.Sqrt), waits=W("v"), inc="s")
                sn = sch.now("s")
                sch.emit("vector", lambda e: e.reciprocal(
                    out=self.cols[:, 7:8], in_=self.cols[:, 6:7]),
                    waits=[("s", sn)])
                self.gap(3)
                sch.emit("vector", lambda e, xt=xt: e.tensor_scalar(
                    out=self.x1_t[:], in0=xt[:], scalar1=self.cols[:, 2:3],
                    scalar2=self.cols[:, 7:8], op0=ALU.subtract, op1=ALU.mult),
                    inc="v")
                vy = sch.now("v")
                sch.dma("sync", lambda e, tt=tt: e.dma_start(
                    out=self.o_y.ap()[tt * 128:(tt + 1) * 128, :],
                    in_=self.x1_t[:]), waits=[("v", vy)])


_NC_CACHE = {}


def _get_nc(debug=(), phases=("attn", "ffn"), paranoid=False):
    key = (tuple(sorted(debug)), tuple(sorted(phases)), paranoid)
    if key not in _NC_CACHE:
        kb = KB(debug=debug, phases=phases, paranoid=paranoid)
        _NC_CACHE[key] = (kb.build(), dict(kb.dbg), kb.cs)
    return _NC_CACHE[key]


def kernel(**inputs):
    nc, dbg, cs = _get_nc(paranoid=True)
    x_full = np.ascontiguousarray(inputs["L0_attn_hidden_states"])
    base = {
        "i_wq": np.ascontiguousarray(inputs["L0_attn_Wq"]),
        "i_wlk": np.ascontiguousarray(inputs["L0_attn_Wlk"]),
        "i_wlv": np.ascontiguousarray(inputs["L0_attn_Wlv"]),
        "i_wo": np.ascontiguousarray(inputs["L0_attn_Wo"]),
        "i_wk": np.ascontiguousarray(inputs["L0_attn_Wk"]),
        "i_wv": np.ascontiguousarray(inputs["L0_attn_Wv"]),
        "i_wc": np.ascontiguousarray(
            inputs["L0_attn_Wc"].reshape(NHC, 128).T),
        "i_bq": np.ascontiguousarray(
            inputs["L0_attn_bq"].reshape(NHC, 128).T),
        "i_blk": np.ascontiguousarray(
            inputs["L0_attn_blk"].reshape(NHC, 128).T),
        "i_blv": np.ascontiguousarray(inputs["L0_attn_blv"]),
        "i_bo": np.ascontiguousarray(inputs["L0_attn_bo"]),
        "i_bk": np.ascontiguousarray(inputs["L0_attn_bk"]),
        "i_bv": np.ascontiguousarray(inputs["L0_attn_bv"]),
        "i_bc": np.ascontiguousarray(inputs["L0_attn_bc"]),
        "i_qnw": np.ascontiguousarray(inputs["L0_attn_qn_w"]),
        "i_knw": np.ascontiguousarray(inputs["L0_attn_kn_w"]),
        "i_sk": np.ascontiguousarray(inputs["L0_attn_sink_k"].T),
        "i_sv": np.ascontiguousarray(inputs["L0_attn_sink_v"]),
        "i_gw": np.ascontiguousarray(inputs["L0_ffn_gate_w"]),
        "i_uw": np.ascontiguousarray(inputs["L0_ffn_up_w"]),
        "i_dw": np.ascontiguousarray(inputs["L0_ffn_down_w"]),
    }
    base.update(cs)
    in_maps = []
    for c in range(B):
        m = dict(base)
        m["i_x"] = np.ascontiguousarray(x_full[c])
        in_maps.append(m)
    res = run_bass_kernel_spmd(nc, in_maps, core_ids=list(range(B)))
    return np.stack([r["o_y"] for r in res.results], axis=0)


# version 02
# speedup vs baseline: 1.0066x; 1.0066x over previous
import math
from contextlib import ExitStack

import numpy as np
import ml_dtypes

import concourse.bass as bass
import concourse.mybir as mybir
from concourse.bass_utils import run_bass_kernel_spmd

F32 = mybir.dt.float32
BF16 = mybir.dt.bfloat16
FP8 = mybir.dt.float8e4
ALU = mybir.AluOpType
ACT = mybir.ActivationFunctionType
AX = mybir.AxisListType

B, S, H = 8, 2048, 2048
NH, HD, R, C, WIN = 16, 128, 16, 128, 128
ROPE, HALF = 64, 32
EPS_RMS, EPS_LN = 1e-6, 1e-5
II = 8192
SCALE = 1.0 / math.sqrt(HD)
NEG = -1.0e9
NT = 16
NSUP, TSUP = 4, 512
NG, HPG = 4, 4
NHC = 16
C224 = float(np.float32(1.0) / np.float32(224.0))
C128 = 1.0 / 128.0
CH = 1.0 / float(H)


def host_consts():
    cs = {}
    cs["c_ident"] = np.eye(128, dtype=np.float32).astype(ml_dtypes.bfloat16)
    cs["c_identf"] = np.eye(128, dtype=np.float32)
    swap = np.zeros((64, 64), np.float32)
    swap[:32, 32:] = np.eye(32)
    swap[32:, :32] = np.eye(32)
    cs["c_swap"] = swap.astype(ml_dtypes.bfloat16)
    inv_freq = 1.0 / 10000.0 ** (np.arange(HALF, dtype=np.float64) * 2.0 / ROPE)
    pos = np.arange(S, dtype=np.float64)
    ang = pos[None, :] * inv_freq[:, None]
    cs["c_cosd"] = np.concatenate([np.cos(ang)] * 2, 0).astype(np.float32).astype(ml_dtypes.bfloat16)
    cs["c_sins"] = np.concatenate([-np.sin(ang), np.sin(ang)], 0).astype(np.float32).astype(ml_dtypes.bfloat16)
    pc = np.arange(C, dtype=np.float64) * R + (R - 1)
    angc = pc[None, :] * inv_freq[:, None]
    cs["c_cosdc"] = np.concatenate([np.cos(angc)] * 2, 0).astype(np.float32).astype(ml_dtypes.bfloat16)
    cs["c_sinsc"] = np.concatenate([-np.sin(angc), np.sin(angc)], 0).astype(np.float32).astype(ml_dtypes.bfloat16)
    qi = np.arange(128)[:, None]
    j = np.arange(256)[None, :]
    gen = np.where((j >= qi) & (j <= qi + WIN), 0.0, NEG)
    t0 = np.where((j >= 128) & (j - 128 <= qi), 0.0, NEG)
    cs["c_lmask"] = np.concatenate([t0, gen], 1).astype(np.float32)
    cm = np.zeros((128, NT, 129), np.float32)
    cidx = np.arange(C)[None, :]
    for tt in range(NT):
        cm[:, tt, 1:] = np.where(cidx * R + (R - 1) <= tt * 128 + qi, 0.0, NEG)
    cs["c_cmask"] = cm.reshape(128, NT * 129)
    g = np.zeros((NT, 128, 128), np.float32)
    for tt in range(NT):
        for lt in range(128):
            g[tt, lt, 8 * tt + lt // 16] = 1.0
    cs["c_g"] = g.transpose(1, 0, 2).reshape(128, NT * 128).astype(ml_dtypes.bfloat16)
    e = np.zeros((128, 8, 8), np.float32)
    for rr in range(8):
        e[:, rr, rr] = 1.0
    cs["c_e"] = e.reshape(128, 64).astype(ml_dtypes.bfloat16)
    f = np.zeros((8, 8, 128), np.float32)
    for rr in range(8):
        f[rr, rr, :] = 1.0
    cs["c_f"] = f.reshape(8, 8 * 128).astype(ml_dtypes.bfloat16)
    return cs


class Sched:
    ENGINES = ("tensor", "vector", "scalar", "gpsimd", "sync")

    def __init__(self):
        self.steps = {e: [] for e in self.ENGINES}
        self.counts = {}
        self.junk_fn = None

    def emit(self, engine, fn, waits=(), inc=None, inc_n=1, gap=0):
        for _ in range(gap):
            self.steps[engine].append((self.junk_fn, [], []))
        self._emit(engine, fn, waits, inc, inc_n)

    def _emit(self, engine, fn, waits=(), inc=None, inc_n=1):
        incs = []
        if inc is not None:
            self.counts[inc] = self.counts.get(inc, 0) + inc_n
            incs = [(inc, inc_n)]
        self.steps[engine].append(
            (fn, [(s, v) for (s, v) in waits if v > 0], incs))

    def dma(self, engine, fn, waits=(), inc="dS"):
        self._emit(engine, fn, waits, inc, inc_n=16)

    def now(self, sem):
        return self.counts.get(sem, 0)

    def build(self, block, semmap):
        for ename in self.ENGINES:
            steps = self.steps[ename]
            if not steps:
                continue

            def make_body(steps=list(steps)):
                def body(eng):
                    for fn, waits, incs in steps:
                        for s, v in waits:
                            eng.wait_ge(semmap[s], v)
                        inst = fn(eng)
                        for s, n in incs:
                            assert inst is not None
                            inst.then_inc(semmap[s], n)
                return body
            getattr(block, ename)(make_body())


class KB:
    def __init__(self, debug=(), phases=("attn", "ffn"), paranoid=False):
        self.debug = set(debug)
        self.phases = set(phases)
        self.paranoid = paranoid
        nc = self.nc = bass.Bass("TRN2", target_bir_lowering=False)
        self.sch = Sched()
        self.sch.junk_fn = lambda e: e.tensor_copy(out=self.junk[:, 0:1],
                                                   in_=self.junk[:, 1:2])
        if paranoid:
            _orig_emit = self.sch.emit
            def _pemit(engine, fn, waits=(), inc=None, inc_n=1):
                if engine == "vector" and inc_n == 1:
                    _orig_emit("vector", lambda e: e.tensor_copy(
                        out=self.junk[:, 0:1], in_=self.junk[:, 1:2]))
                _orig_emit(engine, fn, waits, inc, inc_n)
            self.sch.emit = _pemit
        self.stk = ExitStack()
        self.dbg = {}

        D = lambda n, sh: nc.dram_tensor(n, sh, F32, kind="ExternalInput")
        self.i_x = D("i_x", [S, H])
        self.i_wq, self.i_wlk = D("i_wq", [H, H]), D("i_wlk", [H, H])
        self.i_wlv, self.i_wo = D("i_wlv", [H, H]), D("i_wo", [H, H])
        self.i_wk, self.i_wv = D("i_wk", [H, HD]), D("i_wv", [H, HD])
        self.i_wc = D("i_wc", [128, NHC])
        self.i_bq, self.i_blk = D("i_bq", [128, NHC]), D("i_blk", [128, NHC])
        self.i_blv, self.i_bo = D("i_blv", [H]), D("i_bo", [H])
        self.i_bk, self.i_bv, self.i_bc = D("i_bk", [HD]), D("i_bv", [HD]), D("i_bc", [1])
        self.i_qnw, self.i_knw = D("i_qnw", [HD]), D("i_knw", [HD])
        self.i_sk, self.i_sv = D("i_sk", [128, NH]), D("i_sv", [NH, HD])
        self.i_gw, self.i_uw = D("i_gw", [II, H]), D("i_uw", [II, H])
        self.i_dw = D("i_dw", [H, II])
        self.cs = host_consts()
        self.ch = {}
        for k, v in self.cs.items():
            dt = BF16 if v.dtype == ml_dtypes.bfloat16 else F32
            self.ch[k] = nc.dram_tensor(k, list(v.shape), dt, kind="ExternalInput")

        self.o_y = nc.dram_tensor("o_y", [S, H], F32, kind="ExternalOutput")
        DI = lambda n, sh, dt: nc.dram_tensor(n, sh, dt, kind="Internal")
        self.d_xbf = DI("d_xbf", [S, H], BF16)
        self.d_wrow = DI("d_wrow", [S], F32)
        self.d_ctx = DI("d_ctx", [H, S], BF16)
        self.d_x1 = DI("d_x1", [S, H], F32)
        self.d_xq = DI("d_xq", [S, H], BF16)
        self.d_r2 = DI("d_r2", [S, H], F32)
        self.d_wqb = DI("d_wqb", [H, H], BF16)
        self.d_wlkb = DI("d_wlkb", [H, H], BF16)
        self.d_wlvb = DI("d_wlvb", [H, H], BF16)
        self.d_wob = DI("d_wob", [H, H], BF16)
        self.d_wkb = DI("d_wkb", [H, HD], BF16)
        self.d_wvb = DI("d_wvb", [H, HD], BF16)
        self.d_rows = DI("d_rows", [4 * H], BF16)
        self.d_gw = DI("d_gw", [II, H], BF16)
        self.d_uw = DI("d_uw", [II, H], BF16)
        self.d_dw = DI("d_dw", [H, II], BF16)

        self.semmap = {n: self.stk.enter_context(nc.semaphore("sem_" + n))
                       for n in ("t", "v", "s", "g", "dS", "dG")}

        PS = lambda n, sh, dt: self.stk.enter_context(nc.psum_tensor(n, sh, dt))
        self.ps_big = PS("ps_big", [128, 4, 512], F32)
        self.pb4 = PS("pb4", [128, 512], F32)
        self.pb5 = PS("pb5", [128, 512], F32)
        self.pt0 = PS("pt0", [128, 1024], BF16)
        self.pb7 = PS("pb7", [128, 512], F32)

        SBs = self.SBs = lambda n, sh, dt: self.stk.enter_context(
            nc.sbuf_tensor(n, sh, dt))
        self.ident = SBs("ident", [128, 128], BF16)
        self.ones_row = SBs("ones_row", [1, 512], BF16)
        self.rows_b = SBs("rows_b", [1, 4 * H], BF16)
        self.xt = [SBs("xt_a", [128, H], F32), SBs("xt_b", [128, H], F32)]
        self.x1_t = SBs("x1_t", [128, H], F32)
        self.cols = SBs("cols", [128, 16], F32)
        self.ln_s = SBs("ln_s", [128, NT], F32)
        self.ln_q = SBs("ln_q", [128, NT], F32)
        self.junk = SBs("junk", [128, 4], F32)
        self.xfm = SBs("xfm", [128, NHC, TSUP], BF16)
        self.wcol = SBs("wcol", [128, NT], F32)
        self.sc_attn = ExitStack()
        SBa = self.SBa = lambda n, sh, dt: self.sc_attn.enter_context(
            nc.sbuf_tensor(n, sh, dt))
        self.swapm = SBa("swapm", [64, 64], BF16)
        self.cosd = SBa("cosd", [64, S], BF16)
        self.sins = SBa("sins", [64, S], BF16)
        self.cosdc = SBa("cosdc", [64, C], BF16)
        self.sinsc = SBa("sinsc", [64, C], BF16)
        self.lmask = SBa("lmask", [128, 512], F32)
        self.cmask = SBa("cmask", [128, NT * 129], F32)
        self.emat = SBa("emat", [128, 64], BF16)
        self.fmat = SBa("fmat", [8, 8 * 128], BF16)
        self.wc_bf = SBa("wc_bf", [128, NHC], BF16)
        self.wc_f = SBa("wc_f", [128, NHC], F32)
        self.bq_c = SBa("bq_c", [128, NHC], F32)
        self.blk_c = SBa("blk_c", [128, NHC], F32)
        self.bk_c = SBa("bk_c", [128, 1], F32)
        self.qnw_c = SBa("qnw_c", [128, 1], F32)
        self.knw_c = SBa("knw_c", [128, 1], F32)
        self.bc_c = SBa("bc_c", [1, 1], F32)
        self.bc_bf = SBa("bc_bf", [1, 1], BF16)
        self.bc128 = SBa("bc128", [128, 1], F32)
        self.sink_f = SBa("sink_f", [128, NH], F32)
        self.sinkb = SBa("sinkb", [128, NH], BF16)
        self.scb = SBa("scb", [128, NH, 129], BF16)
        self.cvb = SBa("cvb", [128, HD], BF16)
        self.lcol = SBa("lcol", [128, NT], F32)
        self.ident_f = SBa("ident_f", [128, 128], F32)
        self.lT = SBa("lT", [16, 128], F32)
        self.lm8 = SBa("lm8", [16, 8], F32)
        self.ls8 = SBa("ls8", [16, 8], F32)
        if "wrow" in self.debug:
            self.wrow_st = SBa("wrow_st", [1, S], F32)

        self.W = lambda *ss: [(s, self.sch.now(s)) for s in ss]
        self.a4_store_ds = []

    def mkdbg(self, name, shape):
        if name in self.debug:
            self.dbg[name] = self.nc.dram_tensor("dbg_" + name, shape, F32,
                                                 kind="ExternalOutput")
            return True
        return False

    def gap(self, n=3):
        for _ in range(n):
            self.sch.emit("vector", lambda e: e.tensor_copy(
                out=self.junk[:, 0:1], in_=self.junk[:, 1:2]))

    def build(self):
        sch, W = self.sch, self.W
        self.phase0_consts()
        self.sc_f0 = ExitStack()
        if "ffn" in self.phases:
            self.phase_f0()
        self.phase0_logits()
        with ExitStack() as sc0:
            self.phase0b(sc0)
        with ExitStack() as scA:
            self.phase_groups(scA)
        with ExitStack() as scA4:
            self.phase_a4(scA4)
        self.sc_f0.close()
        self.sc_attn.close()
        if "ffn" in self.phases:
            with ExitStack() as scF:
                self.phase_f(scF)
        if "xqd" in self.debug:
            self.dbg["xqd"] = self.nc.dram_tensor("dbg_xqd", [S, H], F32,
                                                  kind="ExternalOutput")
            sch.dma("gpsimd", lambda e: e.dma_start(out=self.dbg["xqd"].ap(),
                                                    in_=self.d_xq.ap()),
                    waits=W("dS", "v"), inc="dG")
        if "r2d" in self.debug:
            self.dbg["r2d"] = self.nc.dram_tensor("dbg_r2d", [S, H], F32,
                                                  kind="ExternalOutput")
            sch.dma("gpsimd", lambda e: e.dma_start(out=self.dbg["r2d"].ap(),
                                                    in_=self.d_r2.ap()),
                    waits=W("dS", "v"), inc="dG")
        if "lns" in self.debug:
            self.dbg["lns"] = self.nc.dram_tensor("dbg_lns", [128, NT], F32,
                                                  kind="ExternalOutput")
            self.dbg["lnq"] = self.nc.dram_tensor("dbg_lnq", [128, NT], F32,
                                                  kind="ExternalOutput")
            sch.dma("sync", lambda e: e.dma_start(out=self.dbg["lns"].ap(),
                                                  in_=self.ln_s[:]),
                    waits=W("v"))
            sch.dma("sync", lambda e: e.dma_start(out=self.dbg["lnq"].ap(),
                                                  in_=self.ln_q[:]),
                    waits=W("v"))
        sch.emit("gpsimd", lambda e: e.nop(),
                 waits=W("dS", "dG", "v", "t", "s", "g"))
        with self.stk:
            blk = self.stk.enter_context(self.nc.Block())
            sch.build(blk, self.semmap)
        return self.nc

    def phase0_consts(self):
        sch = self.sch
        ld = lambda out, in_: sch.dma("sync", lambda e, o=out, i=in_: e.dma_start(
            out=o, in_=i))
        ch = self.ch
        ld(self.ident[:], ch["c_ident"].ap())
        ld(self.swapm[:], ch["c_swap"].ap())
        ld(self.cosd[:], ch["c_cosd"].ap())
        ld(self.sins[:], ch["c_sins"].ap())
        ld(self.cosdc[:], ch["c_cosdc"].ap())
        ld(self.sinsc[:], ch["c_sinsc"].ap())
        ld(self.lmask[:], ch["c_lmask"].ap())
        ld(self.cmask[:], ch["c_cmask"].ap())
        ld(self.emat[:], ch["c_e"].ap())
        ld(self.ident_f[:], ch["c_identf"].ap())
        ld(self.fmat[:], ch["c_f"].ap())
        ld(self.wc_f[:], self.i_wc.ap())
        ld(self.bq_c[:], self.i_bq.ap())
        ld(self.blk_c[:], self.i_blk.ap())
        ld(self.bk_c[:], self.i_bk.ap().rearrange("(o p) -> p o", p=128))
        ld(self.qnw_c[:], self.i_qnw.ap().rearrange("(o p) -> p o", p=128))
        ld(self.knw_c[:], self.i_knw.ap().rearrange("(o p) -> p o", p=128))
        ld(self.bc_c[:], self.i_bc.ap().rearrange("(o p) -> p o", p=1))
        ld(self.sink_f[:], self.i_sk.ap())
        self.CONST_DS = sch.now("dS")
        lg = lambda out, in_: sch.dma("gpsimd", lambda e, o=out, i=in_:
                                      e.dma_start(out=o, in_=i), inc="dG")
        lg(self.d_rows.ap()[0:H], self.i_blv.ap())
        lg(self.d_rows.ap()[H:2 * H], self.i_bo.ap())
        lg(self.d_rows.ap()[2 * H:2 * H + HD], self.i_bv.ap())
        lg(self.d_rows.ap()[3 * H:4 * H], self.i_sv.ap().rearrange("h d -> (h d)"))
        rg = sch.now("dG")
        sch.dma("sync", lambda e: e.dma_start(
            out=self.rows_b[:], in_=self.d_rows.ap().rearrange("(o h) -> o h", o=1)),
            waits=[("dG", rg)])
        lg(self.d_wkb.ap(), self.i_wk.ap())
        lg(self.d_wvb.ap(), self.i_wv.ap())
        self.WKVMIR_DG = sch.now("dG")
        lg(self.d_wqb.ap(), self.i_wq.ap())
        lg(self.d_wlkb.ap(), self.i_wlk.ap())
        lg(self.d_wlvb.ap(), self.i_wlv.ap())
        self.WQKV_DG = sch.now("dG")
        lg(self.d_wob.ap(), self.i_wo.ap())
        self.WO_DG = sch.now("dG")
        self.ROWS_DG = sch.now("dG")

        sch.dma("gpsimd", lambda e: e.dma_start(out=self.d_xbf.ap(),
                                                in_=self.i_x.ap()), inc="dG")
        self.XBF_DG = sch.now("dG")

        sch.emit("vector", lambda e: e.memset(self.ones_row[:], 1.0))
        sch.emit("vector", lambda e: e.memset(self.junk[:], 0.0))
        sch.emit("vector", lambda e: e.memset(self.ln_s[:], 0.0))
        sch.emit("vector", lambda e: e.memset(self.ln_q[:], 0.0))
        sch.emit("vector", lambda e: e.tensor_copy(out=self.sinkb[:],
                                                   in_=self.sink_f[:]),
                 waits=[("dS", self.CONST_DS)])
        sch.emit("vector", lambda e: e.tensor_copy(out=self.wc_bf[:],
                                                   in_=self.wc_f[:]))
        sch.emit("vector", lambda e: e.tensor_copy(out=self.bc_bf[:],
                                                   in_=self.bc_c[:]), inc="v")
        self.V_SMALL = sch.now("v")

    def phase_f0(self):
        sch = self.sch
        SB = lambda n, sh, dt: self.sc_f0.enter_context(
            self.nc.sbuf_tensor(n, sh, dt))
        FW = 1024
        NBW = FW // 32
        wtile = [SB("f0_wa", [128, FW], F32), SB("f0_wb", [128, FW], F32)]
        w8 = SB("f0_q8", [128, FW], FP8)
        wqd = SB("f0_qd", [128, FW], BF16)
        mw = SB("f0_m", [128, NBW], F32)
        s2w = SB("f0_s2", [128, NBW], F32)
        rrw = SB("f0_rr", [128, NBW], F32)

        jobs = []
        for src_, dst, nrow in ((self.i_gw, self.d_gw, II),
                                (self.i_uw, self.d_uw, II),
                                (self.i_dw, self.d_dw, H)):
            ncol = H if nrow == II else II
            for r0 in range(0, nrow, 128):
                for c0 in range(0, ncol, FW):
                    jobs.append((src_, dst, r0, c0))

        st_g = {}
        for k, (src_, dst, r0, c0) in enumerate(jobs):
            buf = wtile[k % 2]
            w = [("g", st_g[k - 2])] if k - 2 in st_g else []
            sch.dma("gpsimd", lambda e, buf=buf, src_=src_, r0=r0, c0=c0:
                    e.dma_start(out=buf[:],
                                in_=src_.ap()[r0:r0 + 128, c0:c0 + FW]),
                    waits=w, inc="dG")
            ldn = sch.now("dG")
            b3 = buf[:].rearrange("p (b k) -> p b k", k=32)
            sch.emit("vector", lambda e, b3=b3: e.tensor_reduce(
                out=mw[:], in_=b3, axis=AX.X, op=ALU.max,
                apply_absolute_value=True),
                waits=[("dG", ldn), ("g", sch.now("g"))])
            sch.emit("vector", lambda e: e.tensor_scalar(
                out=s2w[:], in0=mw[:], scalar1=C224, scalar2=2e-30,
                op0=ALU.mult, op1=ALU.max), gap=1)
            sch.emit("vector", lambda e: e.reciprocal(out=rrw[:], in_=s2w[:]),
                     inc="v", gap=1)
            vn = sch.now("v")
            rrb = rrw[:].unsqueeze(2).broadcast_to([128, NBW, 32])
            s2b = s2w[:].unsqueeze(2).broadcast_to([128, NBW, 32])
            sch.emit("gpsimd", lambda e, b3=b3, rrb=rrb: e.tensor_tensor(
                out=w8[:], in0=b3, in1=rrb, op=ALU.mult), waits=[("v", vn)])
            sch.emit("gpsimd", lambda e, s2b=s2b: e.tensor_tensor(
                out=wqd[:], in0=w8[:], in1=s2b, op=ALU.mult), inc="g")
            st_g[k] = sch.now("g")
            sch.dma("gpsimd", lambda e, dst=dst, r0=r0, c0=c0: e.dma_start(
                out=dst.ap()[r0:r0 + 128, c0:c0 + FW], in_=wqd[:]), inc="dG")
        self.F0_DG = sch.now("dG")
        self.F0_G = sch.now("g")

    def phase0_logits(self):
        sch, W = self.sch, self.W
        lcol = self.lcol

        for sup in range(NSUP):
            tprev = sch.now("t")
            for hc in range(NHC):
                out = self.xfm[:, hc, :]
                in_ = self.d_xbf.ap()[sup * TSUP:(sup + 1) * TSUP,
                                      hc * 128:(hc + 1) * 128]
                w = [("dG", self.XBF_DG)]
                if sup > 0 and hc == 0:
                    w.append(("t", tprev))
                sch.dma("sync", lambda e, o=out, i=in_: e.dma_start_transpose(
                    out=o, in_=i), waits=w)
            dsn = sch.now("dS")

            def fn_t(e, sup=sup):
                last = None
                for t4 in range(4):
                    tt = sup * 4 + t4
                    for hc in range(NHC):
                        last = e.matmul(
                            self.pb7[:, tt:tt + 1],
                            self.xfm[:, hc, t4 * 128:(t4 + 1) * 128],
                            self.wc_bf[:, hc:hc + 1], start=(hc == 0),
                            stop=(hc == NHC - 1))
                return last
            sch.emit("tensor", fn_t,
                     waits=[("dS", dsn),
                            ("v", self.V_SMALL if sup == 0 else sch.now("v"))],
                     inc="t")
        tn = sch.now("t")
        sch.emit("tensor", lambda e: e.matmul(
            self.pb5[:, 0:1], self.ones_row[0:1, 0:128], self.bc_bf[:],
            start=True, stop=True), waits=[("v", self.V_SMALL)], inc="t")
        tbc = sch.now("t")
        sch.emit("vector", lambda e: e.tensor_copy(out=self.bc128[:],
                                                   in_=self.pb5[:, 0:1]),
                 waits=[("t", tbc)])
        sch.emit("vector", lambda e: e.tensor_tensor(
            out=lcol[:], in0=self.pb7[:, 0:NT],
            in1=self.bc128[:].broadcast_to([128, NT]), op=ALU.add),
            waits=[("t", tn)], inc="v", gap=1)
        vl = sch.now("v")
        if self.mkdbg("lcol", [128, NT]):
            sch.dma("sync", lambda e: e.dma_start(out=self.dbg["lcol"].ap(),
                                                  in_=lcol[:]),
                    waits=[("v", vl)])
        if "xfm" in self.debug:
            self.dbg["xfm"] = self.nc.dram_tensor("dbg_xfm", [128, NHC * TSUP],
                                                  BF16, kind="ExternalOutput")
            dsx = sch.now("dS")
            sch.dma("sync", lambda e: e.dma_start(
                out=self.dbg["xfm"].ap(),
                in_=self.xfm[:].rearrange("p c t -> p (c t)")),
                waits=[("dS", dsx)])
        sch.emit("tensor", lambda e: e.transpose(
            self.pb5[0:16, 0:128], lcol[:], self.ident_f[:]),
            waits=[("v", vl)], inc="t")
        t1 = sch.now("t")
        sch.emit("vector", lambda e: e.tensor_copy(out=self.lT[:],
                                                   in_=self.pb5[0:16, 0:128]),
                 waits=[("t", t1)], inc="v")
        l3 = self.lT[:].rearrange("p (b r) -> p b r", r=R)
        sch.emit("vector", lambda e, l3=l3: e.tensor_reduce(
            out=self.lm8[:], in_=l3, axis=AX.X, op=ALU.max), gap=1)
        sch.emit("vector", lambda e, l3=l3: e.tensor_tensor(
            out=l3, in0=l3,
            in1=self.lm8[:].unsqueeze(2).broadcast_to([16, 8, R]),
            op=ALU.subtract), inc="v", gap=1)
        sch.emit("scalar", lambda e: e.activation(
            out=self.lT[:], in_=self.lT[:], func=ACT.Exp), waits=W("v"),
            inc="s")
        sch.emit("vector", lambda e, l3=l3: e.tensor_reduce(
            out=self.ls8[:], in_=l3, axis=AX.X, op=ALU.add), waits=W("s"))
        sch.emit("vector", lambda e: e.reciprocal(out=self.ls8[:],
                                                  in_=self.ls8[:]), gap=1)
        sch.emit("vector", lambda e, l3=l3: e.tensor_tensor(
            out=l3, in0=l3,
            in1=self.ls8[:].unsqueeze(2).broadcast_to([16, 8, R]),
            op=ALU.mult), inc="v", gap=1)
        vw = sch.now("v")
        sch.emit("tensor", lambda e: e.transpose(
            self.pb5[0:128, 0:16], self.lT[:], self.ident_f[0:16, 0:16]),
            waits=[("v", vw)], inc="t")
        t2 = sch.now("t")
        sch.emit("vector", lambda e: e.tensor_copy(out=self.wcol[:],
                                                   in_=self.pb5[0:128, 0:16]),
                 waits=[("t", t2)], inc="v")
        self.V_WCOL = sch.now("v")
        if self.mkdbg("wrow", [128, NT]):
            sch.dma("sync", lambda e: e.dma_start(out=self.dbg["wrow"].ap(),
                                                  in_=self.wcol[:]),
                    waits=[("v", self.V_WCOL)])

    def phase0b(self, sc):
        sch, W, nc = self.sch, self.W, self.nc
        SB = lambda n, sh, dt: sc.enter_context(nc.sbuf_tensor(n, sh, dt))
        gmat = SB("gmat", [128, NT * 128], BF16)
        sch.dma("sync", lambda e: e.dma_start(out=gmat[:],
                                              in_=self.ch["c_g"].ap()))
        xw_bf = SB("xw_bf", [128, H], BF16)
        entries_b = SB("entries_b", [128, H], BF16)
        entries_fm = SB("entries_fm", [128, NHC, 128], BF16)
        wkb = SB("wkb", [128, NHC, HD], BF16)
        wvb = SB("wvb", [128, NHC, HD], BF16)
        ck_sb = SB("ck_sb", [128, C], F32)
        sq_ck = SB("sq_ck", [128, C], BF16)
        rs8 = SB("rs8", [8, 128], F32)
        rs8b = SB("rs8b", [8, 128], BF16)
        ckb2 = SB("ckb2", [128, C], BF16)
        cksw = SB("cksw", [64, C], BF16)
        rp1 = SB("rp1", [64, 2 * C], BF16)

        WKVm = self.WKVMIR_DG
        sch.dma("sync", lambda e: e.dma_start(
            out=wkb[:], in_=self.d_wkb.ap().rearrange("(c p) d -> p c d", p=128)),
            waits=[("dG", WKVm)])
        sch.dma("sync", lambda e: e.dma_start(
            out=wvb[:], in_=self.d_wvb.ap().rearrange("(c p) d -> p c d", p=128)),
            waits=[("dG", WKVm)])
        WKV = sch.now("dS")
        if "wkb" in self.debug:
            self.dbg["wkb"] = self.nc.dram_tensor("dbg_wkb", [128, NHC * HD],
                                                  BF16, kind="ExternalOutput")
            sch.dma("sync", lambda e: e.dma_start(
                out=self.dbg["wkb"].ap(),
                in_=wkb[:].rearrange("p c d -> p (c d)")), waits=[("dS", WKV)])

        for tt in range(NT):
            xt = self.xt[tt % 2]
            wv = [("v", sch.now("v"))] if tt >= 2 else []
            sch.dma("sync", lambda e, xt=xt, tt=tt: e.dma_start(
                out=xt[:], in_=self.i_x.ap()[tt * 128:(tt + 1) * 128, :]),
                waits=wv)
            dsn = sch.now("dS")
            sch.emit("vector", lambda e, xt=xt, tt=tt: e.tensor_scalar(
                out=xw_bf[:], in0=xt[:], scalar1=self.wcol[:, tt:tt + 1],
                scalar2=None, op0=ALU.mult),
                waits=[("dS", dsn), ("v", self.V_WCOL), ("t", sch.now("t"))],
                inc="v", gap=(2 if tt == 0 else 0))
            vn = sch.now("v")

            def fn(e, tt=tt):
                last = None
                for j in range(4):
                    last = e.matmul(self.ps_big[:, j, :],
                                    gmat[:, tt * 128:(tt + 1) * 128],
                                    xw_bf[:, j * 512:(j + 1) * 512],
                                    start=(tt == 0), stop=(tt == NT - 1))
                return last
            sch.emit("tensor", fn, waits=[("v", vn)], inc="t")
        t_ent = sch.now("t")
        sch.emit("vector", lambda e: e.tensor_copy(out=entries_b[:],
                                                   in_=self.ps_big[:, :, :]),
                 waits=[("t", t_ent)], inc="v")
        ven = sch.now("v")
        if self.mkdbg("entries", [C, H]):
            ent_f = SB("ent_f", [128, H], F32)
            sch.emit("vector", lambda e: e.tensor_copy(out=ent_f[:],
                                                       in_=self.ps_big[:, :, :]),
                     waits=[("t", t_ent)], inc="v")
            sch.dma("sync", lambda e: e.dma_start(out=self.dbg["entries"].ap(),
                                                  in_=ent_f[:]), waits=W("v"))
        for hc in range(NHC):
            w = [("v", ven)]
            if hc >= 2:
                w = [("v", sch.now("v"))]
            sch.emit("tensor", lambda e, hc=hc: e.transpose(
                self.pt0[:, (hc % 2) * 512:(hc % 2) * 512 + 128],
                entries_b[:, hc * 128:(hc + 1) * 128], self.ident[:]),
                waits=w, inc="t")
            tn = sch.now("t")
            sch.emit("vector", lambda e, hc=hc: e.tensor_copy(
                out=entries_fm[:, hc, :],
                in_=self.pt0[:, (hc % 2) * 512:(hc % 2) * 512 + 128]),
                waits=[("t", tn)], inc="v")
        vfm = sch.now("v")
        if "efm" in self.debug:
            self.dbg["efm"] = self.nc.dram_tensor("dbg_efm", [128, NHC * 128],
                                                  BF16, kind="ExternalOutput")
            sch.dma("sync", lambda e: e.dma_start(
                out=self.dbg["efm"].ap(),
                in_=entries_fm[:].rearrange("p c t -> p (c t)")),
                waits=[("v", vfm)])

        def fn_ck(e):
            last = None
            for hc in range(NHC):
                last = e.matmul(self.pb4[:, 0:128], wkb[:, hc, :],
                                entries_fm[:, hc, :], start=(hc == 0),
                                stop=(hc == NHC - 1))
            return last
        sch.emit("tensor", fn_ck, waits=[("v", vfm), ("dS", WKV)], inc="t")
        tck = sch.now("t")

        def fn_cv(e):
            for hc in range(NHC):
                e.matmul(self.pb5[:, 0:128], entries_fm[:, hc, :], wvb[:, hc, :],
                         start=(hc == 0), stop=False)
            return e.matmul(self.pb5[:, 0:128], self.ones_row[0:1, 0:128],
                            self.rows_b[0:1, 2 * H:2 * H + HD], start=False,
                            stop=True)
        sch.emit("tensor", fn_cv, waits=[("v", vfm), ("dS", WKV)], inc="t")
        tcv = sch.now("t")
        sch.emit("vector", lambda e: e.tensor_copy(out=self.cvb[:],
                                                   in_=self.pb5[:, 0:128]),
                 waits=[("t", tcv)], inc="v")
        if self.mkdbg("cv", [C, HD]):
            cv_f = SB("cv_f", [128, HD], F32)
            sch.emit("vector", lambda e: e.tensor_copy(out=cv_f[:],
                                                       in_=self.pb5[:, 0:128]),
                     inc="v")
            sch.dma("sync", lambda e: e.dma_start(out=self.dbg["cv"].ap(),
                                                  in_=cv_f[:]), waits=W("v"))

        sch.emit("vector", lambda e: e.tensor_scalar(
            out=ck_sb[:], in0=self.pb4[:, 0:128], scalar1=self.bk_c[:],
            scalar2=None, op0=ALU.add), waits=[("t", tck)], inc="v")
        if self.mkdbg("cklin", [HD, C]):
            sch.dma("sync", lambda e: e.dma_start(out=self.dbg["cklin"].ap(),
                                                  in_=ck_sb[:]), waits=W("v"))
        sch.emit("scalar", lambda e: e.activation(out=sq_ck[:], in_=ck_sb[:],
                                                  func=ACT.Square),
                 waits=W("v"), inc="s")
        sch.emit("tensor", lambda e: e.matmul(
            self.pb4[0:8, 128:256], self.emat[:, 0:8], sq_ck[:], start=True,
            stop=True), waits=W("s"), inc="t")
        sch.emit("vector", lambda e: e.tensor_scalar(
            out=rs8[0:8, :], in0=self.pb4[0:8, 128:256], scalar1=C128,
            scalar2=EPS_RMS, op0=ALU.mult, op1=ALU.add), waits=W("t"), inc="v")
        sch.emit("scalar", lambda e: e.activation(out=rs8[0:8, :],
                                                  in_=rs8[0:8, :], func=ACT.Sqrt),
                 waits=W("v"), inc="s")
        sch.emit("vector", lambda e: e.reciprocal(out=rs8[0:8, :],
                                                  in_=rs8[0:8, :]), waits=W("s"))
        sch.emit("vector", lambda e: e.tensor_copy(out=rs8b[0:8, :],
                                                   in_=rs8[0:8, :]), inc="v",
                 gap=1)
        sch.emit("tensor", lambda e: e.matmul(
            self.pb5[:, 128:256], self.fmat[0:8, 0:128], rs8b[0:8, :],
            start=True, stop=True), waits=W("v"), inc="t")
        sch.emit("vector", lambda e: e.scalar_tensor_tensor(
            out=ck_sb[:], in0=ck_sb[:], scalar=self.knw_c[:],
            in1=self.pb5[:, 128:256], op0=ALU.mult, op1=ALU.mult),
            waits=W("t"), inc="v")
        sch.emit("vector", lambda e: e.tensor_copy(out=ckb2[:], in_=ck_sb[:]),
                 inc="v", gap=1)
        sch.emit("tensor", lambda e: e.matmul(
            self.pb7[0:64, 0:128], self.swapm[:], ckb2[0:64, :], start=True,
            stop=True), waits=W("v"), inc="t")
        sch.emit("vector", lambda e: e.tensor_copy(
            out=cksw[0:64, :], in_=self.pb7[0:64, 0:128]), waits=W("t"))
        sch.emit("vector", lambda e: e.tensor_tensor(
            out=rp1[0:64, 0:C], in0=cksw[0:64, :], in1=self.sinsc[:],
            op=ALU.mult), gap=1)
        sch.emit("vector", lambda e: e.tensor_tensor(
            out=rp1[0:64, C:2 * C], in0=ckb2[0:64, :], in1=self.cosdc[:],
            op=ALU.mult))
        sch.emit("vector", lambda e: e.tensor_tensor(
            out=ckb2[0:64, :], in0=rp1[0:64, C:2 * C], in1=rp1[0:64, 0:C],
            op=ALU.add), inc="v", gap=1)
        vck = sch.now("v")
        if self.mkdbg("ck", [HD, C]):
            ck_f = SB("ck_f", [128, C], F32)
            sch.emit("vector", lambda e: e.tensor_copy(out=ck_f[:], in_=ckb2[:]),
                     waits=[("v", vck)], inc="v")
            sch.dma("sync", lambda e: e.dma_start(out=self.dbg["ck"].ap(),
                                                  in_=ck_f[:]), waits=W("v"))

        def fn_scb(e):
            last = None
            for h in range(NH):
                e.tensor_copy(out=self.scb[:, h, 0:1],
                              in_=self.sinkb[:, h:h + 1])
                last = e.tensor_copy(out=self.scb[:, h, 1:129], in_=ckb2[:])
            return last
        sch.emit("vector", fn_scb, waits=[("v", vck)], inc="v")

    def phase_groups(self, sc):
        sch, W, nc = self.sch, self.W, self.nc
        SB = lambda n, sh, dt: sc.enter_context(nc.sbuf_tensor(n, sh, dt))
        wq_g = SB("wq_g", [128, NHC, 512], BF16)
        wlk_g = SB("wlk_g", [128, NHC, 512], BF16)
        wlv_g = SB("wlv_g", [128, NHC, 512], BF16)
        qn_sb = SB("qn_sb", [128, HPG, TSUP], F32)
        kn_sb = SB("kn_sb", [128, HPG, TSUP], F32)
        sq_bf = SB("sq_bf", [128, 2 * HPG, TSUP], BF16)
        q_sup = SB("q_sup", [128, HPG, TSUP], BF16)
        lk_sup = [SB("lk_supA", [128, HPG, TSUP], BF16),
                  SB("lk_supB", [128, HPG, TSUP], BF16)]
        lv_sup = [SB("lv_supA", [128, 4, HPG * 128], BF16),
                  SB("lv_supB", [128, 4, HPG * 128], BF16)]
        rs8 = SB("rs8g", [8, TSUP], F32)
        rs8b = SB("rs8gb", [8, TSUP], BF16)
        rope_sc = SB("rope_sc", [64, 4, TSUP], BF16)
        ms_c = SB("ms_c", [128, 2, 129], F32)
        ms_l = SB("ms_l", [128, 2, 256], F32)
        es_c = SB("es_c", [128, 2, 129], F32)
        es_l = SB("es_l", [128, 2, 256], F32)
        nm_c = SB("nm_c", [128, 4], F32)
        nm_l = SB("nm_l", [128, 4], F32)
        dn_c = SB("dn_c", [128, 4], F32)
        dn_l = SB("dn_l", [128, 4], F32)
        rc_c = SB("rc_c", [128, 4], F32)
        rc_l = SB("rc_l", [128, 4], F32)
        p_bf = SB("p_bf", [128, 2, 385], BF16)
        pT_sb = SB("pT_sb", [128, 2, 512], BF16)
        ctx_st = SB("ctx_st", [128, 2, HPG, 128], BF16)

        self.ctx_ds = []

        for g in range(NG):
            gsl = slice(g * 512, (g + 1) * 512)
            wload = [("dG", self.WQKV_DG), ("t", sch.now("t")),
                     ("v", sch.now("v"))]
            sch.dma("sync", lambda e, gsl=gsl: e.dma_start(
                out=wq_g[:], in_=self.d_wqb.ap()[:, gsl].rearrange(
                    "(c p) m -> p c m", p=128)), waits=wload)
            sch.dma("sync", lambda e, gsl=gsl: e.dma_start(
                out=wlk_g[:], in_=self.d_wlkb.ap()[:, gsl].rearrange(
                    "(c p) m -> p c m", p=128)))
            sch.dma("sync", lambda e, gsl=gsl: e.dma_start(
                out=wlv_g[:], in_=self.d_wlvb.ap()[:, gsl].rearrange(
                    "(c p) m -> p c m", p=128)))
            WG = sch.now("dS")
            sch.emit("vector", lambda e: e.memset(
                lk_sup[1][:, :, TSUP - 128:TSUP], 0.0),
                waits=[("t", sch.now("t"))])
            sch.emit("vector", lambda e: e.memset(lv_sup[1][:, 3, :], 0.0),
                     inc="v")

            for sup in range(NSUP):
                par = sup % 2
                tprev = sch.now("t")
                for hc in range(NHC):
                    out = self.xfm[:, hc, :]
                    in_ = self.d_xbf.ap()[sup * TSUP:(sup + 1) * TSUP,
                                          hc * 128:(hc + 1) * 128]
                    sch.dma("sync", lambda e, o=out, i=in_:
                            e.dma_start_transpose(out=o, in_=i),
                            waits=[("t", tprev)] if hc == 0 else ())
                XFM = sch.now("dS")

                vprev = sch.now("v")

                def fn_q(e):
                    last = None
                    for h in range(HPG):
                        for hc in range(NHC):
                            last = e.matmul(
                                self.ps_big[:, h, :],
                                wq_g[:, hc, h * 128:(h + 1) * 128],
                                self.xfm[:, hc, :], start=(hc == 0),
                                stop=(hc == NHC - 1))
                    return last
                sch.emit("tensor", fn_q,
                         waits=[("dS", max(XFM, WG)), ("v", vprev)], inc="t")
                tq = sch.now("t")
                for h in range(HPG):
                    hg = g * HPG + h
                    sch.emit("vector", lambda e, h=h, hg=hg: e.tensor_scalar(
                        out=qn_sb[:, h, :], in0=self.ps_big[:, h, :],
                        scalar1=self.bq_c[:, hg:hg + 1], scalar2=None,
                        op0=ALU.add), waits=[("t", tq)], inc="v")
                    sch.emit("scalar", lambda e, h=h: e.activation(
                        out=sq_bf[:, h, :], in_=qn_sb[:, h, :],
                        func=ACT.Square), waits=W("v"), inc="s")
                vq_evac = sch.now("v")

                def fn_lk(e):
                    last = None
                    for h in range(HPG):
                        for hc in range(NHC):
                            last = e.matmul(
                                self.ps_big[:, h, :],
                                wlk_g[:, hc, h * 128:(h + 1) * 128],
                                self.xfm[:, hc, :], start=(hc == 0),
                                stop=(hc == NHC - 1))
                    return last
                sch.emit("tensor", fn_lk, waits=[("v", vq_evac)], inc="t")
                tlk = sch.now("t")
                for h in range(HPG):
                    hg = g * HPG + h
                    sch.emit("vector", lambda e, h=h, hg=hg: e.tensor_scalar(
                        out=kn_sb[:, h, :], in0=self.ps_big[:, h, :],
                        scalar1=self.blk_c[:, hg:hg + 1], scalar2=None,
                        op0=ALU.add), waits=[("t", tlk)], inc="v")
                    sch.emit("scalar", lambda e, h=h: e.activation(
                        out=sq_bf[:, HPG + h, :], in_=kn_sb[:, h, :],
                        func=ACT.Square), waits=W("v"), inc="s")
                vlk_evac = sch.now("v")
                sq_all = sch.now("s")

                def fn_e(e):
                    last = None
                    for r in range(8):
                        last = e.matmul(self.pb4[0:8, 0:TSUP],
                                        self.emat[:, r * 8:(r + 1) * 8],
                                        sq_bf[:, r, :], start=(r == 0),
                                        stop=(r == 7))
                    return last
                sch.emit("tensor", fn_e,
                         waits=[("s", sq_all), ("v", sch.now("v"))], inc="t")
                te = sch.now("t")

                def fn_lv(e, g=g):
                    last = None
                    for t4 in range(4):
                        for hc in range(NHC):
                            e.matmul(self.ps_big[:, t4, :],
                                     self.xfm[:, hc, t4 * 128:(t4 + 1) * 128],
                                     wlv_g[:, hc, :], start=(hc == 0),
                                     stop=False)
                        last = e.matmul(self.ps_big[:, t4, :],
                                        self.ones_row[0:1, 0:128],
                                        self.rows_b[0:1, g * 512:(g + 1) * 512],
                                        start=False, stop=True)
                    return last
                sch.emit("tensor", fn_lv, waits=[("v", vlk_evac)], inc="t")
                tlv = sch.now("t")

                sch.emit("vector", lambda e: e.tensor_scalar(
                    out=rs8[0:8, :], in0=self.pb4[0:8, 0:TSUP], scalar1=C128,
                    scalar2=EPS_RMS, op0=ALU.mult, op1=ALU.add),
                    waits=[("t", te)], inc="v")
                sch.emit("scalar", lambda e: e.activation(
                    out=rs8[0:8, :], in_=rs8[0:8, :], func=ACT.Sqrt),
                    waits=W("v"), inc="s")
                sch.emit("vector", lambda e: e.reciprocal(
                    out=rs8[0:8, :], in_=rs8[0:8, :]), waits=W("s"))
                sch.emit("vector", lambda e: e.tensor_copy(
                    out=rs8b[0:8, :], in_=rs8[0:8, :]), inc="v", gap=1)
                vrs = sch.now("v")

                for t4 in range(4):
                    sch.emit("vector", lambda e, t4=t4, par=par: e.tensor_copy(
                        out=lv_sup[par][:, t4, :], in_=self.ps_big[:, t4, :]),
                        waits=[("t", tlv)], inc="v")
                vlv = sch.now("v")

                cs_sl = self.cosd[:, sup * TSUP:(sup + 1) * TSUP]
                sn_sl = self.sins[:, sup * TSUP:(sup + 1) * TSUP]
                for r in range(8):
                    bank = [self.pb5, self.pb7][r % 2]
                    is_q = r < HPG
                    h = r % HPG
                    dst = q_sup if is_q else lk_sup[par]
                    src = qn_sb if is_q else kn_sb
                    wcol_ = self.qnw_c if is_q else self.knw_c
                    sch.emit("tensor", lambda e, r=r, bank=bank: e.matmul(
                        bank[:, 0:TSUP], self.fmat[0:8, r * 128:(r + 1) * 128],
                        rs8b[0:8, :], start=True, stop=True),
                        waits=[("v", sch.now("v"))], inc="t")
                    tb = sch.now("t")
                    sch.emit("vector", lambda e, h=h, dst=dst, src=src,
                             wcol_=wcol_, bank=bank: e.scalar_tensor_tensor(
                        out=dst[:, h, :], in0=src[:, h, :], scalar=wcol_[:],
                        in1=bank[:, 0:TSUP], op0=ALU.mult, op1=ALU.mult),
                        waits=[("t", tb)], inc="v")
                    vstt = sch.now("v")
                    obank = [self.pb5, self.pb7][(r + 1) % 2]
                    sch.emit("tensor", lambda e, h=h, dst=dst, obank=obank:
                             e.matmul(obank[0:64, 0:TSUP], self.swapm[:],
                                      dst[0:64, h, :], start=True, stop=True),
                             waits=[("v", vstt)], inc="t")
                    tsw = sch.now("t")
                    qsh = rope_sc[0:64, 2, :]
                    sa = rope_sc[0:64, 0, :]
                    sb_ = rope_sc[0:64, 1, :]
                    sch.emit("vector", lambda e, qsh=qsh, obank=obank:
                             e.tensor_copy(out=qsh, in_=obank[0:64, 0:TSUP]),
                             waits=[("t", tsw)], inc="v")
                    sch.emit("vector", lambda e, qsh=qsh, sa=sa, sn_sl=sn_sl:
                             e.tensor_tensor(out=sa, in0=qsh, in1=sn_sl,
                                             op=ALU.mult), gap=1)
                    sch.emit("vector", lambda e, h=h, dst=dst, sb_=sb_,
                             cs_sl=cs_sl: e.tensor_tensor(
                        out=sb_, in0=dst[0:64, h, :], in1=cs_sl, op=ALU.mult))
                    sch.emit("vector", lambda e, h=h, dst=dst, sa=sa, sb_=sb_:
                             e.tensor_tensor(out=dst[0:64, h, :], in0=sb_,
                                             in1=sa, op=ALU.add), inc="v",
                             gap=1)
                vrope = sch.now("v")

                if self.mkdbg("q", [H, S]) and g == 0:
                    pass

                units = [(t4, h) for t4 in range(4) for h in range(HPG)]
                NU = len(units)
                st_mask_v = {}
                st_ts_v = {}
                st_ptev_v = {}
                st_ctx_v = {}
                st_qk_t = {}
                st_tr_t = {}
                st_pv_t = {}
                st_exp_s = {}
                ctx_dma_ds = {}

                def em_qk(u):
                    t4, h = units[u]
                    hg = g * HPG + h
                    bank = [self.pb4, self.pb5][u % 2]
                    tt = sup * 4 + t4
                    if t4 > 0:
                        lkp = lk_sup[par][:, h, (t4 - 1) * 128:t4 * 128]
                    else:
                        lkp = lk_sup[1 - par][:, h, 3 * 128:4 * 128]
                    lkc = lk_sup[par][:, h, t4 * 128:(t4 + 1) * 128]
                    qs = q_sup[:, h, t4 * 128:(t4 + 1) * 128]
                    w = [("v", st_mask_v[u - 2] if u - 2 >= 0 else vrope)]
                    if u == 0:
                        w.append(("v", vlv))

                    def fn(e, qs=qs, bank=bank, hg=hg, lkp=lkp, lkc=lkc):
                        e.matmul(bank[:, 0:129], qs, self.scb[:, hg, :],
                                 start=True, stop=True)
                        e.matmul(bank[:, 129:257], qs, lkp, start=True, stop=True)
                        return e.matmul(bank[:, 257:385], qs, lkc, start=True,
                                        stop=True)
                    sch.emit("tensor", fn, waits=w, inc="t")
                    st_qk_t[u] = sch.now("t")

                def em_mask(u):
                    t4, h = units[u]
                    tt = sup * 4 + t4
                    sl = u % 2
                    bank = [self.pb4, self.pb5][u % 2]
                    lm = self.lmask[:, 0:256] if tt == 0 else self.lmask[:, 256:512]
                    cm = self.cmask[:, tt * 129:(tt + 1) * 129]
                    wm = [("t", st_qk_t[u])]
                    if u - 2 in st_exp_s:
                        wm.append(("s", st_exp_s[u - 2]))
                    sch.emit("vector", lambda e, bank=bank, sl=sl, cm=cm:
                             e.scalar_tensor_tensor(
                        out=ms_c[:, sl, :], in0=bank[:, 0:129], scalar=SCALE,
                        in1=cm, op0=ALU.mult, op1=ALU.add),
                        waits=wm)
                    sch.emit("vector", lambda e, bank=bank, sl=sl, lm=lm:
                             e.scalar_tensor_tensor(
                        out=ms_l[:, sl, :], in0=bank[:, 129:385], scalar=SCALE,
                        in1=lm, op0=ALU.mult, op1=ALU.add))
                    sch.emit("vector", lambda e, sl=sl: e.tensor_reduce(
                        out=nm_c[:, sl:sl + 1], in_=ms_c[:, sl, :], axis=AX.X,
                        op=ALU.max, negate=True))
                    sch.emit("vector", lambda e, sl=sl: e.tensor_reduce(
                        out=nm_l[:, sl:sl + 1], in_=ms_l[:, sl, :], axis=AX.X,
                        op=ALU.max, negate=True), inc="v")
                    st_mask_v[u] = sch.now("v")

                def em_exp(u):
                    sl = u % 2

                    def fn(e, sl=sl):
                        e.activation(out=es_c[:, sl, :], in_=ms_c[:, sl, :],
                                     func=ACT.Exp, bias=nm_c[:, sl:sl + 1],
                                     scale=1.0, accum_out=dn_c[:, sl:sl + 1])
                        return e.activation(out=es_l[:, sl, :], in_=ms_l[:, sl, :],
                                            func=ACT.Exp, bias=nm_l[:, sl:sl + 1],
                                            scale=1.0,
                                            accum_out=dn_l[:, sl:sl + 1])
                    sch.emit("scalar", fn, waits=[("v", st_mask_v[u])], inc="s")
                    st_exp_s[u] = sch.now("s")

                def em_recips(u):
                    sl = u % 2
                    sch.emit("vector", lambda e, sl=sl: e.reciprocal(
                        out=rc_c[:, sl:sl + 1], in_=dn_c[:, sl:sl + 1]),
                        waits=[("s", st_exp_s[u])])
                    sch.emit("vector", lambda e, sl=sl: e.reciprocal(
                        out=rc_l[:, sl:sl + 1], in_=dn_l[:, sl:sl + 1]))

                def em_ts(u):
                    sl = u % 2
                    sch.emit("vector", lambda e, sl=sl: e.tensor_scalar(
                        out=p_bf[:, sl, 0:129], in0=es_c[:, sl, :],
                        scalar1=rc_c[:, sl:sl + 1], scalar2=0.5, op0=ALU.mult,
                        op1=ALU.mult))
                    sch.emit("vector", lambda e, sl=sl: e.tensor_scalar(
                        out=p_bf[:, sl, 129:385], in0=es_l[:, sl, :],
                        scalar1=rc_l[:, sl:sl + 1], scalar2=0.5, op0=ALU.mult,
                        op1=ALU.mult), inc="v")
                    st_ts_v[u] = sch.now("v")

                def em_tr(u):
                    sl = u % 2
                    w = [("v", st_ts_v[u])]
                    if u - 1 >= 0:
                        w.append(("v", st_ptev_v[u - 1]))

                    def fn(e, sl=sl):
                        e.transpose(self.pt0[:, 0:128], p_bf[:, sl, 1:129],
                                    self.ident[:])
                        e.transpose(self.pt0[:, 128:256], p_bf[:, sl, 129:257],
                                    self.ident[:])
                        e.transpose(self.pt0[:, 256:384], p_bf[:, sl, 257:385],
                                    self.ident[:])
                        return e.transpose(self.pt0[0:1, 384:512],
                                           p_bf[:, sl, 0:1], self.ident[:])
                    sch.emit("tensor", fn, waits=w, inc="t")
                    st_tr_t[u] = sch.now("t")

                def em_ptev(u):
                    sl = u % 2

                    def fn(e, sl=sl):
                        e.tensor_copy(out=pT_sb[:, sl, 0:384],
                                      in_=self.pt0[:, 0:384])
                        return e.tensor_copy(out=pT_sb[0:1, sl, 384:512],
                                             in_=self.pt0[0:1, 384:512])
                    sch.emit("vector", fn, waits=[("t", st_tr_t[u])], inc="v")
                    st_ptev_v[u] = sch.now("v")

                def em_pv(u):
                    t4, h = units[u]
                    hg = g * HPG + h
                    sl = u % 2
                    if t4 > 0:
                        lvp = lv_sup[par][:, t4 - 1, h * 128:(h + 1) * 128]
                    else:
                        lvp = lv_sup[1 - par][:, 3, h * 128:(h + 1) * 128]
                    lvc = lv_sup[par][:, t4, h * 128:(h + 1) * 128]
                    w = [("v", st_ptev_v[u])]
                    if u - 2 >= 0:
                        w.append(("v", st_ctx_v[u - 2]))

                    def fn(e, sl=sl, hg=hg, lvp=lvp, lvc=lvc):
                        e.matmul(self.ps_big[:, sl, 0:128], self.cvb[:],
                                 pT_sb[:, sl, 0:128], start=True, stop=False)
                        e.matmul(self.ps_big[:, sl, 0:128],
                                 self.rows_b[0:1, 3 * H + hg * 128:3 * H + (hg + 1) * 128],
                                 pT_sb[0:1, sl, 384:512], start=False, stop=False)
                        e.matmul(self.ps_big[:, sl, 0:128], lvp,
                                 pT_sb[:, sl, 128:256], start=False, stop=False)
                        return e.matmul(self.ps_big[:, sl, 0:128], lvc,
                                        pT_sb[:, sl, 256:384], start=False,
                                        stop=True)
                    sch.emit("tensor", fn, waits=w, inc="t")
                    st_pv_t[u] = sch.now("t")

                def em_ctx(u):
                    t4, h = units[u]
                    sl = u % 2
                    w = [("t", st_pv_t[u])]
                    if u - 2 * HPG >= 0:
                        w.append(("dS", ctx_dma_ds[u - 2 * HPG]))

                    sch.emit("vector", lambda e, t4=t4, h=h, sl=sl: e.tensor_copy(
                        out=ctx_st[:, t4 % 2, h, :], in_=self.ps_big[:, sl, 0:128]),
                        waits=w, inc="v")
                    st_ctx_v[u] = sch.now("v")
                    if h == HPG - 1:
                        tt = sup * 4 + t4
                        out = self.d_ctx.ap()[g * 512:(g + 1) * 512,
                                              tt * 128:(tt + 1) * 128].rearrange(
                                                  "(h p) t -> p h t", p=128)
                        sch.dma("sync", lambda e, out=out, t4=t4: e.dma_start(
                            out=out, in_=ctx_st[:, t4 % 2, :, :]),
                            waits=[("v", st_ctx_v[u])])
                        for hh in range(HPG):
                            ctx_dma_ds[u - hh] = sch.now("dS")
                        self.ctx_ds.append(sch.now("dS"))

                for u in range(NU + 1):
                    if u < NU:
                        em_qk(u)
                        em_mask(u)
                        em_exp(u)
                        em_recips(u)
                    if 0 <= u - 1:
                        em_ts(u - 1)
                        em_tr(u - 1)
                        em_ptev(u - 1)
                        em_pv(u - 1)
                        em_ctx(u - 1)

    def phase_a4(self, sc):
        sch, W, nc = self.sch, self.W, self.nc
        SB = lambda n, sh, dt: sc.enter_context(nc.sbuf_tensor(n, sh, dt))
        wo_sb = SB("wo_sb", [128, NHC, H], BF16)
        ctx_t = SB("ctx_t", [128, 2, NHC, 128], BF16)
        r1_sb = SB("r1_sb", [128, H], F32)
        sq_t = SB("sq_t", [128, H], F32)
        mq = SB("mq", [128, 64], F32)
        s2q = SB("s2q", [128, 64], F32)
        rrq = SB("rrq", [128, 64], F32)
        xq8 = SB("xq8", [128, H], FP8)
        xq_bf = SB("xq_bf", [128, H], BF16)

        sch.dma("sync", lambda e: e.dma_start(
            out=wo_sb[:], in_=self.d_wob.ap().rearrange("(c p) m -> p c m", p=128)),
            waits=[("dG", self.WO_DG), ("t", sch.now("t")), ("v", sch.now("v"))])
        WO = sch.now("dS")
        ctx_all = self.ctx_ds[-1] if self.ctx_ds else 0

        dbg_x1 = self.mkdbg("x1", [S, H])

        for tt in range(NT):
            xt = self.xt[tt % 2]
            par = tt % 2
            inctx = self.d_ctx.ap()[:, tt * 128:(tt + 1) * 128].rearrange(
                "(c p) t -> p c t", p=128)
            wv = [("t", sch.now("t"))] if tt >= 2 else [("dS", ctx_all)]
            sch.dma("sync", lambda e, inctx=inctx, par=par: e.dma_start(
                out=ctx_t[:, par, :, :], in_=inctx), waits=wv)
            sch.dma("sync", lambda e, xt=xt, tt=tt: e.dma_start(
                out=xt[:], in_=self.i_x.ap()[tt * 128:(tt + 1) * 128, :]),
                waits=[("v", sch.now("v"))] if tt >= 2 else ())
            dsn = sch.now("dS")

            def fn(e, par=par):
                for hc in range(NHC):
                    for j in range(4):
                        e.matmul(self.ps_big[:, j, :], ctx_t[:, par, hc, :],
                                 wo_sb[:, hc, j * 512:(j + 1) * 512],
                                 start=(hc == 0), stop=False)
                last = None
                for j in range(4):
                    last = e.matmul(self.ps_big[:, j, :],
                                    self.ones_row[0:1, 0:128],
                                    self.rows_b[0:1, H + j * 512:H + (j + 1) * 512],
                                    start=False, stop=True)
                return last
            sch.emit("tensor", fn,
                     waits=[("dS", max(dsn, WO)), ("v", sch.now("v"))],
                     inc="t")
            tn = sch.now("t")

            sch.emit("vector", lambda e, xt=xt: e.tensor_tensor(
                out=r1_sb[:], in0=self.ps_big[:, :, :], in1=xt[:],
                op=ALU.add), waits=[("t", tn)], inc="v")
            sch.emit("vector", lambda e: e.tensor_reduce(
                out=self.cols[:, 0:1], in_=r1_sb[:], axis=AX.X, op=ALU.add),
                inc="v", gap=1)
            sch.emit("scalar", lambda e: e.activation(
                out=sq_t[:], in_=r1_sb[:], func=ACT.Square,
                accum_out=self.cols[:, 1:2]), waits=W("v"), inc="s")
            sn = sch.now("s")
            sch.emit("vector", lambda e: e.tensor_scalar(
                out=self.cols[:, 2:3], in0=self.cols[:, 0:1], scalar1=CH,
                scalar2=None, op0=ALU.mult), waits=[("s", sn)], gap=1)
            sch.emit("vector", lambda e: e.tensor_scalar(
                out=self.cols[:, 4:5], in0=self.cols[:, 1:2], scalar1=CH,
                scalar2=None, op0=ALU.mult))
            sch.emit("vector", lambda e: e.tensor_tensor(
                out=self.cols[:, 3:4], in0=self.cols[:, 2:3],
                in1=self.cols[:, 2:3], op=ALU.mult))
            sch.emit("vector", lambda e: e.tensor_tensor(
                out=self.cols[:, 5:6], in0=self.cols[:, 4:5],
                in1=self.cols[:, 3:4], op=ALU.subtract), gap=1)
            sch.emit("vector", lambda e: e.tensor_scalar(
                out=self.cols[:, 6:7], in0=self.cols[:, 5:6], scalar1=EPS_LN,
                scalar2=None, op0=ALU.add), inc="v", gap=1)
            sch.emit("scalar", lambda e: e.activation(
                out=self.cols[:, 6:7], in_=self.cols[:, 6:7], func=ACT.Sqrt),
                waits=W("v"), inc="s")
            sn = sch.now("s")
            sch.emit("vector", lambda e: e.reciprocal(
                out=self.cols[:, 7:8], in_=self.cols[:, 6:7]), waits=[("s", sn)])
            self.gap(3)
            sch.emit("vector", lambda e: e.tensor_scalar(
                out=self.x1_t[:], in0=r1_sb[:], scalar1=self.cols[:, 2:3],
                scalar2=self.cols[:, 7:8], op0=ALU.subtract, op1=ALU.mult),
                inc="v")
            x13 = self.x1_t[:].rearrange("p (b k) -> p b k", k=32)
            sch.emit("vector", lambda e, x13=x13: e.tensor_reduce(
                out=mq[:], in_=x13, axis=AX.X, op=ALU.max,
                apply_absolute_value=True), gap=1)
            sch.emit("vector", lambda e: e.tensor_scalar(
                out=s2q[:], in0=mq[:], scalar1=C224, scalar2=2e-30,
                op0=ALU.mult, op1=ALU.max), gap=1)
            sch.emit("vector", lambda e: e.reciprocal(out=rrq[:],
                                                      in_=s2q[:]), gap=1)
            self.gap(3)
            rrb = rrq[:].unsqueeze(2).broadcast_to([128, 64, 32])
            s2b = s2q[:].unsqueeze(2).broadcast_to([128, 64, 32])
            sch.emit("vector", lambda e, x13=x13, rrb=rrb: e.tensor_tensor(
                out=xq8[:], in0=x13, in1=rrb, op=ALU.mult))
            sch.emit("vector", lambda e, s2b=s2b: e.tensor_tensor(
                out=xq_bf[:], in0=xq8[:], in1=s2b, op=ALU.mult),
                inc="v", gap=1)
            vq = sch.now("v")
            sch.dma("sync", lambda e, tt=tt: e.dma_start(
                out=self.d_x1.ap()[tt * 128:(tt + 1) * 128, :], in_=self.x1_t[:]),
                waits=[("v", vq)])
            sch.dma("sync", lambda e, tt=tt: e.dma_start(
                out=self.d_xq.ap()[tt * 128:(tt + 1) * 128, :], in_=xq_bf[:]),
                waits=[("v", vq)])
            if dbg_x1:
                sch.dma("sync", lambda e, tt=tt: e.dma_start(
                    out=self.dbg["x1"].ap()[tt * 128:(tt + 1) * 128, :],
                    in_=self.x1_t[:]), waits=[("v", vq)])
            self.a4_store_ds.append(sch.now("dS"))
            if "ffn" not in self.phases:
                sch.dma("sync", lambda e, tt=tt: e.dma_start(
                    out=self.o_y.ap()[tt * 128:(tt + 1) * 128, :],
                    in_=self.x1_t[:]), waits=[("v", vq)])

    def phase_f(self, sc):
        sch, W, nc = self.sch, self.W, self.nc
        SB = lambda n, sh, dt: sc.enter_context(nc.sbuf_tensor(n, sh, dt))
        xfm_q = self.xfm
        wbig = SB("wbig", [128, 64, 512], BF16)
        gwT = wbig[:, 0:NHC, :]
        uwT = wbig[:, NHC:2 * NHC, :]
        dwT = wbig
        hqT = SB("hqT", [128, 64, TSUP], BF16)
        sil = SB("sil", [128, 512], F32)
        h_sb = SB("h_sb", [128, 512], F32)
        mh = SB("mh", [128, 16], F32)
        s2h = SB("s2h", [128, 16], F32)
        rrh = SB("rrh", [128, 16], F32)
        h8 = SB("h8", [128, 512], FP8)
        hq = SB("hq", [128, 2, 512], BF16)
        r2c = SB("r2c", [128, 2, 512], F32)
        sqf = SB("sqf", [128, 512], F32)

        for sup in range(NSUP):
            a4ds = self.a4_store_ds[sup * 4 + 3]
            tprev = sch.now("t")
            for hc in range(NHC):
                out = xfm_q[:, hc, :]
                in_ = self.d_xq.ap()[sup * TSUP:(sup + 1) * TSUP,
                                     hc * 128:(hc + 1) * 128]
                w = [("dS", a4ds)]
                if sup > 0 and hc == 0:
                    w.append(("t", tprev))
                sch.dma("sync", lambda e, o=out, i=in_: e.dma_start_transpose(
                    out=o, in_=i), waits=w)
            XQ = sch.now("dS")

            for ic in range(16):
                tload = sch.now("t")
                for hc in range(NHC):
                    w = [("dG", self.F0_DG)]
                    if hc == 0:
                        w.append(("t", tload))
                        if sup == 0 and ic == 0:
                            w.append(("v", sch.now("v")))
                            w.append(("s", sch.now("s")))
                    sch.dma("sync", lambda e, hc=hc, ic=ic: e.dma_start_transpose(
                        out=wbig[:, hc, :],
                        in_=self.d_gw.ap()[ic * 512:(ic + 1) * 512,
                                           hc * 128:(hc + 1) * 128]), waits=w)
                    sch.dma("sync", lambda e, hc=hc, ic=ic: e.dma_start_transpose(
                        out=wbig[:, NHC + hc, :],
                        in_=self.d_uw.ap()[ic * 512:(ic + 1) * 512,
                                           hc * 128:(hc + 1) * 128]))
                GW = sch.now("dS")
                for t4 in range(4):
                    pg = self.ps_big[:, (t4 % 2) * 2, :]
                    pu = self.ps_big[:, (t4 % 2) * 2 + 1, :]
                    w = [("dS", max(GW, XQ)), ("v", sch.now("v")),
                         ("s", sch.now("s"))]

                    def fn(e, t4=t4, pg=pg, pu=pu):
                        last = None
                        for hc in range(NHC):
                            lh = xfm_q[:, hc, t4 * 128:(t4 + 1) * 128]
                            e.matmul(pg, lh, wbig[:, hc, :], start=(hc == 0),
                                     stop=(hc == NHC - 1))
                            last = e.matmul(pu, lh, wbig[:, NHC + hc, :],
                                            start=(hc == 0), stop=(hc == NHC - 1))
                        return last
                    sch.emit("tensor", fn, waits=w, inc="t")
                    tgu = sch.now("t")
                    sch.emit("scalar", lambda e, pg=pg: e.activation(
                        out=sil[:], in_=pg, func=ACT.Silu),
                        waits=[("t", tgu), ("v", sch.now("v"))], inc="s")
                    sn = sch.now("s")
                    sch.emit("vector", lambda e, pu=pu: e.tensor_tensor(
                        out=h_sb[:], in0=sil[:], in1=pu, op=ALU.mult),
                        waits=[("s", sn)], inc="v")
                    h3 = h_sb[:].rearrange("p (b k) -> p b k", k=32)
                    sch.emit("vector", lambda e, h3=h3: e.tensor_reduce(
                        out=mh[:], in_=h3, axis=AX.X, op=ALU.max,
                        apply_absolute_value=True), gap=1)
                    sch.emit("vector", lambda e: e.tensor_scalar(
                        out=s2h[:], in0=mh[:], scalar1=C224, scalar2=2e-30,
                        op0=ALU.mult, op1=ALU.max), gap=1)
                    sch.emit("vector", lambda e: e.reciprocal(out=rrh[:],
                                                              in_=s2h[:]), gap=1)
                    self.gap(3)
                    rrb = rrh[:].unsqueeze(2).broadcast_to([128, 16, 32])
                    s2b = s2h[:].unsqueeze(2).broadcast_to([128, 16, 32])
                    sch.emit("vector", lambda e, h3=h3, rrb=rrb: e.tensor_tensor(
                        out=h8[:], in0=h3, in1=rrb, op=ALU.mult))
                    sch.emit("vector", lambda e, s2b=s2b, t4=t4: e.tensor_tensor(
                        out=hq[:, t4 % 2, :], in0=h8[:], in1=s2b, op=ALU.mult),
                        inc="v", gap=1)
                    vh = sch.now("v")
                    w = [("v", vh), ("s", sch.now("s"))]

                    def ftr(e, t4=t4):
                        last = None
                        for isub in range(4):
                            last = e.transpose(
                                self.pt0[:, isub * 128:(isub + 1) * 128],
                                hq[:, t4 % 2, isub * 128:(isub + 1) * 128],
                                self.ident[:])
                        return last
                    sch.emit("tensor", ftr, waits=w, inc="t")
                    ttr = sch.now("t")

                    def fev(e, ic=ic, t4=t4):
                        last = None
                        for isub in range(4):
                            last = e.copy(
                                out=hqT[:, ic * 4 + isub, t4 * 128:(t4 + 1) * 128],
                                in_=self.pt0[:, isub * 128:(isub + 1) * 128])
                        return last
                    sch.emit("scalar", fev, waits=[("t", ttr)], inc="s")

            SEVAC = sch.now("s")
            TGU_ALL = sch.now("t")
            for hcd in range(4):
                tload = sch.now("t")
                for icd in range(64):
                    w = [("dG", self.F0_DG)]
                    if icd == 0:
                        w.append(("t", tload if hcd >= 1 else TGU_ALL))
                    sch.dma("sync", lambda e, hcd=hcd, icd=icd:
                            e.dma_start_transpose(
                        out=wbig[:, icd, :],
                        in_=self.d_dw.ap()[hcd * 512:(hcd + 1) * 512,
                                           icd * 128:(icd + 1) * 128]), waits=w)
                DW = sch.now("dS")
                for t4 in range(4):
                    tt = sup * 4 + t4
                    xt = self.xt[(hcd * 4 + t4) % 2]
                    sch.dma("sync", lambda e, tt=tt, xt=xt, hcd=hcd: e.dma_start(
                        out=xt[:, hcd * 512:(hcd + 1) * 512],
                        in_=self.d_x1.ap()[tt * 128:(tt + 1) * 128,
                                           hcd * 512:(hcd + 1) * 512]),
                        waits=[("v", sch.now("v"))])
                    X1 = sch.now("dS")
                    pbank = self.ps_big[:, t4, :]
                    w = [("dS", DW), ("s", SEVAC if (hcd == 0 and t4 == 0) else sch.now("s")),
                         ("v", sch.now("v"))]

                    def fn(e, t4=t4, pbank=pbank):
                        last = None
                        for icd in range(64):
                            last = e.matmul(pbank,
                                            hqT[:, icd, t4 * 128:(t4 + 1) * 128],
                                            wbig[:, icd, :], start=(icd == 0),
                                            stop=(icd == 63))
                        return last
                    sch.emit("tensor", fn, waits=w, inc="t")
                    tdn = sch.now("t")
                    sl = t4 % 2
                    sch.emit("vector", lambda e, pbank=pbank, sl=sl, xt=xt,
                             hcd=hcd: e.tensor_tensor(
                        out=r2c[:, sl, :], in0=pbank,
                        in1=xt[:, hcd * 512:(hcd + 1) * 512], op=ALU.add),
                        waits=[("t", tdn), ("dS", X1)], inc="v")
                    sch.emit("vector", lambda e, sl=sl: e.tensor_reduce(
                        out=self.cols[:, 8:9], in_=r2c[:, sl, :], axis=AX.X,
                        op=ALU.add), gap=1)
                    sch.emit("vector", lambda e, tt=tt: e.tensor_tensor(
                        out=self.ln_s[:, tt:tt + 1], in0=self.ln_s[:, tt:tt + 1],
                        in1=self.cols[:, 8:9], op=ALU.add), gap=1)
                    sch.emit("scalar", lambda e, sl=sl: e.activation(
                        out=sqf[:], in_=r2c[:, sl, :],
                        func=ACT.Square, accum_out=self.cols[:, 9:10]),
                        waits=W("v"), inc="s")
                    sn2 = sch.now("s")
                    sch.emit("vector", lambda e, tt=tt: e.tensor_tensor(
                        out=self.ln_q[:, tt:tt + 1], in0=self.ln_q[:, tt:tt + 1],
                        in1=self.cols[:, 9:10], op=ALU.add),
                        waits=[("s", sn2)], inc="v")
                    vr2 = sch.now("v")
                    sch.dma("sync", lambda e, tt=tt, sl=sl, hcd=hcd: e.dma_start(
                        out=self.d_r2.ap()[tt * 128:(tt + 1) * 128,
                                           hcd * 512:(hcd + 1) * 512],
                        in_=r2c[:, sl, :]), waits=[("v", vr2)])
            R2D = sch.now("dS")

            for t4 in range(4):
                tt = sup * 4 + t4
                xt = self.xt[t4 % 2]
                sch.dma("sync", lambda e, tt=tt, xt=xt: e.dma_start(
                    out=xt[:], in_=self.d_r2.ap()[tt * 128:(tt + 1) * 128, :]),
                    waits=[("dS", R2D), ("v", sch.now("v"))])
                dsn = sch.now("dS")
                sch.emit("vector", lambda e, tt=tt: e.tensor_scalar(
                    out=self.cols[:, 2:3], in0=self.ln_s[:, tt:tt + 1],
                    scalar1=CH, scalar2=None, op0=ALU.mult),
                    waits=[("dS", dsn)])
                sch.emit("vector", lambda e, tt=tt: e.tensor_scalar(
                    out=self.cols[:, 4:5], in0=self.ln_q[:, tt:tt + 1],
                    scalar1=CH, scalar2=None, op0=ALU.mult))
                sch.emit("vector", lambda e: e.tensor_tensor(
                    out=self.cols[:, 3:4], in0=self.cols[:, 2:3],
                    in1=self.cols[:, 2:3], op=ALU.mult))
                sch.emit("vector", lambda e: e.tensor_tensor(
                    out=self.cols[:, 5:6], in0=self.cols[:, 4:5],
                    in1=self.cols[:, 3:4], op=ALU.subtract), gap=1)
                sch.emit("vector", lambda e: e.tensor_scalar(
                    out=self.cols[:, 6:7], in0=self.cols[:, 5:6],
                    scalar1=EPS_LN, scalar2=None, op0=ALU.add), inc="v", gap=1)
                sch.emit("scalar", lambda e: e.activation(
                    out=self.cols[:, 6:7], in_=self.cols[:, 6:7],
                    func=ACT.Sqrt), waits=W("v"), inc="s")
                sn = sch.now("s")
                sch.emit("vector", lambda e: e.reciprocal(
                    out=self.cols[:, 7:8], in_=self.cols[:, 6:7]),
                    waits=[("s", sn)])
                self.gap(3)
                sch.emit("vector", lambda e, xt=xt: e.tensor_scalar(
                    out=self.x1_t[:], in0=xt[:], scalar1=self.cols[:, 2:3],
                    scalar2=self.cols[:, 7:8], op0=ALU.subtract, op1=ALU.mult),
                    inc="v")
                vy = sch.now("v")
                sch.dma("sync", lambda e, tt=tt: e.dma_start(
                    out=self.o_y.ap()[tt * 128:(tt + 1) * 128, :],
                    in_=self.x1_t[:]), waits=[("v", vy)])


_NC_CACHE = {}


def _get_nc(debug=(), phases=("attn", "ffn"), paranoid=False):
    key = (tuple(sorted(debug)), tuple(sorted(phases)), paranoid)
    if key not in _NC_CACHE:
        kb = KB(debug=debug, phases=phases, paranoid=paranoid)
        _NC_CACHE[key] = (kb.build(), dict(kb.dbg), kb.cs)
    return _NC_CACHE[key]


def kernel(**inputs):
    nc, dbg, cs = _get_nc(paranoid=False)
    x_full = np.ascontiguousarray(inputs["L0_attn_hidden_states"])
    base = {
        "i_wq": np.ascontiguousarray(inputs["L0_attn_Wq"]),
        "i_wlk": np.ascontiguousarray(inputs["L0_attn_Wlk"]),
        "i_wlv": np.ascontiguousarray(inputs["L0_attn_Wlv"]),
        "i_wo": np.ascontiguousarray(inputs["L0_attn_Wo"]),
        "i_wk": np.ascontiguousarray(inputs["L0_attn_Wk"]),
        "i_wv": np.ascontiguousarray(inputs["L0_attn_Wv"]),
        "i_wc": np.ascontiguousarray(
            inputs["L0_attn_Wc"].reshape(NHC, 128).T),
        "i_bq": np.ascontiguousarray(
            inputs["L0_attn_bq"].reshape(NHC, 128).T),
        "i_blk": np.ascontiguousarray(
            inputs["L0_attn_blk"].reshape(NHC, 128).T),
        "i_blv": np.ascontiguousarray(inputs["L0_attn_blv"]),
        "i_bo": np.ascontiguousarray(inputs["L0_attn_bo"]),
        "i_bk": np.ascontiguousarray(inputs["L0_attn_bk"]),
        "i_bv": np.ascontiguousarray(inputs["L0_attn_bv"]),
        "i_bc": np.ascontiguousarray(inputs["L0_attn_bc"]),
        "i_qnw": np.ascontiguousarray(inputs["L0_attn_qn_w"]),
        "i_knw": np.ascontiguousarray(inputs["L0_attn_kn_w"]),
        "i_sk": np.ascontiguousarray(inputs["L0_attn_sink_k"].T),
        "i_sv": np.ascontiguousarray(inputs["L0_attn_sink_v"]),
        "i_gw": np.ascontiguousarray(inputs["L0_ffn_gate_w"]),
        "i_uw": np.ascontiguousarray(inputs["L0_ffn_up_w"]),
        "i_dw": np.ascontiguousarray(inputs["L0_ffn_down_w"]),
    }
    base.update(cs)
    in_maps = []
    for c in range(B):
        m = dict(base)
        m["i_x"] = np.ascontiguousarray(x_full[c])
        in_maps.append(m)
    res = run_bass_kernel_spmd(nc, in_maps, core_ids=list(range(B)))
    return np.stack([r["o_y"] for r in res.results], axis=0)


# version 08
# speedup vs baseline: 1.2073x; 1.1993x over previous
import math
from contextlib import ExitStack

import numpy as np
import ml_dtypes

import concourse.bass as bass
import concourse.mybir as mybir
from concourse.bass_utils import run_bass_kernel_spmd

F32 = mybir.dt.float32
BF16 = mybir.dt.bfloat16
FP8 = mybir.dt.float8e4
ALU = mybir.AluOpType
ACT = mybir.ActivationFunctionType
AX = mybir.AxisListType

B, S, H = 8, 2048, 2048
NH, HD, R, C, WIN = 16, 128, 16, 128, 128
ROPE, HALF = 64, 32
EPS_RMS, EPS_LN = 1e-6, 1e-5
II = 8192
SCALE = 1.0 / math.sqrt(HD)
NEG = -1.0e9
NT = 16
NSUP, TSUP = 4, 512
NG, HPG = 4, 4
NHC = 16
C224 = float(np.float32(1.0) / np.float32(224.0))
C128 = 1.0 / 128.0
CH = 1.0 / float(H)


def host_consts():
    cs = {}
    cs["c_ident"] = np.eye(128, dtype=np.float32).astype(ml_dtypes.bfloat16)
    cs["c_identf"] = np.eye(128, dtype=np.float32)
    swap = np.zeros((64, 64), np.float32)
    swap[:32, 32:] = np.eye(32)
    swap[32:, :32] = np.eye(32)
    cs["c_swap"] = swap.astype(ml_dtypes.bfloat16)
    inv_freq = 1.0 / 10000.0 ** (np.arange(HALF, dtype=np.float64) * 2.0 / ROPE)
    pos = np.arange(S, dtype=np.float64)
    ang = pos[None, :] * inv_freq[:, None]
    cs["c_cosd"] = np.concatenate([np.cos(ang)] * 2, 0).astype(np.float32).astype(ml_dtypes.bfloat16)
    cs["c_sins"] = np.concatenate([-np.sin(ang), np.sin(ang)], 0).astype(np.float32).astype(ml_dtypes.bfloat16)
    pc = np.arange(C, dtype=np.float64) * R + (R - 1)
    angc = pc[None, :] * inv_freq[:, None]
    cs["c_cosdc"] = np.concatenate([np.cos(angc)] * 2, 0).astype(np.float32).astype(ml_dtypes.bfloat16)
    cs["c_sinsc"] = np.concatenate([-np.sin(angc), np.sin(angc)], 0).astype(np.float32).astype(ml_dtypes.bfloat16)
    qi = np.arange(128)[:, None]
    j = np.arange(256)[None, :]
    gen = np.where((j >= qi) & (j <= qi + WIN), 0.0, NEG)
    t0 = np.where((j >= 128) & (j - 128 <= qi), 0.0, NEG)
    cs["c_lmask"] = np.concatenate([t0, gen], 1).astype(np.float32)
    cm = np.zeros((128, NT, 129), np.float32)
    cidx = np.arange(C)[None, :]
    for tt in range(NT):
        cm[:, tt, 1:] = np.where(cidx * R + (R - 1) <= tt * 128 + qi, 0.0, NEG)
    cs["c_cmask"] = cm.reshape(128, NT * 129)
    g = np.zeros((NT, 128, 128), np.float32)
    for tt in range(NT):
        for lt in range(128):
            g[tt, lt, 8 * tt + lt // 16] = 1.0
    cs["c_g"] = g.transpose(1, 0, 2).reshape(128, NT * 128).astype(ml_dtypes.bfloat16)
    e = np.zeros((128, 8, 8), np.float32)
    for rr in range(8):
        e[:, rr, rr] = 1.0
    cs["c_e"] = e.reshape(128, 64).astype(ml_dtypes.bfloat16)
    f = np.zeros((8, 8, 128), np.float32)
    for rr in range(8):
        f[rr, rr, :] = 1.0
    cs["c_f"] = f.reshape(8, 8 * 128).astype(ml_dtypes.bfloat16)
    return cs


class Sched:
    ENGINES = ("tensor", "vector", "scalar", "gpsimd", "sync")

    def __init__(self):
        self.steps = {e: [] for e in self.ENGINES}
        self.counts = {}
        self.junk_fn = None

    def emit(self, engine, fn, waits=(), inc=None, inc_n=1, gap=0):
        for _ in range(gap):
            self.steps[engine].append((self.junk_fn, [], []))
        self._emit(engine, fn, waits, inc, inc_n)

    def _emit(self, engine, fn, waits=(), inc=None, inc_n=1):
        incs = []
        if inc is not None:
            self.counts[inc] = self.counts.get(inc, 0) + inc_n
            incs = [(inc, inc_n)]
        self.steps[engine].append(
            (fn, [(s, v) for (s, v) in waits if v > 0], incs))

    def dma(self, engine, fn, waits=(), inc="dS"):
        self._emit(engine, fn, waits, inc, inc_n=16)

    def now(self, sem):
        return self.counts.get(sem, 0)

    def build(self, block, semmap):
        for ename in self.ENGINES:
            steps = self.steps[ename]
            if not steps:
                continue

            def make_body(steps=list(steps)):
                def body(eng):
                    for fn, waits, incs in steps:
                        for s, v in waits:
                            eng.wait_ge(semmap[s], v)
                        inst = fn(eng)
                        for s, n in incs:
                            assert inst is not None
                            inst.then_inc(semmap[s], n)
                return body
            getattr(block, ename)(make_body())


class KB:
    def __init__(self, debug=(), phases=("attn", "ffn"), paranoid=False):
        self.debug = set(debug)
        self.phases = set(phases)
        self.paranoid = paranoid
        nc = self.nc = bass.Bass("TRN2", target_bir_lowering=False)
        self.sch = Sched()
        self.sch.junk_fn = lambda e: e.tensor_copy(out=self.junk[:, 0:1],
                                                   in_=self.junk[:, 1:2])
        if paranoid:
            _orig_emit = self.sch.emit
            def _pemit(engine, fn, waits=(), inc=None, inc_n=1):
                if engine == "vector" and inc_n == 1:
                    _orig_emit("vector", lambda e: e.tensor_copy(
                        out=self.junk[:, 0:1], in_=self.junk[:, 1:2]))
                _orig_emit(engine, fn, waits, inc, inc_n)
            self.sch.emit = _pemit
        self.stk = ExitStack()
        self.dbg = {}

        D = lambda n, sh: nc.dram_tensor(n, sh, F32, kind="ExternalInput")
        self.i_x = D("i_x", [S, H])
        self.i_wq, self.i_wlk = D("i_wq", [H, H]), D("i_wlk", [H, H])
        self.i_wlv, self.i_wo = D("i_wlv", [H, H]), D("i_wo", [H, H])
        self.i_wk, self.i_wv = D("i_wk", [H, HD]), D("i_wv", [H, HD])
        self.i_wc = D("i_wc", [128, NHC])
        self.i_bq, self.i_blk = D("i_bq", [128, NHC]), D("i_blk", [128, NHC])
        self.i_blv, self.i_bo = D("i_blv", [H]), D("i_bo", [H])
        self.i_bk, self.i_bv, self.i_bc = D("i_bk", [HD]), D("i_bv", [HD]), D("i_bc", [1])
        self.i_qnw, self.i_knw = D("i_qnw", [HD]), D("i_knw", [HD])
        self.i_sk, self.i_sv = D("i_sk", [128, NH]), D("i_sv", [NH, HD])
        self.i_gw, self.i_uw = D("i_gw", [II, H]), D("i_uw", [II, H])
        self.i_dw = D("i_dw", [H, II])
        self.cs = host_consts()
        self.ch = {}
        for k, v in self.cs.items():
            dt = BF16 if v.dtype == ml_dtypes.bfloat16 else F32
            self.ch[k] = nc.dram_tensor(k, list(v.shape), dt, kind="ExternalInput")

        self.o_y = nc.dram_tensor("o_y", [S, H], F32, kind="ExternalOutput")
        DI = lambda n, sh, dt: nc.dram_tensor(n, sh, dt, kind="Internal")
        self.d_xbf = DI("d_xbf", [S, H], BF16)
        self.d_wrow = DI("d_wrow", [S], F32)
        self.d_ctx = DI("d_ctx", [H, S], BF16)
        self.d_x1 = DI("d_x1", [S, H], F32)
        self.d_xq = DI("d_xq", [S, H], BF16)
        self.d_r2 = DI("d_r2", [S, H], F32)
        self.d_wqb = DI("d_wqb", [H, H], BF16)
        self.d_wlkb = DI("d_wlkb", [H, H], BF16)
        self.d_wlvb = DI("d_wlvb", [H, H], BF16)
        self.d_wob = DI("d_wob", [H, H], BF16)
        self.d_wkb = DI("d_wkb", [H, HD], BF16)
        self.d_wvb = DI("d_wvb", [H, HD], BF16)
        self.d_rows = DI("d_rows", [4 * H], BF16)
        self.d_gw = DI("d_gw", [H, II], BF16)
        self.d_uw = DI("d_uw", [H, II], BF16)
        self.d_dw = DI("d_dw", [II, H], BF16)

        self.semmap = {n: self.stk.enter_context(nc.semaphore("sem_" + n))
                       for n in ("t", "v", "s", "g", "dS", "dG", "vf", "dC")}

        PS = lambda n, sh, dt: self.stk.enter_context(nc.psum_tensor(n, sh, dt))
        self.ps_big = PS("ps_big", [128, 4, 512], F32)
        self.pb4 = PS("pb4", [128, 512], F32)
        self.pb5 = PS("pb5", [128, 512], F32)
        self.pt0 = PS("pt0", [128, 1024], BF16)
        self.pb7 = PS("pb7", [128, 512], F32)

        SBs = self.SBs = lambda n, sh, dt: self.stk.enter_context(
            nc.sbuf_tensor(n, sh, dt))
        self.ident = SBs("ident", [128, 128], BF16)
        self.ones_row = SBs("ones_row", [1, 512], BF16)
        self.rows_b = SBs("rows_b", [1, 4 * H], BF16)
        self.x1_t = SBs("x1_t", [128, H], F32)
        self.cols = SBs("cols", [128, 16], F32)
        self.ln_s = SBs("ln_s", [128, NT], F32)
        self.ln_q = SBs("ln_q", [128, NT], F32)
        self.junk = SBs("junk", [128, 4], F32)
        self.xfm = SBs("xfm", [128, NHC, TSUP], BF16)
        self.wcol = SBs("wcol", [128, NT], F32)
        self.sc_attn = ExitStack()
        SBa = self.SBa = lambda n, sh, dt: self.sc_attn.enter_context(
            nc.sbuf_tensor(n, sh, dt))
        self.swapm = SBa("swapm", [64, 64], BF16)
        self.cosd = SBa("cosd", [64, S], BF16)
        self.sins = SBa("sins", [64, S], BF16)
        self.cosdc = SBa("cosdc", [64, C], BF16)
        self.sinsc = SBa("sinsc", [64, C], BF16)
        self.lmask = SBa("lmask", [128, 512], F32)
        self.cmask = SBa("cmask", [128, NT * 129], F32)
        self.emat = SBa("emat", [128, 64], BF16)
        self.fmat = SBa("fmat", [8, 8 * 128], BF16)
        self.wc_bf = SBa("wc_bf", [128, NHC], BF16)
        self.wc_f = SBa("wc_f", [128, NHC], F32)
        self.bq_c = SBa("bq_c", [128, NHC], F32)
        self.blk_c = SBa("blk_c", [128, NHC], F32)
        self.bk_c = SBa("bk_c", [128, 1], F32)
        self.qnw_c = SBa("qnw_c", [128, 1], F32)
        self.knw_c = SBa("knw_c", [128, 1], F32)
        self.bc_c = SBa("bc_c", [1, 1], F32)
        self.bc_bf = SBa("bc_bf", [1, 1], BF16)
        self.bc128 = SBa("bc128", [128, 1], F32)
        self.sink_f = SBa("sink_f", [128, NH], F32)
        self.sinkb = SBa("sinkb", [128, NH], BF16)
        self.scb = SBa("scb", [128, NH, 129], BF16)
        self.cvb = SBa("cvb", [128, HD], BF16)
        self.lcol = SBa("lcol", [128, NT], F32)
        self.ident_f = SBa("ident_f", [128, 128], F32)
        self.lT = SBa("lT", [16, 128], F32)
        self.lm8 = SBa("lm8", [16, 8], F32)
        self.ls8 = SBa("ls8", [16, 8], F32)
        if "wrow" in self.debug:
            self.wrow_st = SBa("wrow_st", [1, S], F32)

        self.W = lambda *ss: [(s, self.sch.now(s)) for s in ss]
        self.a4_store_ds = []

    def mkdbg(self, name, shape):
        if name in self.debug:
            self.dbg[name] = self.nc.dram_tensor("dbg_" + name, shape, F32,
                                                 kind="ExternalOutput")
            return True
        return False

    def gap(self, n=3):
        for _ in range(n):
            self.sch.emit("vector", lambda e: e.tensor_copy(
                out=self.junk[:, 0:1], in_=self.junk[:, 1:2]))

    def build(self):
        sch, W = self.sch, self.W
        self.phase0_consts()
        self.sc_f0 = ExitStack()
        if "ffn" in self.phases:
            self.phase_f0()
        self.phase0_logits()
        with ExitStack() as sc0:
            self.phase0b(sc0)
        with ExitStack() as scA:
            self.phase_groups(scA)
        if "ffn" in self.phases:
            self.f0_finish()
        with ExitStack() as scA4:
            self.phase_a4(scA4)
        self.sc_f0.close()
        self.sc_attn.close()
        if "ffn" in self.phases:
            with ExitStack() as scF:
                self.phase_f(scF)
        if "gwt" in self.debug:
            self.dbg["gwt"] = self.nc.dram_tensor("dbg_gwt", [128, II], F32,
                                                  kind="ExternalOutput")
            sch.dma("gpsimd", lambda e: e.dma_start(
                out=self.dbg["gwt"].ap(), in_=self.d_gw.ap()[0:128, :]),
                waits=W("dS", "v", "g", "dG"), inc="dG")
        if "xqd" in self.debug:
            self.dbg["xqd"] = self.nc.dram_tensor("dbg_xqd", [S, H], F32,
                                                  kind="ExternalOutput")
            sch.dma("gpsimd", lambda e: e.dma_start(out=self.dbg["xqd"].ap(),
                                                    in_=self.d_xq.ap()),
                    waits=W("dS", "v"), inc="dG")
        if "r2d" in self.debug:
            self.dbg["r2d"] = self.nc.dram_tensor("dbg_r2d", [S, H], F32,
                                                  kind="ExternalOutput")
            sch.dma("gpsimd", lambda e: e.dma_start(out=self.dbg["r2d"].ap(),
                                                    in_=self.d_r2.ap()),
                    waits=W("dS", "v"), inc="dG")
        if "lns" in self.debug:
            self.dbg["lns"] = self.nc.dram_tensor("dbg_lns", [128, NT], F32,
                                                  kind="ExternalOutput")
            self.dbg["lnq"] = self.nc.dram_tensor("dbg_lnq", [128, NT], F32,
                                                  kind="ExternalOutput")
            sch.dma("sync", lambda e: e.dma_start(out=self.dbg["lns"].ap(),
                                                  in_=self.ln_s[:]),
                    waits=W("v"))
            sch.dma("sync", lambda e: e.dma_start(out=self.dbg["lnq"].ap(),
                                                  in_=self.ln_q[:]),
                    waits=W("v"))
        sch.emit("gpsimd", lambda e: e.nop(),
                 waits=W("dS", "dG", "dC", "v", "vf", "t", "s", "g"))
        with self.stk:
            blk = self.stk.enter_context(self.nc.Block())
            sch.build(blk, self.semmap)
        return self.nc

    def phase0_consts(self):
        sch = self.sch
        ld = lambda out, in_: sch.dma("sync", lambda e, o=out, i=in_: e.dma_start(
            out=o, in_=i))
        ch = self.ch
        ld(self.ident[:], ch["c_ident"].ap())
        ld(self.swapm[:], ch["c_swap"].ap())
        ld(self.cosd[:], ch["c_cosd"].ap())
        ld(self.sins[:], ch["c_sins"].ap())
        ld(self.cosdc[:], ch["c_cosdc"].ap())
        ld(self.sinsc[:], ch["c_sinsc"].ap())
        ld(self.lmask[:], ch["c_lmask"].ap())
        ld(self.cmask[:], ch["c_cmask"].ap())
        ld(self.emat[:], ch["c_e"].ap())
        ld(self.ident_f[:], ch["c_identf"].ap())
        ld(self.fmat[:], ch["c_f"].ap())
        ld(self.wc_f[:], self.i_wc.ap())
        ld(self.bq_c[:], self.i_bq.ap())
        ld(self.blk_c[:], self.i_blk.ap())
        ld(self.bk_c[:], self.i_bk.ap().rearrange("(o p) -> p o", p=128))
        ld(self.qnw_c[:], self.i_qnw.ap().rearrange("(o p) -> p o", p=128))
        ld(self.knw_c[:], self.i_knw.ap().rearrange("(o p) -> p o", p=128))
        ld(self.bc_c[:], self.i_bc.ap().rearrange("(o p) -> p o", p=1))
        ld(self.sink_f[:], self.i_sk.ap())
        self.CONST_DS = sch.now("dS")
        lg = lambda out, in_: sch.dma("gpsimd", lambda e, o=out, i=in_:
                                      e.dma_start(out=o, in_=i), inc="dG")
        lg(self.d_rows.ap()[0:H], self.i_blv.ap())
        lg(self.d_rows.ap()[H:2 * H], self.i_bo.ap())
        lg(self.d_rows.ap()[2 * H:2 * H + HD], self.i_bv.ap())
        lg(self.d_rows.ap()[3 * H:4 * H], self.i_sv.ap().rearrange("h d -> (h d)"))
        rg = sch.now("dG")
        sch.dma("sync", lambda e: e.dma_start(
            out=self.rows_b[:], in_=self.d_rows.ap().rearrange("(o h) -> o h", o=1)),
            waits=[("dG", rg)])
        lg(self.d_wkb.ap(), self.i_wk.ap())
        lg(self.d_wvb.ap(), self.i_wv.ap())
        self.WKVMIR_DG = sch.now("dG")
        lg(self.d_wqb.ap(), self.i_wq.ap())
        lg(self.d_wlkb.ap(), self.i_wlk.ap())
        lg(self.d_wlvb.ap(), self.i_wlv.ap())
        self.WQKV_DG = sch.now("dG")
        lg(self.d_wob.ap(), self.i_wo.ap())
        self.WO_DG = sch.now("dG")
        self.ROWS_DG = sch.now("dG")

        sch.dma("gpsimd", lambda e: e.dma_start(out=self.d_xbf.ap(),
                                                in_=self.i_x.ap()), inc="dG")
        self.XBF_DG = sch.now("dG")

        sch.emit("vector", lambda e: e.memset(self.ones_row[:], 1.0))
        sch.emit("vector", lambda e: e.memset(self.junk[:], 0.0))
        sch.emit("vector", lambda e: e.memset(self.ln_s[:], 0.0))
        sch.emit("vector", lambda e: e.memset(self.ln_q[:], 0.0))
        sch.emit("vector", lambda e: e.tensor_copy(out=self.sinkb[:],
                                                   in_=self.sink_f[:]),
                 waits=[("dS", self.CONST_DS)])
        sch.emit("vector", lambda e: e.tensor_copy(out=self.wc_bf[:],
                                                   in_=self.wc_f[:]))
        sch.emit("vector", lambda e: e.tensor_copy(out=self.bc_bf[:],
                                                   in_=self.bc_c[:]), inc="v")
        self.V_SMALL = sch.now("v")

    def phase_f0(self):
        sch = self.sch
        SB = lambda n, sh, dt: self.sc_f0.enter_context(
            self.nc.sbuf_tensor(n, sh, dt))
        FW = 1024
        NBW = FW // 32
        self.F0_FW = FW = 1024
        self._f0_t = {
            "wtile": [SB("f0_wa", [128, FW], F32), SB("f0_wb", [128, FW], F32),
                      SB("f0_wc", [128, FW], F32), SB("f0_wd", [128, FW], F32)],
            "w8": [SB("f0_q8a", [128, FW], FP8), SB("f0_q8b", [128, FW], FP8)],
            "wqd": [SB("f0_qda", [128, FW], BF16), SB("f0_qdb", [128, FW], BF16)],
            "wT": [SB("f0_ta", [128, FW], BF16), SB("f0_tb", [128, FW], BF16)],
            "mw": [SB("f0_ma", [128, FW // 32], F32),
                   SB("f0_mb", [128, FW // 32], F32)],
            "s2w": [SB("f0_s2a", [128, FW // 32], F32),
                    SB("f0_s2b", [128, FW // 32], F32)],
            "rrw": [SB("f0_rra", [128, FW // 32], F32),
                    SB("f0_rrb", [128, FW // 32], F32)],
        }

        jobs = []
        for src_, dst, nrow in ((self.i_gw, self.d_gw, II),
                                (self.i_uw, self.d_uw, II),
                                (self.i_dw, self.d_dw, H)):
            ncol = H if nrow == II else II
            for r0 in range(0, nrow, 128):
                for c0 in range(0, ncol, FW):
                    jobs.append((src_, dst, r0, c0))
        self._f0_jobs = jobs
        self._f0_state = {"k": 0, "st": {}, "ld": {}, "vq": {}, "tr": {}}
        self.F0_DG = None

    def f0_emit_jobs(self, n):
        sch = self.sch
        st = self._f0_state
        jobs = self._f0_jobs
        FW = self.F0_FW
        for _ in range(n):
            k = st["k"]
            if k >= len(jobs):
                return
            st["k"] = k + 1
            src_, dst, r0, c0 = jobs[k]
            sl = k % 2
            wtile = self._f0_t["wtile"][k % 4]
            w8 = self._f0_t["w8"][sl]
            wqd = self._f0_t["wqd"][sl]
            wT = self._f0_t["wT"][sl]
            mw = self._f0_t["mw"][sl]
            s2w = self._f0_t["s2w"][sl]
            rrw = self._f0_t["rrw"][sl]
            w = [("g", st["vq"][k - 4])] if k - 4 in st["vq"] else []
            sch.dma("scalar", lambda e, wtile=wtile, src_=src_, r0=r0, c0=c0:
                    e.dma_start(out=wtile[:],
                                in_=src_.ap()[r0:r0 + 128, c0:c0 + FW]),
                    waits=w, inc="dC")
            ldn = sch.now("dC")
            b3 = wtile[:].rearrange("p (b k) -> p b k", k=32)
            sch.emit("vector", lambda e, b3=b3, mw=mw: e.tensor_reduce(
                out=mw[:], in_=b3, axis=AX.X, op=ALU.max,
                apply_absolute_value=True),
                waits=[("dC", ldn), ("g", sch.now("g"))])
            sch.emit("vector", lambda e, mw=mw, s2w=s2w: e.tensor_scalar(
                out=s2w[:], in0=mw[:], scalar1=C224, scalar2=2e-30,
                op0=ALU.mult, op1=ALU.max), gap=1)
            sch.emit("vector", lambda e, s2w=s2w, rrw=rrw: e.reciprocal(
                out=rrw[:], in_=s2w[:]), inc="vf", gap=1)
            vn = sch.now("vf")
            rrb = rrw[:].unsqueeze(2).broadcast_to([128, FW // 32, 32])
            s2b = s2w[:].unsqueeze(2).broadcast_to([128, FW // 32, 32])
            sch.emit("gpsimd", lambda e, b3=b3, rrb=rrb, w8=w8: e.tensor_tensor(
                out=w8[:], in0=b3, in1=rrb, op=ALU.mult), waits=[("vf", vn)])
            sch.emit("gpsimd", lambda e, s2b=s2b, w8=w8, wqd=wqd: e.tensor_tensor(
                out=wqd[:], in0=w8[:], in1=s2b, op=ALU.mult), inc="g")
            st["vq"][k] = sch.now("g")
            pb = self.pb7[:, :].bitcast(BF16)
            w = [("g", st["vq"][k]), ("v", sch.now("v")),
                 ("vf", sch.now("vf"))]

            def ftr(e, wqd=wqd, pb=pb):
                last = None
                for j in range(FW // 128):
                    last = e.transpose(pb[:, j * 128:(j + 1) * 128],
                                       wqd[:, j * 128:(j + 1) * 128],
                                       self.ident[:])
                return last
            sch.emit("tensor", ftr, waits=w, inc="t")
            tn = sch.now("t")
            sch.emit("vector", lambda e, wT=wT, pb=pb: e.tensor_copy(
                out=wT[:], in_=pb[:, 0:FW]), waits=[("t", tn)], inc="vf")
            st["tr"][k] = sch.now("vf")
            out = dst.ap()[c0:c0 + FW, r0:r0 + 128].rearrange(
                "(j p) q -> p j q", p=128)
            sch.dma("scalar", lambda e, out=out, wT=wT: e.dma_start(
                out=out, in_=wT[:].rearrange("p (j q) -> p j q", q=128)),
                waits=[("vf", st["tr"][k])], inc="dC")
            st["st"][k] = sch.now("dC")

    def f0_finish(self):
        sch = self.sch
        while self._f0_state["k"] < len(self._f0_jobs):
            self.f0_emit_jobs(8)
        self.F0_DG = ("dC", sch.now("dC"))
        self.F0_G = sch.now("g")

    def phase0_logits(self):
        sch, W = self.sch, self.W
        lcol = self.lcol

        for sup in range(NSUP):
            tprev = sch.now("t")
            for hc in range(NHC):
                out = self.xfm[:, hc, :]
                in_ = self.d_xbf.ap()[sup * TSUP:(sup + 1) * TSUP,
                                      hc * 128:(hc + 1) * 128]
                w = [("dG", self.XBF_DG)]
                if sup > 0 and hc == 0:
                    w.append(("t", tprev))
                sch.dma("sync", lambda e, o=out, i=in_: e.dma_start_transpose(
                    out=o, in_=i), waits=w)
            dsn = sch.now("dS")

            def fn_t(e, sup=sup):
                last = None
                for t4 in range(4):
                    tt = sup * 4 + t4
                    for hc in range(NHC):
                        last = e.matmul(
                            self.pb7[:, tt:tt + 1],
                            self.xfm[:, hc, t4 * 128:(t4 + 1) * 128],
                            self.wc_bf[:, hc:hc + 1], start=(hc == 0),
                            stop=(hc == NHC - 1))
                return last
            sch.emit("tensor", fn_t,
                     waits=[("dS", dsn),
                            ("v", self.V_SMALL if sup == 0 else sch.now("v"))],
                     inc="t")
        tn = sch.now("t")
        sch.emit("tensor", lambda e: e.matmul(
            self.pb5[:, 0:1], self.ones_row[0:1, 0:128], self.bc_bf[:],
            start=True, stop=True), waits=[("v", self.V_SMALL)], inc="t")
        tbc = sch.now("t")
        sch.emit("vector", lambda e: e.tensor_copy(out=self.bc128[:],
                                                   in_=self.pb5[:, 0:1]),
                 waits=[("t", tbc)])
        sch.emit("vector", lambda e: e.tensor_tensor(
            out=lcol[:], in0=self.pb7[:, 0:NT],
            in1=self.bc128[:].broadcast_to([128, NT]), op=ALU.add),
            waits=[("t", tn)], inc="v", gap=1)
        vl = sch.now("v")
        if self.mkdbg("lcol", [128, NT]):
            sch.dma("sync", lambda e: e.dma_start(out=self.dbg["lcol"].ap(),
                                                  in_=lcol[:]),
                    waits=[("v", vl)])
        if "xfm" in self.debug:
            self.dbg["xfm"] = self.nc.dram_tensor("dbg_xfm", [128, NHC * TSUP],
                                                  BF16, kind="ExternalOutput")
            dsx = sch.now("dS")
            sch.dma("sync", lambda e: e.dma_start(
                out=self.dbg["xfm"].ap(),
                in_=self.xfm[:].rearrange("p c t -> p (c t)")),
                waits=[("dS", dsx)])
        sch.emit("tensor", lambda e: e.transpose(
            self.pb5[0:16, 0:128], lcol[:], self.ident_f[:]),
            waits=[("v", vl)], inc="t")
        t1 = sch.now("t")
        sch.emit("vector", lambda e: e.tensor_copy(out=self.lT[:],
                                                   in_=self.pb5[0:16, 0:128]),
                 waits=[("t", t1)], inc="v")
        l3 = self.lT[:].rearrange("p (b r) -> p b r", r=R)
        sch.emit("vector", lambda e, l3=l3: e.tensor_reduce(
            out=self.lm8[:], in_=l3, axis=AX.X, op=ALU.max), gap=1)
        sch.emit("vector", lambda e, l3=l3: e.tensor_tensor(
            out=l3, in0=l3,
            in1=self.lm8[:].unsqueeze(2).broadcast_to([16, 8, R]),
            op=ALU.subtract), inc="v", gap=1)
        sch.emit("scalar", lambda e: e.activation(
            out=self.lT[:], in_=self.lT[:], func=ACT.Exp), waits=W("v"),
            inc="s")
        sch.emit("vector", lambda e, l3=l3: e.tensor_reduce(
            out=self.ls8[:], in_=l3, axis=AX.X, op=ALU.add), waits=W("s"))
        sch.emit("vector", lambda e: e.reciprocal(out=self.ls8[:],
                                                  in_=self.ls8[:]), gap=1)
        sch.emit("vector", lambda e, l3=l3: e.tensor_tensor(
            out=l3, in0=l3,
            in1=self.ls8[:].unsqueeze(2).broadcast_to([16, 8, R]),
            op=ALU.mult), inc="v", gap=1)
        vw = sch.now("v")
        sch.emit("tensor", lambda e: e.transpose(
            self.pb5[0:128, 0:16], self.lT[:], self.ident_f[0:16, 0:16]),
            waits=[("v", vw)], inc="t")
        t2 = sch.now("t")
        sch.emit("vector", lambda e: e.tensor_copy(out=self.wcol[:],
                                                   in_=self.pb5[0:128, 0:16]),
                 waits=[("t", t2)], inc="v")
        self.V_WCOL = sch.now("v")
        if self.mkdbg("wrow", [128, NT]):
            sch.dma("sync", lambda e: e.dma_start(out=self.dbg["wrow"].ap(),
                                                  in_=self.wcol[:]),
                    waits=[("v", self.V_WCOL)])

    def phase0b(self, sc):
        sch, W, nc = self.sch, self.W, self.nc
        SB = lambda n, sh, dt: sc.enter_context(nc.sbuf_tensor(n, sh, dt))
        self.xt = [SB("xt_a0", [128, H], F32), SB("xt_b0", [128, H], F32)]
        gmat = SB("gmat", [128, NT * 128], BF16)
        sch.dma("sync", lambda e: e.dma_start(out=gmat[:],
                                              in_=self.ch["c_g"].ap()))
        xw_bf = SB("xw_bf", [128, H], BF16)
        entries_b = SB("entries_b", [128, H], BF16)
        entries_fm = SB("entries_fm", [128, NHC, 128], BF16)
        wkb = SB("wkb", [128, NHC, HD], BF16)
        wvb = SB("wvb", [128, NHC, HD], BF16)
        ck_sb = SB("ck_sb", [128, C], F32)
        sq_ck = SB("sq_ck", [128, C], BF16)
        rs8 = SB("rs8", [8, 128], F32)
        rs8b = SB("rs8b", [8, 128], BF16)
        ckb2 = SB("ckb2", [128, C], BF16)
        cksw = SB("cksw", [64, C], BF16)
        rp1 = SB("rp1", [64, 2 * C], BF16)

        WKVm = self.WKVMIR_DG
        sch.dma("sync", lambda e: e.dma_start(
            out=wkb[:], in_=self.d_wkb.ap().rearrange("(c p) d -> p c d", p=128)),
            waits=[("dG", WKVm)])
        sch.dma("sync", lambda e: e.dma_start(
            out=wvb[:], in_=self.d_wvb.ap().rearrange("(c p) d -> p c d", p=128)),
            waits=[("dG", WKVm)])
        WKV = sch.now("dS")
        if "wkb" in self.debug:
            self.dbg["wkb"] = self.nc.dram_tensor("dbg_wkb", [128, NHC * HD],
                                                  BF16, kind="ExternalOutput")
            sch.dma("sync", lambda e: e.dma_start(
                out=self.dbg["wkb"].ap(),
                in_=wkb[:].rearrange("p c d -> p (c d)")), waits=[("dS", WKV)])

        for tt in range(NT):
            xt = self.xt[tt % 2]
            wv = [("v", sch.now("v"))] if tt >= 2 else []
            sch.dma("sync", lambda e, xt=xt, tt=tt: e.dma_start(
                out=xt[:], in_=self.i_x.ap()[tt * 128:(tt + 1) * 128, :]),
                waits=wv)
            dsn = sch.now("dS")
            sch.emit("vector", lambda e, xt=xt, tt=tt: e.tensor_scalar(
                out=xw_bf[:], in0=xt[:], scalar1=self.wcol[:, tt:tt + 1],
                scalar2=None, op0=ALU.mult),
                waits=[("dS", dsn), ("v", self.V_WCOL), ("t", sch.now("t"))],
                inc="v", gap=(2 if tt == 0 else 0))
            vn = sch.now("v")

            def fn(e, tt=tt):
                last = None
                for j in range(4):
                    last = e.matmul(self.ps_big[:, j, :],
                                    gmat[:, tt * 128:(tt + 1) * 128],
                                    xw_bf[:, j * 512:(j + 1) * 512],
                                    start=(tt == 0), stop=(tt == NT - 1))
                return last
            sch.emit("tensor", fn, waits=[("v", vn)], inc="t")
        t_ent = sch.now("t")
        sch.emit("vector", lambda e: e.tensor_copy(out=entries_b[:],
                                                   in_=self.ps_big[:, :, :]),
                 waits=[("t", t_ent)], inc="v")
        ven = sch.now("v")
        if self.mkdbg("entries", [C, H]):
            ent_f = SB("ent_f", [128, H], F32)
            sch.emit("vector", lambda e: e.tensor_copy(out=ent_f[:],
                                                       in_=self.ps_big[:, :, :]),
                     waits=[("t", t_ent)], inc="v")
            sch.dma("sync", lambda e: e.dma_start(out=self.dbg["entries"].ap(),
                                                  in_=ent_f[:]), waits=W("v"))
        for hc in range(NHC):
            w = [("v", ven)]
            if hc >= 2:
                w = [("v", sch.now("v"))]
            sch.emit("tensor", lambda e, hc=hc: e.transpose(
                self.pt0[:, (hc % 2) * 512:(hc % 2) * 512 + 128],
                entries_b[:, hc * 128:(hc + 1) * 128], self.ident[:]),
                waits=w, inc="t")
            tn = sch.now("t")
            sch.emit("vector", lambda e, hc=hc: e.tensor_copy(
                out=entries_fm[:, hc, :],
                in_=self.pt0[:, (hc % 2) * 512:(hc % 2) * 512 + 128]),
                waits=[("t", tn)], inc="v")
        vfm = sch.now("v")
        if "efm" in self.debug:
            self.dbg["efm"] = self.nc.dram_tensor("dbg_efm", [128, NHC * 128],
                                                  BF16, kind="ExternalOutput")
            sch.dma("sync", lambda e: e.dma_start(
                out=self.dbg["efm"].ap(),
                in_=entries_fm[:].rearrange("p c t -> p (c t)")),
                waits=[("v", vfm)])

        def fn_ck(e):
            last = None
            for hc in range(NHC):
                last = e.matmul(self.pb4[:, 0:128], wkb[:, hc, :],
                                entries_fm[:, hc, :], start=(hc == 0),
                                stop=(hc == NHC - 1))
            return last
        sch.emit("tensor", fn_ck, waits=[("v", vfm), ("dS", WKV)], inc="t")
        tck = sch.now("t")

        def fn_cv(e):
            for hc in range(NHC):
                e.matmul(self.pb5[:, 0:128], entries_fm[:, hc, :], wvb[:, hc, :],
                         start=(hc == 0), stop=False)
            return e.matmul(self.pb5[:, 0:128], self.ones_row[0:1, 0:128],
                            self.rows_b[0:1, 2 * H:2 * H + HD], start=False,
                            stop=True)
        sch.emit("tensor", fn_cv, waits=[("v", vfm), ("dS", WKV)], inc="t")
        tcv = sch.now("t")
        sch.emit("vector", lambda e: e.tensor_copy(out=self.cvb[:],
                                                   in_=self.pb5[:, 0:128]),
                 waits=[("t", tcv)], inc="v")
        if self.mkdbg("cv", [C, HD]):
            cv_f = SB("cv_f", [128, HD], F32)
            sch.emit("vector", lambda e: e.tensor_copy(out=cv_f[:],
                                                       in_=self.pb5[:, 0:128]),
                     inc="v")
            sch.dma("sync", lambda e: e.dma_start(out=self.dbg["cv"].ap(),
                                                  in_=cv_f[:]), waits=W("v"))

        sch.emit("vector", lambda e: e.tensor_scalar(
            out=ck_sb[:], in0=self.pb4[:, 0:128], scalar1=self.bk_c[:],
            scalar2=None, op0=ALU.add), waits=[("t", tck)], inc="v")
        if self.mkdbg("cklin", [HD, C]):
            sch.dma("sync", lambda e: e.dma_start(out=self.dbg["cklin"].ap(),
                                                  in_=ck_sb[:]), waits=W("v"))
        sch.emit("scalar", lambda e: e.activation(out=sq_ck[:], in_=ck_sb[:],
                                                  func=ACT.Square),
                 waits=W("v"), inc="s")
        sch.emit("tensor", lambda e: e.matmul(
            self.pb4[0:8, 128:256], self.emat[:, 0:8], sq_ck[:], start=True,
            stop=True), waits=W("s"), inc="t")
        sch.emit("vector", lambda e: e.tensor_scalar(
            out=rs8[0:8, :], in0=self.pb4[0:8, 128:256], scalar1=C128,
            scalar2=EPS_RMS, op0=ALU.mult, op1=ALU.add), waits=W("t"), inc="v")
        sch.emit("scalar", lambda e: e.activation(out=rs8[0:8, :],
                                                  in_=rs8[0:8, :], func=ACT.Sqrt),
                 waits=W("v"), inc="s")
        sch.emit("vector", lambda e: e.reciprocal(out=rs8[0:8, :],
                                                  in_=rs8[0:8, :]), waits=W("s"))
        sch.emit("vector", lambda e: e.tensor_copy(out=rs8b[0:8, :],
                                                   in_=rs8[0:8, :]), inc="v",
                 gap=1)
        sch.emit("tensor", lambda e: e.matmul(
            self.pb5[:, 128:256], self.fmat[0:8, 0:128], rs8b[0:8, :],
            start=True, stop=True), waits=W("v"), inc="t")
        sch.emit("vector", lambda e: e.scalar_tensor_tensor(
            out=ck_sb[:], in0=ck_sb[:], scalar=self.knw_c[:],
            in1=self.pb5[:, 128:256], op0=ALU.mult, op1=ALU.mult),
            waits=W("t"), inc="v")
        sch.emit("vector", lambda e: e.tensor_copy(out=ckb2[:], in_=ck_sb[:]),
                 inc="v", gap=1)
        sch.emit("tensor", lambda e: e.matmul(
            self.pb7[0:64, 0:128], self.swapm[:], ckb2[0:64, :], start=True,
            stop=True), waits=W("v"), inc="t")
        sch.emit("vector", lambda e: e.tensor_copy(
            out=cksw[0:64, :], in_=self.pb7[0:64, 0:128]), waits=W("t"))
        sch.emit("vector", lambda e: e.tensor_tensor(
            out=rp1[0:64, 0:C], in0=cksw[0:64, :], in1=self.sinsc[:],
            op=ALU.mult), gap=1)
        sch.emit("vector", lambda e: e.tensor_tensor(
            out=rp1[0:64, C:2 * C], in0=ckb2[0:64, :], in1=self.cosdc[:],
            op=ALU.mult))
        sch.emit("vector", lambda e: e.tensor_tensor(
            out=ckb2[0:64, :], in0=rp1[0:64, C:2 * C], in1=rp1[0:64, 0:C],
            op=ALU.add), inc="v", gap=1)
        vck = sch.now("v")
        if self.mkdbg("ck", [HD, C]):
            ck_f = SB("ck_f", [128, C], F32)
            sch.emit("vector", lambda e: e.tensor_copy(out=ck_f[:], in_=ckb2[:]),
                     waits=[("v", vck)], inc="v")
            sch.dma("sync", lambda e: e.dma_start(out=self.dbg["ck"].ap(),
                                                  in_=ck_f[:]), waits=W("v"))

        def fn_scb(e):
            last = None
            for h in range(NH):
                e.tensor_copy(out=self.scb[:, h, 0:1],
                              in_=self.sinkb[:, h:h + 1])
                last = e.tensor_copy(out=self.scb[:, h, 1:129], in_=ckb2[:])
            return last
        sch.emit("vector", fn_scb, waits=[("v", vck)], inc="v")

    def phase_groups(self, sc):
        sch, W, nc = self.sch, self.W, self.nc
        SB = lambda n, sh, dt: sc.enter_context(nc.sbuf_tensor(n, sh, dt))
        wq_g = SB("wq_g", [128, NHC, 512], BF16)
        wlk_g = SB("wlk_g", [128, NHC, 512], BF16)
        wlv_g = SB("wlv_g", [128, NHC, 512], BF16)
        qn_sb = SB("qn_sb", [128, HPG, TSUP], F32)
        kn_sb = SB("kn_sb", [128, HPG, TSUP], F32)
        sq_bf = SB("sq_bf", [128, 2 * HPG, TSUP], BF16)
        q_sup = SB("q_sup", [128, HPG, TSUP], BF16)
        lk_sup = [SB("lk_supA", [128, HPG, TSUP], BF16),
                  SB("lk_supB", [128, HPG, TSUP], BF16)]
        lv_sup = [SB("lv_supA", [128, 4, HPG * 128], BF16),
                  SB("lv_supB", [128, 4, HPG * 128], BF16)]
        rs8 = SB("rs8g", [8, TSUP], F32)
        rs8b = SB("rs8gb", [8, TSUP], BF16)
        rope_sc = SB("rope_sc", [64, 4, TSUP], BF16)
        es_c = SB("es_c", [128, 4, 129], BF16)
        es_l = SB("es_l", [128, 4, 256], BF16)
        nm_c = SB("nm_c", [128, 4], F32)
        nm_l = SB("nm_l", [128, 4], F32)
        dn_c = SB("dn_c", [128, 4], F32)
        dn_l = SB("dn_l", [128, 4], F32)
        rc_c = SB("rc_c", [128, 4], F32)
        rc_l = SB("rc_l", [128, 4], F32)
        p_bf = SB("p_bf", [128, 4, 385], BF16)
        pT_sb = SB("pT_sb", [128, 4, 512], BF16)
        ctx_st = SB("ctx_st", [128, 2, HPG, 128], BF16)

        self.ctx_ds = []

        for g in range(NG):
            gsl = slice(g * 512, (g + 1) * 512)
            wload = [("dG", self.WQKV_DG), ("t", sch.now("t")),
                     ("v", sch.now("v"))]
            sch.dma("sync", lambda e, gsl=gsl: e.dma_start(
                out=wq_g[:], in_=self.d_wqb.ap()[:, gsl].rearrange(
                    "(c p) m -> p c m", p=128)), waits=wload)
            sch.dma("sync", lambda e, gsl=gsl: e.dma_start(
                out=wlk_g[:], in_=self.d_wlkb.ap()[:, gsl].rearrange(
                    "(c p) m -> p c m", p=128)))
            sch.dma("sync", lambda e, gsl=gsl: e.dma_start(
                out=wlv_g[:], in_=self.d_wlvb.ap()[:, gsl].rearrange(
                    "(c p) m -> p c m", p=128)))
            WG = sch.now("dS")
            sch.emit("vector", lambda e: e.memset(
                lk_sup[1][:, :, TSUP - 128:TSUP], 0.0),
                waits=[("t", sch.now("t"))])
            sch.emit("vector", lambda e: e.memset(lv_sup[1][:, 3, :], 0.0),
                     inc="v")

            for sup in range(NSUP):
                par = sup % 2
                tprev = sch.now("t")
                for hc in range(NHC):
                    out = self.xfm[:, hc, :]
                    in_ = self.d_xbf.ap()[sup * TSUP:(sup + 1) * TSUP,
                                          hc * 128:(hc + 1) * 128]
                    sch.dma("sync", lambda e, o=out, i=in_:
                            e.dma_start_transpose(out=o, in_=i),
                            waits=[("t", tprev)] if hc == 0 else ())
                XFM = sch.now("dS")

                vprev = sch.now("v")

                def fn_q(e):
                    last = None
                    for h in range(HPG):
                        for hc in range(NHC):
                            last = e.matmul(
                                self.ps_big[:, h, :],
                                wq_g[:, hc, h * 128:(h + 1) * 128],
                                self.xfm[:, hc, :], start=(hc == 0),
                                stop=(hc == NHC - 1))
                    return last
                sch.emit("tensor", fn_q,
                         waits=[("dS", max(XFM, WG)), ("v", vprev),
                                ("s", sch.now("s"))], inc="t")
                tq = sch.now("t")
                for h in range(HPG):
                    hg = g * HPG + h
                    sch.emit("vector", lambda e, h=h, hg=hg: e.tensor_scalar(
                        out=qn_sb[:, h, :], in0=self.ps_big[:, h, :],
                        scalar1=self.bq_c[:, hg:hg + 1], scalar2=None,
                        op0=ALU.add), waits=[("t", tq)], inc="v")
                    sch.emit("scalar", lambda e, h=h: e.activation(
                        out=sq_bf[:, h, :], in_=qn_sb[:, h, :],
                        func=ACT.Square), waits=W("v"), inc="s")
                vq_evac = sch.now("v")

                def fn_lk(e):
                    last = None
                    for h in range(HPG):
                        for hc in range(NHC):
                            last = e.matmul(
                                self.ps_big[:, h, :],
                                wlk_g[:, hc, h * 128:(h + 1) * 128],
                                self.xfm[:, hc, :], start=(hc == 0),
                                stop=(hc == NHC - 1))
                    return last
                sch.emit("tensor", fn_lk, waits=[("v", vq_evac)], inc="t")
                tlk = sch.now("t")
                for h in range(HPG):
                    hg = g * HPG + h
                    sch.emit("vector", lambda e, h=h, hg=hg: e.tensor_scalar(
                        out=kn_sb[:, h, :], in0=self.ps_big[:, h, :],
                        scalar1=self.blk_c[:, hg:hg + 1], scalar2=None,
                        op0=ALU.add), waits=[("t", tlk)], inc="v")
                    sch.emit("scalar", lambda e, h=h: e.activation(
                        out=sq_bf[:, HPG + h, :], in_=kn_sb[:, h, :],
                        func=ACT.Square), waits=W("v"), inc="s")
                vlk_evac = sch.now("v")
                sq_all = sch.now("s")

                def fn_e(e):
                    last = None
                    for r in range(8):
                        last = e.matmul(self.pb4[0:8, 0:TSUP],
                                        self.emat[:, r * 8:(r + 1) * 8],
                                        sq_bf[:, r, :], start=(r == 0),
                                        stop=(r == 7))
                    return last
                sch.emit("tensor", fn_e,
                         waits=[("s", sq_all), ("v", sch.now("v"))], inc="t")
                te = sch.now("t")

                def fn_lv(e, g=g):
                    last = None
                    for t4 in range(4):
                        for hc in range(NHC):
                            e.matmul(self.ps_big[:, t4, :],
                                     self.xfm[:, hc, t4 * 128:(t4 + 1) * 128],
                                     wlv_g[:, hc, :], start=(hc == 0),
                                     stop=False)
                        last = e.matmul(self.ps_big[:, t4, :],
                                        self.ones_row[0:1, 0:128],
                                        self.rows_b[0:1, g * 512:(g + 1) * 512],
                                        start=False, stop=True)
                    return last
                sch.emit("tensor", fn_lv, waits=[("v", vlk_evac)], inc="t")
                tlv = sch.now("t")

                sch.emit("vector", lambda e: e.tensor_scalar(
                    out=rs8[0:8, :], in0=self.pb4[0:8, 0:TSUP], scalar1=C128,
                    scalar2=EPS_RMS, op0=ALU.mult, op1=ALU.add),
                    waits=[("t", te)], inc="v")
                sch.emit("scalar", lambda e: e.activation(
                    out=rs8[0:8, :], in_=rs8[0:8, :], func=ACT.Sqrt),
                    waits=W("v"), inc="s")
                sch.emit("vector", lambda e: e.reciprocal(
                    out=rs8[0:8, :], in_=rs8[0:8, :]), waits=W("s"))
                sch.emit("vector", lambda e: e.tensor_copy(
                    out=rs8b[0:8, :], in_=rs8[0:8, :]), inc="v", gap=1)
                vrs = sch.now("v")

                for t4 in range(4):
                    sch.emit("vector", lambda e, t4=t4, par=par: e.tensor_copy(
                        out=lv_sup[par][:, t4, :], in_=self.ps_big[:, t4, :]),
                        waits=[("t", tlv)], inc="v")
                vlv = sch.now("v")

                cs_sl = self.cosd[:, sup * TSUP:(sup + 1) * TSUP]
                sn_sl = self.sins[:, sup * TSUP:(sup + 1) * TSUP]
                for r in range(8):
                    bank = [self.pb5, self.pb7][r % 2]
                    is_q = r < HPG
                    h = r % HPG
                    dst = q_sup if is_q else lk_sup[par]
                    src = qn_sb if is_q else kn_sb
                    wcol_ = self.qnw_c if is_q else self.knw_c
                    sch.emit("tensor", lambda e, r=r, bank=bank: e.matmul(
                        bank[:, 0:TSUP], self.fmat[0:8, r * 128:(r + 1) * 128],
                        rs8b[0:8, :], start=True, stop=True),
                        waits=[("v", sch.now("v"))], inc="t")
                    tb = sch.now("t")
                    sch.emit("vector", lambda e, h=h, dst=dst, src=src,
                             wcol_=wcol_, bank=bank: e.scalar_tensor_tensor(
                        out=dst[:, h, :], in0=src[:, h, :], scalar=wcol_[:],
                        in1=bank[:, 0:TSUP], op0=ALU.mult, op1=ALU.mult),
                        waits=[("t", tb)], inc="v")
                    vstt = sch.now("v")
                    obank = [self.pb5, self.pb7][(r + 1) % 2]
                    sch.emit("tensor", lambda e, h=h, dst=dst, obank=obank:
                             e.matmul(obank[0:64, 0:TSUP], self.swapm[:],
                                      dst[0:64, h, :], start=True, stop=True),
                             waits=[("v", vstt)], inc="t")
                    tsw = sch.now("t")
                    qsh = rope_sc[0:64, 2, :]
                    sa = rope_sc[0:64, 0, :]
                    sb_ = rope_sc[0:64, 1, :]
                    sch.emit("vector", lambda e, qsh=qsh, obank=obank:
                             e.tensor_copy(out=qsh, in_=obank[0:64, 0:TSUP]),
                             waits=[("t", tsw)], inc="v")
                    sch.emit("vector", lambda e, qsh=qsh, sa=sa, sn_sl=sn_sl:
                             e.tensor_tensor(out=sa, in0=qsh, in1=sn_sl,
                                             op=ALU.mult), gap=1)
                    sch.emit("vector", lambda e, h=h, dst=dst, sb_=sb_,
                             cs_sl=cs_sl: e.tensor_tensor(
                        out=sb_, in0=dst[0:64, h, :], in1=cs_sl, op=ALU.mult))
                    sch.emit("vector", lambda e, h=h, dst=dst, sa=sa, sb_=sb_:
                             e.tensor_tensor(out=dst[0:64, h, :], in0=sb_,
                                             in1=sa, op=ALU.add), inc="v",
                             gap=1)
                vrope = sch.now("v")

                if self.mkdbg("q", [H, S]) and g == 0:
                    pass

                units = [(t4, h) for t4 in range(4) for h in range(HPG)]
                NU = len(units)
                st_mask_v = {}
                st_ts_v = {}
                st_ptev_v = {}
                st_ctx_v = {}
                st_qk_t = {}
                st_tr_t = {}
                st_pv_t = {}
                st_exp_s = {}
                ctx_dma_ds = {}

                def sc_bank(u):
                    return [self.pb4[:, :], self.pb5[:, :],
                            self.ps_big[:, 2, :], self.ps_big[:, 3, :]][u % 4]

                def em_qk(u):
                    t4, h = units[u]
                    hg = g * HPG + h
                    bank = sc_bank(u)
                    tt = sup * 4 + t4
                    if t4 > 0:
                        lkp = lk_sup[par][:, h, (t4 - 1) * 128:t4 * 128]
                    else:
                        lkp = lk_sup[1 - par][:, h, 3 * 128:4 * 128]
                    lkc = lk_sup[par][:, h, t4 * 128:(t4 + 1) * 128]
                    qs = q_sup[:, h, t4 * 128:(t4 + 1) * 128]
                    if u - 4 >= 0:
                        w = [("s", st_exp_s[u - 4])]
                    else:
                        w = [("v", vrope)]
                    if u == 0:
                        w.append(("v", vlv))

                    def fn(e, qs=qs, bank=bank, hg=hg, lkp=lkp, lkc=lkc):
                        e.matmul(bank[:, 0:129], qs, self.scb[:, hg, :],
                                 start=True, stop=True)
                        e.matmul(bank[:, 129:257], qs, lkp, start=True, stop=True)
                        return e.matmul(bank[:, 257:385], qs, lkc, start=True,
                                        stop=True)
                    sch.emit("tensor", fn, waits=w, inc="t")
                    st_qk_t[u] = sch.now("t")

                def em_mask(u):
                    t4, h = units[u]
                    tt = sup * 4 + t4
                    sl = u % 4
                    bank = sc_bank(u)
                    lm = self.lmask[:, 0:256] if tt == 0 else self.lmask[:, 256:512]
                    cm = self.cmask[:, tt * 129:(tt + 1) * 129]
                    wm = [("t", st_qk_t[u])]
                    if u - 4 in st_exp_s:
                        wm.append(("s", st_exp_s[u - 4]))
                    sch.emit("vector", lambda e, bank=bank, cm=cm:
                             e.scalar_tensor_tensor(
                        out=bank[:, 0:129], in0=bank[:, 0:129], scalar=SCALE,
                        in1=cm, op0=ALU.mult, op1=ALU.add),
                        waits=wm)
                    sch.emit("vector", lambda e, bank=bank, lm=lm:
                             e.scalar_tensor_tensor(
                        out=bank[:, 129:385], in0=bank[:, 129:385], scalar=SCALE,
                        in1=lm, op0=ALU.mult, op1=ALU.add))
                    sch.emit("vector", lambda e, bank=bank, sl=sl: e.tensor_reduce(
                        out=nm_c[:, sl:sl + 1], in_=bank[:, 0:129], axis=AX.X,
                        op=ALU.max, negate=True))
                    sch.emit("vector", lambda e, bank=bank, sl=sl: e.tensor_reduce(
                        out=nm_l[:, sl:sl + 1], in_=bank[:, 129:385], axis=AX.X,
                        op=ALU.max, negate=True), inc="v")
                    st_mask_v[u] = sch.now("v")

                def em_exp(u):
                    sl = u % 4

                    bank = sc_bank(u)

                    def fn(e, sl=sl, bank=bank):
                        e.activation(out=es_c[:, sl, :], in_=bank[:, 0:129],
                                     func=ACT.Exp, bias=nm_c[:, sl:sl + 1],
                                     scale=1.0, accum_out=dn_c[:, sl:sl + 1])
                        return e.activation(out=es_l[:, sl, :], in_=bank[:, 129:385],
                                            func=ACT.Exp, bias=nm_l[:, sl:sl + 1],
                                            scale=1.0,
                                            accum_out=dn_l[:, sl:sl + 1])
                    sch.emit("scalar", fn, waits=[("v", st_mask_v[u])], inc="s")
                    st_exp_s[u] = sch.now("s")

                def em_recips(u):
                    sl = u % 4
                    sch.emit("vector", lambda e, sl=sl: e.reciprocal(
                        out=rc_c[:, sl:sl + 1], in_=dn_c[:, sl:sl + 1]),
                        waits=[("s", st_exp_s[u])])
                    sch.emit("vector", lambda e, sl=sl: e.reciprocal(
                        out=rc_l[:, sl:sl + 1], in_=dn_l[:, sl:sl + 1]))

                def em_ts(u):
                    sl = u % 4
                    sch.emit("vector", lambda e, sl=sl: e.tensor_scalar(
                        out=p_bf[:, sl, 0:129], in0=es_c[:, sl, :],
                        scalar1=rc_c[:, sl:sl + 1], scalar2=0.5, op0=ALU.mult,
                        op1=ALU.mult))
                    sch.emit("vector", lambda e, sl=sl: e.tensor_scalar(
                        out=p_bf[:, sl, 129:385], in0=es_l[:, sl, :],
                        scalar1=rc_l[:, sl:sl + 1], scalar2=0.5, op0=ALU.mult,
                        op1=ALU.mult), inc="v")
                    st_ts_v[u] = sch.now("v")

                def em_tr(u):
                    sl = u % 4
                    rg = (u % 2) * 512
                    w = [("v", st_ts_v[u])]
                    if u - 2 >= 0:
                        w.append(("v", st_ptev_v[u - 2]))

                    def fn(e, sl=sl, rg=rg):
                        e.transpose(self.pt0[:, rg:rg + 128], p_bf[:, sl, 1:129],
                                    self.ident[:])
                        e.transpose(self.pt0[:, rg + 128:rg + 256],
                                    p_bf[:, sl, 129:257], self.ident[:])
                        e.transpose(self.pt0[:, rg + 256:rg + 384],
                                    p_bf[:, sl, 257:385], self.ident[:])
                        return e.transpose(self.pt0[0:1, rg + 384:rg + 512],
                                           p_bf[:, sl, 0:1], self.ident[:])
                    sch.emit("tensor", fn, waits=w, inc="t")
                    st_tr_t[u] = sch.now("t")

                def em_ptev(u):
                    sl = u % 4
                    rg = (u % 2) * 512

                    def fn(e, sl=sl, rg=rg):
                        e.tensor_copy(out=pT_sb[:, sl, 0:384],
                                      in_=self.pt0[:, rg:rg + 384])
                        return e.tensor_copy(out=pT_sb[0:1, sl, 384:512],
                                             in_=self.pt0[0:1, rg + 384:rg + 512])
                    sch.emit("vector", fn, waits=[("t", st_tr_t[u])], inc="v")
                    st_ptev_v[u] = sch.now("v")

                def em_pv(u):
                    t4, h = units[u]
                    hg = g * HPG + h
                    sl = u % 4
                    cb = u % 2
                    if t4 > 0:
                        lvp = lv_sup[par][:, t4 - 1, h * 128:(h + 1) * 128]
                    else:
                        lvp = lv_sup[1 - par][:, 3, h * 128:(h + 1) * 128]
                    lvc = lv_sup[par][:, t4, h * 128:(h + 1) * 128]
                    w = [("v", st_ptev_v[u])]
                    if u - 2 >= 0:
                        w.append(("v", st_ctx_v[u - 2]))

                    def fn(e, sl=sl, cb=cb, hg=hg, lvp=lvp, lvc=lvc):
                        e.matmul(self.ps_big[:, cb, 0:128], self.cvb[:],
                                 pT_sb[:, sl, 0:128], start=True, stop=False)
                        e.matmul(self.ps_big[:, cb, 0:128],
                                 self.rows_b[0:1, 3 * H + hg * 128:3 * H + (hg + 1) * 128],
                                 pT_sb[0:1, sl, 384:512], start=False, stop=False)
                        e.matmul(self.ps_big[:, cb, 0:128], lvp,
                                 pT_sb[:, sl, 128:256], start=False, stop=False)
                        return e.matmul(self.ps_big[:, cb, 0:128], lvc,
                                        pT_sb[:, sl, 256:384], start=False,
                                        stop=True)
                    sch.emit("tensor", fn, waits=w, inc="t")
                    st_pv_t[u] = sch.now("t")

                def em_ctx(u):
                    t4, h = units[u]
                    sl = u % 2
                    w = [("t", st_pv_t[u])]
                    if u - 2 * HPG >= 0:
                        w.append(("dS", ctx_dma_ds[u - 2 * HPG]))

                    sch.emit("vector", lambda e, t4=t4, h=h, sl=sl: e.tensor_copy(
                        out=ctx_st[:, t4 % 2, h, :], in_=self.ps_big[:, sl, 0:128]),
                        waits=w, inc="v")
                    st_ctx_v[u] = sch.now("v")
                    if h == HPG - 1:
                        tt = sup * 4 + t4
                        out = self.d_ctx.ap()[g * 512:(g + 1) * 512,
                                              tt * 128:(tt + 1) * 128].rearrange(
                                                  "(h p) t -> p h t", p=128)
                        sch.dma("sync", lambda e, out=out, t4=t4: e.dma_start(
                            out=out, in_=ctx_st[:, t4 % 2, :, :]),
                            waits=[("v", st_ctx_v[u])])
                        for hh in range(HPG):
                            ctx_dma_ds[u - hh] = sch.now("dS")
                        self.ctx_ds.append(sch.now("dS"))

                for u in range(NU + 1):
                    if u < NU:
                        em_qk(u)
                        em_mask(u)
                        em_exp(u)
                        em_recips(u)
                    if 0 <= u - 1:
                        em_ts(u - 1)
                        em_tr(u - 1)
                        em_ptev(u - 1)
                        em_pv(u - 1)
                        em_ctx(u - 1)
                if "ffn" in self.phases:
                    self.f0_emit_jobs(12)

    def phase_a4(self, sc):
        sch, W, nc = self.sch, self.W, self.nc
        SB = lambda n, sh, dt: sc.enter_context(nc.sbuf_tensor(n, sh, dt))
        self.xt = [SB("xt_a4", [128, H], F32), SB("xt_b4", [128, H], F32)]
        wo_sb = SB("wo_sb", [128, NHC, H], BF16)
        ctx_t = SB("ctx_t", [128, 2, NHC, 128], BF16)
        r1_sb = SB("r1_sb", [128, H], F32)
        sq_t = SB("sq_t", [128, H], F32)
        mq = SB("mq", [128, 64], F32)
        s2q = SB("s2q", [128, 64], F32)
        rrq = SB("rrq", [128, 64], F32)
        xq8 = SB("xq8", [128, H], FP8)
        xq_bf = SB("xq_bf", [128, H], BF16)

        sch.dma("sync", lambda e: e.dma_start(
            out=wo_sb[:], in_=self.d_wob.ap().rearrange("(c p) m -> p c m", p=128)),
            waits=[("dG", self.WO_DG), ("t", sch.now("t")), ("v", sch.now("v"))])
        WO = sch.now("dS")
        ctx_all = self.ctx_ds[-1] if self.ctx_ds else 0

        dbg_x1 = self.mkdbg("x1", [S, H])

        for tt in range(NT):
            xt = self.xt[tt % 2]
            par = tt % 2
            inctx = self.d_ctx.ap()[:, tt * 128:(tt + 1) * 128].rearrange(
                "(c p) t -> p c t", p=128)
            wv = [("t", sch.now("t"))] if tt >= 2 else [("dS", ctx_all)]
            sch.dma("sync", lambda e, inctx=inctx, par=par: e.dma_start(
                out=ctx_t[:, par, :, :], in_=inctx), waits=wv)
            sch.dma("sync", lambda e, xt=xt, tt=tt: e.dma_start(
                out=xt[:], in_=self.i_x.ap()[tt * 128:(tt + 1) * 128, :]),
                waits=[("v", sch.now("v"))] if tt >= 2 else ())
            dsn = sch.now("dS")

            def fn(e, par=par):
                for hc in range(NHC):
                    for j in range(4):
                        e.matmul(self.ps_big[:, j, :], ctx_t[:, par, hc, :],
                                 wo_sb[:, hc, j * 512:(j + 1) * 512],
                                 start=(hc == 0), stop=False)
                last = None
                for j in range(4):
                    last = e.matmul(self.ps_big[:, j, :],
                                    self.ones_row[0:1, 0:128],
                                    self.rows_b[0:1, H + j * 512:H + (j + 1) * 512],
                                    start=False, stop=True)
                return last
            sch.emit("tensor", fn,
                     waits=[("dS", max(dsn, WO)), ("v", sch.now("v"))],
                     inc="t")
            tn = sch.now("t")

            sch.emit("vector", lambda e, xt=xt: e.tensor_tensor(
                out=r1_sb[:], in0=self.ps_big[:, :, :], in1=xt[:],
                op=ALU.add), waits=[("t", tn)], inc="v")
            sch.emit("vector", lambda e: e.tensor_reduce(
                out=self.cols[:, 0:1], in_=r1_sb[:], axis=AX.X, op=ALU.add),
                inc="v", gap=1)
            sch.emit("scalar", lambda e: e.activation(
                out=sq_t[:], in_=r1_sb[:], func=ACT.Square,
                accum_out=self.cols[:, 1:2]), waits=W("v"), inc="s")
            sn = sch.now("s")
            sch.emit("vector", lambda e: e.tensor_scalar(
                out=self.cols[:, 2:3], in0=self.cols[:, 0:1], scalar1=CH,
                scalar2=None, op0=ALU.mult), waits=[("s", sn)], gap=1)
            sch.emit("vector", lambda e: e.tensor_scalar(
                out=self.cols[:, 4:5], in0=self.cols[:, 1:2], scalar1=CH,
                scalar2=None, op0=ALU.mult))
            sch.emit("vector", lambda e: e.tensor_tensor(
                out=self.cols[:, 3:4], in0=self.cols[:, 2:3],
                in1=self.cols[:, 2:3], op=ALU.mult))
            sch.emit("vector", lambda e: e.tensor_tensor(
                out=self.cols[:, 5:6], in0=self.cols[:, 4:5],
                in1=self.cols[:, 3:4], op=ALU.subtract), gap=1)
            sch.emit("vector", lambda e: e.tensor_scalar(
                out=self.cols[:, 6:7], in0=self.cols[:, 5:6], scalar1=EPS_LN,
                scalar2=None, op0=ALU.add), inc="v", gap=1)
            sch.emit("scalar", lambda e: e.activation(
                out=self.cols[:, 6:7], in_=self.cols[:, 6:7], func=ACT.Sqrt),
                waits=W("v"), inc="s")
            sn = sch.now("s")
            sch.emit("vector", lambda e: e.reciprocal(
                out=self.cols[:, 7:8], in_=self.cols[:, 6:7]), waits=[("s", sn)])
            self.gap(3)
            sch.emit("vector", lambda e: e.tensor_scalar(
                out=self.x1_t[:], in0=r1_sb[:], scalar1=self.cols[:, 2:3],
                scalar2=self.cols[:, 7:8], op0=ALU.subtract, op1=ALU.mult),
                inc="v")
            x13 = self.x1_t[:].rearrange("p (b k) -> p b k", k=32)
            sch.emit("vector", lambda e, x13=x13: e.tensor_reduce(
                out=mq[:], in_=x13, axis=AX.X, op=ALU.max,
                apply_absolute_value=True), gap=1)
            sch.emit("vector", lambda e: e.tensor_scalar(
                out=s2q[:], in0=mq[:], scalar1=C224, scalar2=2e-30,
                op0=ALU.mult, op1=ALU.max), gap=1)
            sch.emit("vector", lambda e: e.reciprocal(out=rrq[:],
                                                      in_=s2q[:]), gap=1)
            self.gap(3)
            rrb = rrq[:].unsqueeze(2).broadcast_to([128, 64, 32])
            s2b = s2q[:].unsqueeze(2).broadcast_to([128, 64, 32])
            sch.emit("vector", lambda e, x13=x13, rrb=rrb: e.tensor_tensor(
                out=xq8[:], in0=x13, in1=rrb, op=ALU.mult))
            sch.emit("vector", lambda e, s2b=s2b: e.tensor_tensor(
                out=xq_bf[:], in0=xq8[:], in1=s2b, op=ALU.mult),
                inc="v", gap=1)
            vq = sch.now("v")
            sch.dma("sync", lambda e, tt=tt: e.dma_start(
                out=self.d_x1.ap()[tt * 128:(tt + 1) * 128, :], in_=self.x1_t[:]),
                waits=[("v", vq)])
            sch.dma("sync", lambda e, tt=tt: e.dma_start(
                out=self.d_xq.ap()[tt * 128:(tt + 1) * 128, :], in_=xq_bf[:]),
                waits=[("v", vq)])
            if dbg_x1:
                sch.dma("sync", lambda e, tt=tt: e.dma_start(
                    out=self.dbg["x1"].ap()[tt * 128:(tt + 1) * 128, :],
                    in_=self.x1_t[:]), waits=[("v", vq)])
            self.a4_store_ds.append(sch.now("dS"))
            if "ffn" not in self.phases:
                sch.dma("sync", lambda e, tt=tt: e.dma_start(
                    out=self.o_y.ap()[tt * 128:(tt + 1) * 128, :],
                    in_=self.x1_t[:]), waits=[("v", vq)])

    def phase_f(self, sc):
        sch, W, nc = self.sch, self.W, self.nc
        SB = lambda n, sh, dt: sc.enter_context(nc.sbuf_tensor(n, sh, dt))
        self.xt = [SB("xt_af", [128, H], F32), SB("xt_bf", [128, H], F32)]
        xfm_q = self.xfm
        wbig = SB("wbig", [128, 64, 512], BF16)
        hqT = SB("hqT", [128, 64, TSUP], BF16)
        sil = SB("sil", [128, 512], F32)
        h_sb = SB("h_sb", [128, 512], F32)
        mh = SB("mh", [128, 16], F32)
        s2h = SB("s2h", [128, 16], F32)
        rrh = SB("rrh", [128, 16], F32)
        h8 = SB("h8", [128, 512], FP8)
        hq = SB("hq", [128, 2, 512], BF16)
        r2c = SB("r2c", [128, 2, 512], F32)
        sqf = SB("sqf", [128, 512], F32)

        for sup in range(NSUP):
            a4ds = self.a4_store_ds[sup * 4 + 3]
            tprev = sch.now("t")
            for hc in range(NHC):
                out = xfm_q[:, hc, :]
                in_ = self.d_xq.ap()[sup * TSUP:(sup + 1) * TSUP,
                                     hc * 128:(hc + 1) * 128]
                w = [("dS", a4ds)]
                if sup > 0 and hc == 0:
                    w.append(("t", tprev))
                sch.dma("sync", lambda e, o=out, i=in_: e.dma_start_transpose(
                    out=o, in_=i), waits=w)
            XQ = sch.now("dS")

            mm_done = {}
            for ic in range(16):
                q = (ic % 2) * 32
                w = [self.F0_DG]
                if ic - 2 in mm_done:
                    w.append(("t", mm_done[ic - 2]))
                if sup == 0 and ic == 0:
                    w.append(("v", sch.now("v")))
                    w.append(("s", sch.now("s")))
                elif sup > 0 and ic == 0:
                    w.append(("t", tprev))
                sch.dma("sync", lambda e, ic=ic, q=q: e.dma_start(
                    out=wbig[:, q:q + 16, :],
                    in_=self.d_gw.ap()[:, ic * 512:(ic + 1) * 512].rearrange(
                        "(c p) m -> p c m", p=128)), waits=w)
                sch.dma("sync", lambda e, ic=ic, q=q: e.dma_start(
                    out=wbig[:, q + 16:q + 32, :],
                    in_=self.d_uw.ap()[:, ic * 512:(ic + 1) * 512].rearrange(
                        "(c p) m -> p c m", p=128)))
                GW = sch.now("dS")
                for t4 in range(4):
                    pg = self.ps_big[:, (t4 % 2) * 2, :]
                    pu = self.ps_big[:, (t4 % 2) * 2 + 1, :]
                    wmm = [("dS", max(GW, XQ)), ("v", sch.now("v")),
                           ("s", sch.now("s"))]

                    def fn(e, t4=t4, pg=pg, pu=pu, q=q):
                        last = None
                        for hc in range(NHC):
                            lh = xfm_q[:, hc, t4 * 128:(t4 + 1) * 128]
                            e.matmul(pg, lh, wbig[:, q + hc, :],
                                     start=(hc == 0), stop=(hc == NHC - 1))
                            last = e.matmul(pu, lh, wbig[:, q + 16 + hc, :],
                                            start=(hc == 0),
                                            stop=(hc == NHC - 1))
                        return last
                    sch.emit("tensor", fn, waits=wmm, inc="t")
                    tgu = sch.now("t")
                    sch.emit("scalar", lambda e, pg=pg: e.activation(
                        out=sil[:], in_=pg, func=ACT.Silu),
                        waits=[("t", tgu), ("v", sch.now("v"))], inc="s")
                    sn = sch.now("s")
                    sch.emit("vector", lambda e, pu=pu: e.tensor_tensor(
                        out=h_sb[:], in0=sil[:], in1=pu, op=ALU.mult),
                        waits=[("s", sn)], inc="v")
                    h3 = h_sb[:].rearrange("p (b k) -> p b k", k=32)
                    sch.emit("vector", lambda e, h3=h3: e.tensor_reduce(
                        out=mh[:], in_=h3, axis=AX.X, op=ALU.max,
                        apply_absolute_value=True), gap=1)
                    sch.emit("vector", lambda e: e.tensor_scalar(
                        out=s2h[:], in0=mh[:], scalar1=C224, scalar2=2e-30,
                        op0=ALU.mult, op1=ALU.max), gap=1)
                    sch.emit("vector", lambda e: e.reciprocal(out=rrh[:],
                                                              in_=s2h[:]),
                             gap=1)
                    self.gap(3)
                    rrb = rrh[:].unsqueeze(2).broadcast_to([128, 16, 32])
                    s2b = s2h[:].unsqueeze(2).broadcast_to([128, 16, 32])
                    sch.emit("vector", lambda e, h3=h3, rrb=rrb: e.tensor_tensor(
                        out=h8[:], in0=h3, in1=rrb, op=ALU.mult))
                    sch.emit("vector", lambda e, s2b=s2b, t4=t4: e.tensor_tensor(
                        out=hq[:, t4 % 2, :], in0=h8[:], in1=s2b, op=ALU.mult),
                        inc="v", gap=1)
                    vh = sch.now("v")
                    wtr = [("v", vh), ("s", sch.now("s"))]

                    def ftr(e, t4=t4):
                        last = None
                        for isub in range(4):
                            last = e.transpose(
                                self.pt0[:, isub * 128:(isub + 1) * 128],
                                hq[:, t4 % 2, isub * 128:(isub + 1) * 128],
                                self.ident[:])
                        return last
                    sch.emit("tensor", ftr, waits=wtr, inc="t")
                    ttr = sch.now("t")

                    def fev(e, ic=ic, t4=t4):
                        last = None
                        for isub in range(4):
                            last = e.copy(
                                out=hqT[:, ic * 4 + isub,
                                        t4 * 128:(t4 + 1) * 128],
                                in_=self.pt0[:, isub * 128:(isub + 1) * 128])
                        return last
                    sch.emit("scalar", fev, waits=[("t", ttr)], inc="s")
                mm_done[ic] = sch.now("t")

            SEVAC = sch.now("s")
            TGU_ALL = sch.now("t")
            dl_done = {}
            r2_store_ds = {}
            for hcd in range(4):
                for hf in range(2):
                    w = [self.F0_DG]
                    key = hcd * 2 + hf
                    if key == 0:
                        w.append(("t", TGU_ALL))
                    elif key - 2 in dl_done:
                        w.append(("t", dl_done[key - 2]))
                    rows = slice(hf * 4096, (hf + 1) * 4096)
                    sch.dma("sync", lambda e, hcd=hcd, hf=hf, rows=rows: e.dma_start(
                        out=wbig[:, hf * 32:(hf + 1) * 32, :],
                        in_=self.d_dw.ap()[rows, hcd * 512:(hcd + 1) * 512]
                        .rearrange("(c p) m -> p c m", p=128)), waits=w)
                    dlw = sch.now("dS")
                    for t4 in range(4):
                        tt = sup * 4 + t4
                        xt = self.xt[(hcd * 4 + t4) % 2]
                        if hf == 1:
                            sch.dma("sync", lambda e, tt=tt, xt=xt, hcd=hcd:
                                    e.dma_start(
                                out=xt[:, hcd * 512:(hcd + 1) * 512],
                                in_=self.d_x1.ap()[tt * 128:(tt + 1) * 128,
                                                   hcd * 512:(hcd + 1) * 512]),
                                waits=[("v", sch.now("v"))])
                        X1 = sch.now("dS")
                        pbank = self.ps_big[:, t4, :]
                        wmm = [("dS", dlw),
                               ("s", SEVAC if (key == 0 and t4 == 0)
                                else sch.now("s")),
                               ("v", sch.now("v"))]

                        def fn(e, t4=t4, pbank=pbank, hf=hf):
                            last = None
                            for i32 in range(32):
                                icd = hf * 32 + i32
                                last = e.matmul(
                                    pbank,
                                    hqT[:, icd, t4 * 128:(t4 + 1) * 128],
                                    wbig[:, hf * 32 + i32, :],
                                    start=(icd == 0), stop=(icd == 63))
                            return last
                        sch.emit("tensor", fn, waits=wmm, inc="t")
                        tdn = sch.now("t")
                        if hf == 1:
                            sl = t4 % 2
                            wv = [("t", tdn), ("dS", X1)]
                            if (hcd, t4 - 2) in r2_store_ds:
                                wv.append(("dS", r2_store_ds[(hcd, t4 - 2)]))
                            elif hcd > 0 and (hcd - 1, t4 + 2) in r2_store_ds:
                                wv.append(("dS", r2_store_ds[(hcd - 1, t4 + 2)]))
                            sch.emit("vector", lambda e, pbank=pbank, sl=sl,
                                     xt=xt, hcd=hcd: e.tensor_tensor(
                                out=r2c[:, sl, :], in0=pbank,
                                in1=xt[:, hcd * 512:(hcd + 1) * 512],
                                op=ALU.add),
                                waits=wv, inc="v")
                            sch.emit("vector", lambda e, sl=sl: e.tensor_reduce(
                                out=self.cols[:, 8:9], in_=r2c[:, sl, :],
                                axis=AX.X, op=ALU.add), gap=1)
                            sch.emit("vector", lambda e, tt=tt: e.tensor_tensor(
                                out=self.ln_s[:, tt:tt + 1],
                                in0=self.ln_s[:, tt:tt + 1],
                                in1=self.cols[:, 8:9], op=ALU.add), gap=1)
                            sch.emit("scalar", lambda e, sl=sl: e.activation(
                                out=sqf[:], in_=r2c[:, sl, :],
                                func=ACT.Square, accum_out=self.cols[:, 9:10]),
                                waits=W("v"), inc="s")
                            sn2 = sch.now("s")
                            sch.emit("vector", lambda e, tt=tt: e.tensor_tensor(
                                out=self.ln_q[:, tt:tt + 1],
                                in0=self.ln_q[:, tt:tt + 1],
                                in1=self.cols[:, 9:10], op=ALU.add),
                                waits=[("s", sn2)], inc="v")
                            vr2 = sch.now("v")
                            sch.dma("sync", lambda e, tt=tt, sl=sl, hcd=hcd:
                                    e.dma_start(
                                out=self.d_r2.ap()[tt * 128:(tt + 1) * 128,
                                                   hcd * 512:(hcd + 1) * 512],
                                in_=r2c[:, sl, :]), waits=[("v", vr2)])
                            r2_store_ds[(hcd, t4)] = sch.now("dS")
                    dl_done[key] = sch.now("t")
            R2D = sch.now("dS")

            for t4 in range(4):
                tt = sup * 4 + t4
                xt = self.xt[t4 % 2]
                sch.dma("sync", lambda e, tt=tt, xt=xt: e.dma_start(
                    out=xt[:], in_=self.d_r2.ap()[tt * 128:(tt + 1) * 128, :]),
                    waits=[("dS", R2D), ("v", sch.now("v"))])
                dsn = sch.now("dS")
                sch.emit("vector", lambda e, tt=tt: e.tensor_scalar(
                    out=self.cols[:, 2:3], in0=self.ln_s[:, tt:tt + 1],
                    scalar1=CH, scalar2=None, op0=ALU.mult),
                    waits=[("dS", dsn)])
                sch.emit("vector", lambda e, tt=tt: e.tensor_scalar(
                    out=self.cols[:, 4:5], in0=self.ln_q[:, tt:tt + 1],
                    scalar1=CH, scalar2=None, op0=ALU.mult))
                sch.emit("vector", lambda e: e.tensor_tensor(
                    out=self.cols[:, 3:4], in0=self.cols[:, 2:3],
                    in1=self.cols[:, 2:3], op=ALU.mult))
                sch.emit("vector", lambda e: e.tensor_tensor(
                    out=self.cols[:, 5:6], in0=self.cols[:, 4:5],
                    in1=self.cols[:, 3:4], op=ALU.subtract), gap=1)
                sch.emit("vector", lambda e: e.tensor_scalar(
                    out=self.cols[:, 6:7], in0=self.cols[:, 5:6],
                    scalar1=EPS_LN, scalar2=None, op0=ALU.add), inc="v", gap=1)
                sch.emit("scalar", lambda e: e.activation(
                    out=self.cols[:, 6:7], in_=self.cols[:, 6:7],
                    func=ACT.Sqrt), waits=W("v"), inc="s")
                sn = sch.now("s")
                sch.emit("vector", lambda e: e.reciprocal(
                    out=self.cols[:, 7:8], in_=self.cols[:, 6:7]),
                    waits=[("s", sn)])
                self.gap(3)
                sch.emit("vector", lambda e, xt=xt: e.tensor_scalar(
                    out=self.x1_t[:], in0=xt[:], scalar1=self.cols[:, 2:3],
                    scalar2=self.cols[:, 7:8], op0=ALU.subtract, op1=ALU.mult),
                    inc="v")
                vy = sch.now("v")
                sch.dma("sync", lambda e, tt=tt: e.dma_start(
                    out=self.o_y.ap()[tt * 128:(tt + 1) * 128, :],
                    in_=self.x1_t[:]), waits=[("v", vy)])


_NC_CACHE = {}


def _get_nc(debug=(), phases=("attn", "ffn"), paranoid=False):
    key = (tuple(sorted(debug)), tuple(sorted(phases)), paranoid)
    if key not in _NC_CACHE:
        kb = KB(debug=debug, phases=phases, paranoid=paranoid)
        _NC_CACHE[key] = (kb.build(), dict(kb.dbg), kb.cs)
    return _NC_CACHE[key]


def kernel(**inputs):
    nc, dbg, cs = _get_nc(paranoid=False)
    x_full = np.ascontiguousarray(inputs["L0_attn_hidden_states"])
    base = {
        "i_wq": np.ascontiguousarray(inputs["L0_attn_Wq"]),
        "i_wlk": np.ascontiguousarray(inputs["L0_attn_Wlk"]),
        "i_wlv": np.ascontiguousarray(inputs["L0_attn_Wlv"]),
        "i_wo": np.ascontiguousarray(inputs["L0_attn_Wo"]),
        "i_wk": np.ascontiguousarray(inputs["L0_attn_Wk"]),
        "i_wv": np.ascontiguousarray(inputs["L0_attn_Wv"]),
        "i_wc": np.ascontiguousarray(
            inputs["L0_attn_Wc"].reshape(NHC, 128).T),
        "i_bq": np.ascontiguousarray(
            inputs["L0_attn_bq"].reshape(NHC, 128).T),
        "i_blk": np.ascontiguousarray(
            inputs["L0_attn_blk"].reshape(NHC, 128).T),
        "i_blv": np.ascontiguousarray(inputs["L0_attn_blv"]),
        "i_bo": np.ascontiguousarray(inputs["L0_attn_bo"]),
        "i_bk": np.ascontiguousarray(inputs["L0_attn_bk"]),
        "i_bv": np.ascontiguousarray(inputs["L0_attn_bv"]),
        "i_bc": np.ascontiguousarray(inputs["L0_attn_bc"]),
        "i_qnw": np.ascontiguousarray(inputs["L0_attn_qn_w"]),
        "i_knw": np.ascontiguousarray(inputs["L0_attn_kn_w"]),
        "i_sk": np.ascontiguousarray(inputs["L0_attn_sink_k"].T),
        "i_sv": np.ascontiguousarray(inputs["L0_attn_sink_v"]),
        "i_gw": np.ascontiguousarray(inputs["L0_ffn_gate_w"]),
        "i_uw": np.ascontiguousarray(inputs["L0_ffn_up_w"]),
        "i_dw": np.ascontiguousarray(inputs["L0_ffn_down_w"]),
    }
    base.update(cs)
    in_maps = []
    for c in range(B):
        m = dict(base)
        m["i_x"] = np.ascontiguousarray(x_full[c])
        in_maps.append(m)
    res = run_bass_kernel_spmd(nc, in_maps, core_ids=list(range(B)))
    return np.stack([r["o_y"] for r in res.results], axis=0)


# version 09
# speedup vs baseline: 1.3080x; 1.0834x over previous
import math
from contextlib import ExitStack

import numpy as np
import ml_dtypes

import concourse.bass as bass
import concourse.mybir as mybir
from concourse.bass_utils import run_bass_kernel_spmd

F32 = mybir.dt.float32
BF16 = mybir.dt.bfloat16
FP8 = mybir.dt.float8e4
ALU = mybir.AluOpType
ACT = mybir.ActivationFunctionType
AX = mybir.AxisListType

B, S, H = 8, 2048, 2048
NH, HD, R, C, WIN = 16, 128, 16, 128, 128
ROPE, HALF = 64, 32
EPS_RMS, EPS_LN = 1e-6, 1e-5
II = 8192
SCALE = 1.0 / math.sqrt(HD)
NEG = -1.0e9
NT = 16
NSUP, TSUP = 4, 512
NG, HPG = 4, 4
NHC = 16
C224 = float(np.float32(1.0) / np.float32(224.0))
C128 = 1.0 / 128.0
CH = 1.0 / float(H)


def host_consts():
    cs = {}
    cs["c_ident"] = np.eye(128, dtype=np.float32).astype(ml_dtypes.bfloat16)
    cs["c_identf"] = np.eye(128, dtype=np.float32)
    swap = np.zeros((64, 64), np.float32)
    swap[:32, 32:] = np.eye(32)
    swap[32:, :32] = np.eye(32)
    cs["c_swap"] = swap.astype(ml_dtypes.bfloat16)
    inv_freq = 1.0 / 10000.0 ** (np.arange(HALF, dtype=np.float64) * 2.0 / ROPE)
    pos = np.arange(S, dtype=np.float64)
    ang = pos[None, :] * inv_freq[:, None]
    cs["c_cosd"] = np.concatenate([np.cos(ang)] * 2, 0).astype(np.float32).astype(ml_dtypes.bfloat16)
    cs["c_sins"] = np.concatenate([-np.sin(ang), np.sin(ang)], 0).astype(np.float32).astype(ml_dtypes.bfloat16)
    pc = np.arange(C, dtype=np.float64) * R + (R - 1)
    angc = pc[None, :] * inv_freq[:, None]
    cs["c_cosdc"] = np.concatenate([np.cos(angc)] * 2, 0).astype(np.float32).astype(ml_dtypes.bfloat16)
    cs["c_sinsc"] = np.concatenate([-np.sin(angc), np.sin(angc)], 0).astype(np.float32).astype(ml_dtypes.bfloat16)
    qi = np.arange(128)[:, None]
    j = np.arange(256)[None, :]
    gen = np.where((j >= qi) & (j <= qi + WIN), 0.0, NEG)
    t0 = np.where((j >= 128) & (j - 128 <= qi), 0.0, NEG)
    cs["c_lmask"] = np.concatenate([t0, gen], 1).astype(np.float32)
    cm = np.zeros((128, NT, 129), np.float32)
    cidx = np.arange(C)[None, :]
    for tt in range(NT):
        cm[:, tt, 1:] = np.where(cidx * R + (R - 1) <= tt * 128 + qi, 0.0, NEG)
    cs["c_cmask"] = cm.reshape(128, NT * 129)
    g = np.zeros((NT, 128, 128), np.float32)
    for tt in range(NT):
        for lt in range(128):
            g[tt, lt, 8 * tt + lt // 16] = 1.0
    cs["c_g"] = g.transpose(1, 0, 2).reshape(128, NT * 128).astype(ml_dtypes.bfloat16)
    e = np.zeros((128, 8, 8), np.float32)
    for rr in range(8):
        e[:, rr, rr] = 1.0
    cs["c_e"] = e.reshape(128, 64).astype(ml_dtypes.bfloat16)
    f = np.zeros((8, 8, 128), np.float32)
    for rr in range(8):
        f[rr, rr, :] = 1.0
    cs["c_f"] = f.reshape(8, 8 * 128).astype(ml_dtypes.bfloat16)
    return cs


class Sched:
    ENGINES = ("tensor", "vector", "scalar", "gpsimd", "sync")

    def __init__(self):
        self.steps = {e: [] for e in self.ENGINES}
        self.counts = {}
        self.junk_fn = None

    def emit(self, engine, fn, waits=(), inc=None, inc_n=1, gap=0):
        for _ in range(gap):
            self.steps[engine].append((self.junk_fn, [], []))
        self._emit(engine, fn, waits, inc, inc_n)

    def _emit(self, engine, fn, waits=(), inc=None, inc_n=1):
        incs = []
        if inc is not None:
            self.counts[inc] = self.counts.get(inc, 0) + inc_n
            incs = [(inc, inc_n)]
        self.steps[engine].append(
            (fn, [(s, v) for (s, v) in waits if v > 0], incs))

    def dma(self, engine, fn, waits=(), inc="dS"):
        self._emit(engine, fn, waits, inc, inc_n=16)

    def now(self, sem):
        return self.counts.get(sem, 0)

    def build(self, block, semmap):
        for ename in self.ENGINES:
            steps = self.steps[ename]
            if not steps:
                continue

            def make_body(steps=list(steps)):
                def body(eng):
                    for fn, waits, incs in steps:
                        for s, v in waits:
                            eng.wait_ge(semmap[s], v)
                        inst = fn(eng)
                        for s, n in incs:
                            assert inst is not None
                            inst.then_inc(semmap[s], n)
                return body
            getattr(block, ename)(make_body())


class KB:
    def __init__(self, debug=(), phases=("attn", "ffn"), paranoid=False):
        self.debug = set(debug)
        self.phases = set(phases)
        self.paranoid = paranoid
        nc = self.nc = bass.Bass("TRN2", target_bir_lowering=False)
        self.sch = Sched()
        self.sch.junk_fn = lambda e: e.tensor_copy(out=self.junk[:, 0:1],
                                                   in_=self.junk[:, 1:2])
        if paranoid:
            _orig_emit = self.sch.emit
            def _pemit(engine, fn, waits=(), inc=None, inc_n=1):
                if engine == "vector" and inc_n == 1:
                    _orig_emit("vector", lambda e: e.tensor_copy(
                        out=self.junk[:, 0:1], in_=self.junk[:, 1:2]))
                _orig_emit(engine, fn, waits, inc, inc_n)
            self.sch.emit = _pemit
        self.stk = ExitStack()
        self.dbg = {}

        D = lambda n, sh: nc.dram_tensor(n, sh, F32, kind="ExternalInput")
        self.i_x = D("i_x", [S, H])
        self.i_wq, self.i_wlk = D("i_wq", [H, H]), D("i_wlk", [H, H])
        self.i_wlv, self.i_wo = D("i_wlv", [H, H]), D("i_wo", [H, H])
        self.i_wk, self.i_wv = D("i_wk", [H, HD]), D("i_wv", [H, HD])
        self.i_wc = D("i_wc", [128, NHC])
        self.i_bq, self.i_blk = D("i_bq", [128, NHC]), D("i_blk", [128, NHC])
        self.i_blv, self.i_bo = D("i_blv", [H]), D("i_bo", [H])
        self.i_bk, self.i_bv, self.i_bc = D("i_bk", [HD]), D("i_bv", [HD]), D("i_bc", [1])
        self.i_qnw, self.i_knw = D("i_qnw", [HD]), D("i_knw", [HD])
        self.i_sk, self.i_sv = D("i_sk", [128, NH]), D("i_sv", [NH, HD])
        self.i_gw, self.i_uw = D("i_gw", [II, H]), D("i_uw", [II, H])
        self.i_dw = D("i_dw", [H, II])
        self.cs = host_consts()
        self.ch = {}
        for k, v in self.cs.items():
            dt = BF16 if v.dtype == ml_dtypes.bfloat16 else F32
            self.ch[k] = nc.dram_tensor(k, list(v.shape), dt, kind="ExternalInput")

        self.o_y = nc.dram_tensor("o_y", [S, H], F32, kind="ExternalOutput")
        DI = lambda n, sh, dt: nc.dram_tensor(n, sh, dt, kind="Internal")
        self.d_xbf = DI("d_xbf", [S, H], BF16)
        self.d_wrow = DI("d_wrow", [S], F32)
        self.d_ctx = DI("d_ctx", [H, S], BF16)
        self.d_x1 = DI("d_x1", [S, H], F32)
        self.d_xq = DI("d_xq", [S, H], BF16)
        self.d_r2 = DI("d_r2", [S, H], F32)
        self.d_wqb = DI("d_wqb", [H, H], BF16)
        self.d_wlkb = DI("d_wlkb", [H, H], BF16)
        self.d_wlvb = DI("d_wlvb", [H, H], BF16)
        self.d_wob = DI("d_wob", [H, H], BF16)
        self.d_wkb = DI("d_wkb", [H, HD], BF16)
        self.d_wvb = DI("d_wvb", [H, HD], BF16)
        self.d_rows = DI("d_rows", [4 * H], BF16)
        self.d_gw = DI("d_gw", [H, II], BF16)
        self.d_uw = DI("d_uw", [H, II], BF16)
        self.d_dw = DI("d_dw", [II, H], BF16)

        self.semmap = {n: self.stk.enter_context(nc.semaphore("sem_" + n))
                       for n in ("t", "v", "s", "g", "dS", "dG", "vf", "dC")}

        PS = lambda n, sh, dt: self.stk.enter_context(nc.psum_tensor(n, sh, dt))
        self.ps_big = PS("ps_big", [128, 4, 512], F32)
        self.pb4 = PS("pb4", [128, 512], F32)
        self.pb5 = PS("pb5", [128, 512], F32)
        self.pt0 = PS("pt0", [128, 1024], BF16)
        self.pb7 = PS("pb7", [128, 512], F32)

        SBs = self.SBs = lambda n, sh, dt: self.stk.enter_context(
            nc.sbuf_tensor(n, sh, dt))
        self.ident = SBs("ident", [128, 128], BF16)
        self.ones_row = SBs("ones_row", [1, 512], BF16)
        self.rows_b = SBs("rows_b", [1, 4 * H], BF16)
        self.x1_t = SBs("x1_t", [128, H], F32)
        self.cols = SBs("cols", [128, 16], F32)
        self.ln_s = SBs("ln_s", [128, NT], F32)
        self.ln_q = SBs("ln_q", [128, NT], F32)
        self.junk = SBs("junk", [128, 4], F32)
        self.xfm = SBs("xfm", [128, NHC, TSUP], BF16)
        self.wcol = SBs("wcol", [128, NT], F32)
        self.sc_attn = ExitStack()
        SBa = self.SBa = lambda n, sh, dt: self.sc_attn.enter_context(
            nc.sbuf_tensor(n, sh, dt))
        self.swapm = SBa("swapm", [64, 64], BF16)
        self.cosd = SBa("cosd", [64, S], BF16)
        self.sins = SBa("sins", [64, S], BF16)
        self.cosdc = SBa("cosdc", [64, C], BF16)
        self.sinsc = SBa("sinsc", [64, C], BF16)
        self.lmask = SBa("lmask", [128, 512], F32)
        self.cmask = SBa("cmask", [128, NT * 129], F32)
        self.emat = SBa("emat", [128, 64], BF16)
        self.fmat = SBa("fmat", [8, 8 * 128], BF16)
        self.wc_bf = SBa("wc_bf", [128, NHC], BF16)
        self.wc_f = SBa("wc_f", [128, NHC], F32)
        self.bq_c = SBa("bq_c", [128, NHC], F32)
        self.blk_c = SBa("blk_c", [128, NHC], F32)
        self.bk_c = SBa("bk_c", [128, 1], F32)
        self.qnw_c = SBa("qnw_c", [128, 1], F32)
        self.knw_c = SBa("knw_c", [128, 1], F32)
        self.bc_c = SBa("bc_c", [1, 1], F32)
        self.bc_bf = SBa("bc_bf", [1, 1], BF16)
        self.bc128 = SBa("bc128", [128, 1], F32)
        self.sink_f = SBa("sink_f", [128, NH], F32)
        self.sinkb = SBa("sinkb", [128, NH], BF16)
        self.scb = SBa("scb", [128, NH, 129], BF16)
        self.cvb = SBa("cvb", [128, HD], BF16)
        self.lcol = SBa("lcol", [128, NT], F32)
        self.ident_f = SBa("ident_f", [128, 128], F32)
        self.lT = SBa("lT", [16, 128], F32)
        self.lm8 = SBa("lm8", [16, 8], F32)
        self.ls8 = SBa("ls8", [16, 8], F32)
        if "wrow" in self.debug:
            self.wrow_st = SBa("wrow_st", [1, S], F32)

        self.W = lambda *ss: [(s, self.sch.now(s)) for s in ss]
        self.a4_store_ds = []

    def mkdbg(self, name, shape):
        if name in self.debug:
            self.dbg[name] = self.nc.dram_tensor("dbg_" + name, shape, F32,
                                                 kind="ExternalOutput")
            return True
        return False

    def gap(self, n=3):
        for _ in range(n):
            self.sch.emit("vector", lambda e: e.tensor_copy(
                out=self.junk[:, 0:1], in_=self.junk[:, 1:2]))

    def build(self):
        sch, W = self.sch, self.W
        self.phase0_consts()
        self.sc_f0 = ExitStack()
        if "ffn" in self.phases:
            self.phase_f0()
        self.phase0_logits()
        with ExitStack() as sc0:
            self.phase0b(sc0)
        with ExitStack() as scA:
            self.phase_groups(scA)
        if "ffn" in self.phases:
            self.f0_finish()
        with ExitStack() as scA4:
            self.phase_a4(scA4)
        self.sc_f0.close()
        self.sc_attn.close()
        if "ffn" in self.phases:
            with ExitStack() as scF:
                self.phase_f(scF)
        if "gwt" in self.debug:
            self.dbg["gwt"] = self.nc.dram_tensor("dbg_gwt", [128, II], F32,
                                                  kind="ExternalOutput")
            sch.dma("gpsimd", lambda e: e.dma_start(
                out=self.dbg["gwt"].ap(), in_=self.d_gw.ap()[0:128, :]),
                waits=W("dS", "v", "g", "dG"), inc="dG")
        if "xqd" in self.debug:
            self.dbg["xqd"] = self.nc.dram_tensor("dbg_xqd", [S, H], F32,
                                                  kind="ExternalOutput")
            sch.dma("gpsimd", lambda e: e.dma_start(out=self.dbg["xqd"].ap(),
                                                    in_=self.d_xq.ap()),
                    waits=W("dS", "v"), inc="dG")
        if "r2d" in self.debug:
            self.dbg["r2d"] = self.nc.dram_tensor("dbg_r2d", [S, H], F32,
                                                  kind="ExternalOutput")
            sch.dma("gpsimd", lambda e: e.dma_start(out=self.dbg["r2d"].ap(),
                                                    in_=self.d_r2.ap()),
                    waits=W("dS", "v"), inc="dG")
        if "lns" in self.debug:
            self.dbg["lns"] = self.nc.dram_tensor("dbg_lns", [128, NT], F32,
                                                  kind="ExternalOutput")
            self.dbg["lnq"] = self.nc.dram_tensor("dbg_lnq", [128, NT], F32,
                                                  kind="ExternalOutput")
            sch.dma("sync", lambda e: e.dma_start(out=self.dbg["lns"].ap(),
                                                  in_=self.ln_s[:]),
                    waits=W("v"))
            sch.dma("sync", lambda e: e.dma_start(out=self.dbg["lnq"].ap(),
                                                  in_=self.ln_q[:]),
                    waits=W("v"))
        sch.emit("gpsimd", lambda e: e.nop(),
                 waits=W("dS", "dG", "dC", "v", "vf", "t", "s", "g"))
        with self.stk:
            blk = self.stk.enter_context(self.nc.Block())
            sch.build(blk, self.semmap)
        return self.nc

    def phase0_consts(self):
        sch = self.sch
        ld = lambda out, in_: sch.dma("sync", lambda e, o=out, i=in_: e.dma_start(
            out=o, in_=i))
        ch = self.ch
        ld(self.ident[:], ch["c_ident"].ap())
        ld(self.swapm[:], ch["c_swap"].ap())
        ld(self.cosd[:], ch["c_cosd"].ap())
        ld(self.sins[:], ch["c_sins"].ap())
        ld(self.cosdc[:], ch["c_cosdc"].ap())
        ld(self.sinsc[:], ch["c_sinsc"].ap())
        ld(self.lmask[:], ch["c_lmask"].ap())
        ld(self.cmask[:], ch["c_cmask"].ap())
        ld(self.emat[:], ch["c_e"].ap())
        ld(self.ident_f[:], ch["c_identf"].ap())
        ld(self.fmat[:], ch["c_f"].ap())
        ld(self.wc_f[:], self.i_wc.ap())
        ld(self.bq_c[:], self.i_bq.ap())
        ld(self.blk_c[:], self.i_blk.ap())
        ld(self.bk_c[:], self.i_bk.ap().rearrange("(o p) -> p o", p=128))
        ld(self.qnw_c[:], self.i_qnw.ap().rearrange("(o p) -> p o", p=128))
        ld(self.knw_c[:], self.i_knw.ap().rearrange("(o p) -> p o", p=128))
        ld(self.bc_c[:], self.i_bc.ap().rearrange("(o p) -> p o", p=1))
        ld(self.sink_f[:], self.i_sk.ap())
        self.CONST_DS = sch.now("dS")
        lg = lambda out, in_: sch.dma("gpsimd", lambda e, o=out, i=in_:
                                      e.dma_start(out=o, in_=i), inc="dG")
        lg(self.d_rows.ap()[0:H], self.i_blv.ap())
        lg(self.d_rows.ap()[H:2 * H], self.i_bo.ap())
        lg(self.d_rows.ap()[2 * H:2 * H + HD], self.i_bv.ap())
        lg(self.d_rows.ap()[3 * H:4 * H], self.i_sv.ap().rearrange("h d -> (h d)"))
        rg = sch.now("dG")
        sch.dma("sync", lambda e: e.dma_start(
            out=self.rows_b[:], in_=self.d_rows.ap().rearrange("(o h) -> o h", o=1)),
            waits=[("dG", rg)])
        lg(self.d_wkb.ap(), self.i_wk.ap())
        lg(self.d_wvb.ap(), self.i_wv.ap())
        self.WKVMIR_DG = sch.now("dG")
        lg(self.d_wqb.ap(), self.i_wq.ap())
        lg(self.d_wlkb.ap(), self.i_wlk.ap())
        lg(self.d_wlvb.ap(), self.i_wlv.ap())
        self.WQKV_DG = sch.now("dG")
        lg(self.d_wob.ap(), self.i_wo.ap())
        self.WO_DG = sch.now("dG")
        self.ROWS_DG = sch.now("dG")

        sch.dma("gpsimd", lambda e: e.dma_start(out=self.d_xbf.ap(),
                                                in_=self.i_x.ap()), inc="dG")
        self.XBF_DG = sch.now("dG")

        sch.emit("vector", lambda e: e.memset(self.ones_row[:], 1.0))
        sch.emit("vector", lambda e: e.memset(self.junk[:], 0.0))
        sch.emit("vector", lambda e: e.memset(self.ln_s[:], 0.0))
        sch.emit("vector", lambda e: e.memset(self.ln_q[:], 0.0))
        sch.emit("vector", lambda e: e.tensor_copy(out=self.sinkb[:],
                                                   in_=self.sink_f[:]),
                 waits=[("dS", self.CONST_DS)])
        sch.emit("vector", lambda e: e.tensor_copy(out=self.wc_bf[:],
                                                   in_=self.wc_f[:]))
        sch.emit("vector", lambda e: e.tensor_copy(out=self.bc_bf[:],
                                                   in_=self.bc_c[:]), inc="v")
        self.V_SMALL = sch.now("v")

    def phase_f0(self):
        sch = self.sch
        SB = lambda n, sh, dt: self.sc_f0.enter_context(
            self.nc.sbuf_tensor(n, sh, dt))
        FW = 1024
        NBW = FW // 32
        self.F0_FW = FW = 1024
        self._f0_t = {
            "wtile": [SB("f0_wa", [128, FW], F32), SB("f0_wb", [128, FW], F32),
                      SB("f0_wc", [128, FW], F32), SB("f0_wd", [128, FW], F32)],
            "w8": [SB("f0_q8a", [128, FW], FP8), SB("f0_q8b", [128, FW], FP8)],
            "wqd": [SB("f0_qda", [128, FW], BF16), SB("f0_qdb", [128, FW], BF16)],
            "wT": [SB("f0_ta", [128, FW], BF16), SB("f0_tb", [128, FW], BF16)],
            "mw": [SB("f0_ma", [128, FW // 32], F32),
                   SB("f0_mb", [128, FW // 32], F32)],
            "s2w": [SB("f0_s2a", [128, FW // 32], F32),
                    SB("f0_s2b", [128, FW // 32], F32)],
            "rrw": [SB("f0_rra", [128, FW // 32], F32),
                    SB("f0_rrb", [128, FW // 32], F32)],
        }

        jobs = []
        for src_, dst, nrow in ((self.i_gw, self.d_gw, II),
                                (self.i_uw, self.d_uw, II),
                                (self.i_dw, self.d_dw, H)):
            ncol = H if nrow == II else II
            for r0 in range(0, nrow, 128):
                for c0 in range(0, ncol, FW):
                    jobs.append((src_, dst, r0, c0))
        self._f0_jobs = jobs
        self._f0_state = {"k": 0, "st": {}, "ld": {}, "vq": {}, "tr": {}}
        self.F0_DG = None

    def f0_emit_jobs(self, n):
        sch = self.sch
        st = self._f0_state
        jobs = self._f0_jobs
        FW = self.F0_FW
        for _ in range(n):
            k = st["k"]
            if k >= len(jobs):
                return
            st["k"] = k + 1
            src_, dst, r0, c0 = jobs[k]
            sl = k % 2
            wtile = self._f0_t["wtile"][k % 4]
            w8 = self._f0_t["w8"][sl]
            wqd = self._f0_t["wqd"][sl]
            wT = self._f0_t["wT"][sl]
            mw = self._f0_t["mw"][sl]
            s2w = self._f0_t["s2w"][sl]
            rrw = self._f0_t["rrw"][sl]
            w = [("g", st["vq"][k - 4])] if k - 4 in st["vq"] else []
            sch.dma("scalar", lambda e, wtile=wtile, src_=src_, r0=r0, c0=c0:
                    e.dma_start(out=wtile[:],
                                in_=src_.ap()[r0:r0 + 128, c0:c0 + FW]),
                    waits=w, inc="dC")
            ldn = sch.now("dC")
            b3 = wtile[:].rearrange("p (b k) -> p b k", k=32)
            sch.emit("vector", lambda e, b3=b3, mw=mw: e.tensor_reduce(
                out=mw[:], in_=b3, axis=AX.X, op=ALU.max,
                apply_absolute_value=True),
                waits=[("dC", ldn), ("g", sch.now("g"))])
            sch.emit("vector", lambda e, mw=mw, s2w=s2w: e.tensor_scalar(
                out=s2w[:], in0=mw[:], scalar1=C224, scalar2=2e-30,
                op0=ALU.mult, op1=ALU.max), gap=1)
            sch.emit("vector", lambda e, s2w=s2w, rrw=rrw: e.reciprocal(
                out=rrw[:], in_=s2w[:]), inc="vf", gap=1)
            vn = sch.now("vf")
            rrb = rrw[:].unsqueeze(2).broadcast_to([128, FW // 32, 32])
            s2b = s2w[:].unsqueeze(2).broadcast_to([128, FW // 32, 32])
            sch.emit("gpsimd", lambda e, b3=b3, rrb=rrb, w8=w8: e.tensor_tensor(
                out=w8[:], in0=b3, in1=rrb, op=ALU.mult), waits=[("vf", vn)])
            sch.emit("gpsimd", lambda e, s2b=s2b, w8=w8, wqd=wqd: e.tensor_tensor(
                out=wqd[:], in0=w8[:], in1=s2b, op=ALU.mult), inc="g")
            st["vq"][k] = sch.now("g")
            pb = self.pb7[:, :].bitcast(BF16)
            w = [("g", st["vq"][k]), ("v", sch.now("v")),
                 ("vf", sch.now("vf"))]

            def ftr(e, wqd=wqd, pb=pb):
                last = None
                for j in range(FW // 128):
                    last = e.transpose(pb[:, j * 128:(j + 1) * 128],
                                       wqd[:, j * 128:(j + 1) * 128],
                                       self.ident[:])
                return last
            sch.emit("tensor", ftr, waits=w, inc="t")
            tn = sch.now("t")
            sch.emit("vector", lambda e, wT=wT, pb=pb: e.tensor_copy(
                out=wT[:], in_=pb[:, 0:FW]), waits=[("t", tn)], inc="vf")
            st["tr"][k] = sch.now("vf")
            out = dst.ap()[c0:c0 + FW, r0:r0 + 128].rearrange(
                "(j p) q -> p j q", p=128)
            sch.dma("scalar", lambda e, out=out, wT=wT: e.dma_start(
                out=out, in_=wT[:].rearrange("p (j q) -> p j q", q=128)),
                waits=[("vf", st["tr"][k])], inc="dC")
            st["st"][k] = sch.now("dC")

    def f0_finish(self):
        sch = self.sch
        while self._f0_state["k"] < len(self._f0_jobs):
            self.f0_emit_jobs(8)
        self.F0_DG = ("dC", sch.now("dC"))
        self.F0_G = sch.now("g")

    def phase0_logits(self):
        sch, W = self.sch, self.W
        lcol = self.lcol

        for sup in range(NSUP):
            tprev = sch.now("t")
            for hc in range(NHC):
                out = self.xfm[:, hc, :]
                in_ = self.d_xbf.ap()[sup * TSUP:(sup + 1) * TSUP,
                                      hc * 128:(hc + 1) * 128]
                w = [("dG", self.XBF_DG)]
                if sup > 0 and hc == 0:
                    w.append(("t", tprev))
                sch.dma("sync", lambda e, o=out, i=in_: e.dma_start_transpose(
                    out=o, in_=i), waits=w)
            dsn = sch.now("dS")

            def fn_t(e, sup=sup):
                last = None
                for t4 in range(4):
                    tt = sup * 4 + t4
                    for hc in range(NHC):
                        last = e.matmul(
                            self.pb7[:, tt:tt + 1],
                            self.xfm[:, hc, t4 * 128:(t4 + 1) * 128],
                            self.wc_bf[:, hc:hc + 1], start=(hc == 0),
                            stop=(hc == NHC - 1))
                return last
            sch.emit("tensor", fn_t,
                     waits=[("dS", dsn),
                            ("v", self.V_SMALL if sup == 0 else sch.now("v"))],
                     inc="t")
        tn = sch.now("t")
        sch.emit("tensor", lambda e: e.matmul(
            self.pb5[:, 0:1], self.ones_row[0:1, 0:128], self.bc_bf[:],
            start=True, stop=True), waits=[("v", self.V_SMALL)], inc="t")
        tbc = sch.now("t")
        sch.emit("vector", lambda e: e.tensor_copy(out=self.bc128[:],
                                                   in_=self.pb5[:, 0:1]),
                 waits=[("t", tbc)])
        sch.emit("vector", lambda e: e.tensor_tensor(
            out=lcol[:], in0=self.pb7[:, 0:NT],
            in1=self.bc128[:].broadcast_to([128, NT]), op=ALU.add),
            waits=[("t", tn)], inc="v", gap=1)
        vl = sch.now("v")
        if self.mkdbg("lcol", [128, NT]):
            sch.dma("sync", lambda e: e.dma_start(out=self.dbg["lcol"].ap(),
                                                  in_=lcol[:]),
                    waits=[("v", vl)])
        if "xfm" in self.debug:
            self.dbg["xfm"] = self.nc.dram_tensor("dbg_xfm", [128, NHC * TSUP],
                                                  BF16, kind="ExternalOutput")
            dsx = sch.now("dS")
            sch.dma("sync", lambda e: e.dma_start(
                out=self.dbg["xfm"].ap(),
                in_=self.xfm[:].rearrange("p c t -> p (c t)")),
                waits=[("dS", dsx)])
        sch.emit("tensor", lambda e: e.transpose(
            self.pb5[0:16, 0:128], lcol[:], self.ident_f[:]),
            waits=[("v", vl)], inc="t")
        t1 = sch.now("t")
        sch.emit("vector", lambda e: e.tensor_copy(out=self.lT[:],
                                                   in_=self.pb5[0:16, 0:128]),
                 waits=[("t", t1)], inc="v")
        l3 = self.lT[:].rearrange("p (b r) -> p b r", r=R)
        sch.emit("vector", lambda e, l3=l3: e.tensor_reduce(
            out=self.lm8[:], in_=l3, axis=AX.X, op=ALU.max), gap=1)
        sch.emit("vector", lambda e, l3=l3: e.tensor_tensor(
            out=l3, in0=l3,
            in1=self.lm8[:].unsqueeze(2).broadcast_to([16, 8, R]),
            op=ALU.subtract), inc="v", gap=1)
        sch.emit("scalar", lambda e: e.activation(
            out=self.lT[:], in_=self.lT[:], func=ACT.Exp), waits=W("v"),
            inc="s")
        sch.emit("vector", lambda e, l3=l3: e.tensor_reduce(
            out=self.ls8[:], in_=l3, axis=AX.X, op=ALU.add), waits=W("s"))
        sch.emit("vector", lambda e: e.reciprocal(out=self.ls8[:],
                                                  in_=self.ls8[:]), gap=1)
        sch.emit("vector", lambda e, l3=l3: e.tensor_tensor(
            out=l3, in0=l3,
            in1=self.ls8[:].unsqueeze(2).broadcast_to([16, 8, R]),
            op=ALU.mult), inc="v", gap=1)
        vw = sch.now("v")
        sch.emit("tensor", lambda e: e.transpose(
            self.pb5[0:128, 0:16], self.lT[:], self.ident_f[0:16, 0:16]),
            waits=[("v", vw)], inc="t")
        t2 = sch.now("t")
        sch.emit("vector", lambda e: e.tensor_copy(out=self.wcol[:],
                                                   in_=self.pb5[0:128, 0:16]),
                 waits=[("t", t2)], inc="v")
        self.V_WCOL = sch.now("v")
        if self.mkdbg("wrow", [128, NT]):
            sch.dma("sync", lambda e: e.dma_start(out=self.dbg["wrow"].ap(),
                                                  in_=self.wcol[:]),
                    waits=[("v", self.V_WCOL)])

    def phase0b(self, sc):
        sch, W, nc = self.sch, self.W, self.nc
        SB = lambda n, sh, dt: sc.enter_context(nc.sbuf_tensor(n, sh, dt))
        self.xt = [SB("xt_a0", [128, H], F32), SB("xt_b0", [128, H], F32)]
        gmat = SB("gmat", [128, NT * 128], BF16)
        sch.dma("sync", lambda e: e.dma_start(out=gmat[:],
                                              in_=self.ch["c_g"].ap()))
        xw_bf = SB("xw_bf", [128, H], BF16)
        entries_b = SB("entries_b", [128, H], BF16)
        entries_fm = SB("entries_fm", [128, NHC, 128], BF16)
        wkb = SB("wkb", [128, NHC, HD], BF16)
        wvb = SB("wvb", [128, NHC, HD], BF16)
        ck_sb = SB("ck_sb", [128, C], F32)
        sq_ck = SB("sq_ck", [128, C], BF16)
        rs8 = SB("rs8", [8, 128], F32)
        rs8b = SB("rs8b", [8, 128], BF16)
        ckb2 = SB("ckb2", [128, C], BF16)
        cksw = SB("cksw", [64, C], BF16)
        rp1 = SB("rp1", [64, 2 * C], BF16)

        WKVm = self.WKVMIR_DG
        sch.dma("sync", lambda e: e.dma_start(
            out=wkb[:], in_=self.d_wkb.ap().rearrange("(c p) d -> p c d", p=128)),
            waits=[("dG", WKVm)])
        sch.dma("sync", lambda e: e.dma_start(
            out=wvb[:], in_=self.d_wvb.ap().rearrange("(c p) d -> p c d", p=128)),
            waits=[("dG", WKVm)])
        WKV = sch.now("dS")
        if "wkb" in self.debug:
            self.dbg["wkb"] = self.nc.dram_tensor("dbg_wkb", [128, NHC * HD],
                                                  BF16, kind="ExternalOutput")
            sch.dma("sync", lambda e: e.dma_start(
                out=self.dbg["wkb"].ap(),
                in_=wkb[:].rearrange("p c d -> p (c d)")), waits=[("dS", WKV)])

        for tt in range(NT):
            xt = self.xt[tt % 2]
            wv = [("v", sch.now("v"))] if tt >= 2 else []
            sch.dma("sync", lambda e, xt=xt, tt=tt: e.dma_start(
                out=xt[:], in_=self.i_x.ap()[tt * 128:(tt + 1) * 128, :]),
                waits=wv)
            dsn = sch.now("dS")
            sch.emit("vector", lambda e, xt=xt, tt=tt: e.tensor_scalar(
                out=xw_bf[:], in0=xt[:], scalar1=self.wcol[:, tt:tt + 1],
                scalar2=None, op0=ALU.mult),
                waits=[("dS", dsn), ("v", self.V_WCOL), ("t", sch.now("t"))],
                inc="v", gap=(2 if tt == 0 else 0))
            vn = sch.now("v")

            def fn(e, tt=tt):
                last = None
                for j in range(4):
                    last = e.matmul(self.ps_big[:, j, :],
                                    gmat[:, tt * 128:(tt + 1) * 128],
                                    xw_bf[:, j * 512:(j + 1) * 512],
                                    start=(tt == 0), stop=(tt == NT - 1))
                return last
            sch.emit("tensor", fn, waits=[("v", vn)], inc="t")
        t_ent = sch.now("t")
        sch.emit("vector", lambda e: e.tensor_copy(out=entries_b[:],
                                                   in_=self.ps_big[:, :, :]),
                 waits=[("t", t_ent)], inc="v")
        ven = sch.now("v")
        if self.mkdbg("entries", [C, H]):
            ent_f = SB("ent_f", [128, H], F32)
            sch.emit("vector", lambda e: e.tensor_copy(out=ent_f[:],
                                                       in_=self.ps_big[:, :, :]),
                     waits=[("t", t_ent)], inc="v")
            sch.dma("sync", lambda e: e.dma_start(out=self.dbg["entries"].ap(),
                                                  in_=ent_f[:]), waits=W("v"))
        for hc in range(NHC):
            w = [("v", ven)]
            if hc >= 2:
                w = [("v", sch.now("v"))]
            sch.emit("tensor", lambda e, hc=hc: e.transpose(
                self.pt0[:, (hc % 2) * 512:(hc % 2) * 512 + 128],
                entries_b[:, hc * 128:(hc + 1) * 128], self.ident[:]),
                waits=w, inc="t")
            tn = sch.now("t")
            sch.emit("vector", lambda e, hc=hc: e.tensor_copy(
                out=entries_fm[:, hc, :],
                in_=self.pt0[:, (hc % 2) * 512:(hc % 2) * 512 + 128]),
                waits=[("t", tn)], inc="v")
        vfm = sch.now("v")
        if "efm" in self.debug:
            self.dbg["efm"] = self.nc.dram_tensor("dbg_efm", [128, NHC * 128],
                                                  BF16, kind="ExternalOutput")
            sch.dma("sync", lambda e: e.dma_start(
                out=self.dbg["efm"].ap(),
                in_=entries_fm[:].rearrange("p c t -> p (c t)")),
                waits=[("v", vfm)])

        def fn_ck(e):
            last = None
            for hc in range(NHC):
                last = e.matmul(self.pb4[:, 0:128], wkb[:, hc, :],
                                entries_fm[:, hc, :], start=(hc == 0),
                                stop=(hc == NHC - 1))
            return last
        sch.emit("tensor", fn_ck, waits=[("v", vfm), ("dS", WKV)], inc="t")
        tck = sch.now("t")

        def fn_cv(e):
            for hc in range(NHC):
                e.matmul(self.pb5[:, 0:128], entries_fm[:, hc, :], wvb[:, hc, :],
                         start=(hc == 0), stop=False)
            return e.matmul(self.pb5[:, 0:128], self.ones_row[0:1, 0:128],
                            self.rows_b[0:1, 2 * H:2 * H + HD], start=False,
                            stop=True)
        sch.emit("tensor", fn_cv, waits=[("v", vfm), ("dS", WKV)], inc="t")
        tcv = sch.now("t")
        sch.emit("vector", lambda e: e.tensor_copy(out=self.cvb[:],
                                                   in_=self.pb5[:, 0:128]),
                 waits=[("t", tcv)], inc="v")
        if self.mkdbg("cv", [C, HD]):
            cv_f = SB("cv_f", [128, HD], F32)
            sch.emit("vector", lambda e: e.tensor_copy(out=cv_f[:],
                                                       in_=self.pb5[:, 0:128]),
                     inc="v")
            sch.dma("sync", lambda e: e.dma_start(out=self.dbg["cv"].ap(),
                                                  in_=cv_f[:]), waits=W("v"))

        sch.emit("vector", lambda e: e.tensor_scalar(
            out=ck_sb[:], in0=self.pb4[:, 0:128], scalar1=self.bk_c[:],
            scalar2=None, op0=ALU.add), waits=[("t", tck)], inc="v")
        if self.mkdbg("cklin", [HD, C]):
            sch.dma("sync", lambda e: e.dma_start(out=self.dbg["cklin"].ap(),
                                                  in_=ck_sb[:]), waits=W("v"))
        sch.emit("scalar", lambda e: e.activation(out=sq_ck[:], in_=ck_sb[:],
                                                  func=ACT.Square),
                 waits=W("v"), inc="s")
        sch.emit("tensor", lambda e: e.matmul(
            self.pb4[0:8, 128:256], self.emat[:, 0:8], sq_ck[:], start=True,
            stop=True), waits=W("s"), inc="t")
        sch.emit("vector", lambda e: e.tensor_scalar(
            out=rs8[0:8, :], in0=self.pb4[0:8, 128:256], scalar1=C128,
            scalar2=EPS_RMS, op0=ALU.mult, op1=ALU.add), waits=W("t"), inc="v")
        sch.emit("scalar", lambda e: e.activation(out=rs8[0:8, :],
                                                  in_=rs8[0:8, :], func=ACT.Sqrt),
                 waits=W("v"), inc="s")
        sch.emit("vector", lambda e: e.reciprocal(out=rs8[0:8, :],
                                                  in_=rs8[0:8, :]), waits=W("s"))
        sch.emit("vector", lambda e: e.tensor_copy(out=rs8b[0:8, :],
                                                   in_=rs8[0:8, :]), inc="v",
                 gap=1)
        sch.emit("tensor", lambda e: e.matmul(
            self.pb5[:, 128:256], self.fmat[0:8, 0:128], rs8b[0:8, :],
            start=True, stop=True), waits=W("v"), inc="t")
        sch.emit("vector", lambda e: e.scalar_tensor_tensor(
            out=ck_sb[:], in0=ck_sb[:], scalar=self.knw_c[:],
            in1=self.pb5[:, 128:256], op0=ALU.mult, op1=ALU.mult),
            waits=W("t"), inc="v")
        sch.emit("vector", lambda e: e.tensor_copy(out=ckb2[:], in_=ck_sb[:]),
                 inc="v", gap=1)
        sch.emit("tensor", lambda e: e.matmul(
            self.pb7[0:64, 0:128], self.swapm[:], ckb2[0:64, :], start=True,
            stop=True), waits=W("v"), inc="t")
        sch.emit("vector", lambda e: e.tensor_copy(
            out=cksw[0:64, :], in_=self.pb7[0:64, 0:128]), waits=W("t"))
        sch.emit("vector", lambda e: e.tensor_tensor(
            out=rp1[0:64, 0:C], in0=cksw[0:64, :], in1=self.sinsc[:],
            op=ALU.mult), gap=1)
        sch.emit("vector", lambda e: e.tensor_tensor(
            out=rp1[0:64, C:2 * C], in0=ckb2[0:64, :], in1=self.cosdc[:],
            op=ALU.mult))
        sch.emit("vector", lambda e: e.tensor_tensor(
            out=ckb2[0:64, :], in0=rp1[0:64, C:2 * C], in1=rp1[0:64, 0:C],
            op=ALU.add), inc="v", gap=1)
        vck = sch.now("v")
        if self.mkdbg("ck", [HD, C]):
            ck_f = SB("ck_f", [128, C], F32)
            sch.emit("vector", lambda e: e.tensor_copy(out=ck_f[:], in_=ckb2[:]),
                     waits=[("v", vck)], inc="v")
            sch.dma("sync", lambda e: e.dma_start(out=self.dbg["ck"].ap(),
                                                  in_=ck_f[:]), waits=W("v"))

        def fn_scb(e):
            last = None
            for h in range(NH):
                e.tensor_copy(out=self.scb[:, h, 0:1],
                              in_=self.sinkb[:, h:h + 1])
                last = e.tensor_copy(out=self.scb[:, h, 1:129], in_=ckb2[:])
            return last
        sch.emit("vector", fn_scb, waits=[("v", vck)], inc="v")

    def phase_groups(self, sc):
        sch, W, nc = self.sch, self.W, self.nc
        SB = lambda n, sh, dt: sc.enter_context(nc.sbuf_tensor(n, sh, dt))
        wq_g = SB("wq_g", [128, NHC, 512], BF16)
        wlk_g = SB("wlk_g", [128, NHC, 512], BF16)
        wlv_g = SB("wlv_g", [128, NHC, 512], BF16)
        qn_sb = SB("qn_sb", [128, HPG, TSUP], F32)
        kn_sb = SB("kn_sb", [128, HPG, TSUP], F32)
        sq_bf = SB("sq_bf", [128, 2 * HPG, TSUP], BF16)
        q_sup = SB("q_sup", [128, HPG, TSUP], BF16)
        lk_sup = [SB("lk_supA", [128, HPG, TSUP], BF16),
                  SB("lk_supB", [128, HPG, TSUP], BF16)]
        lv_sup = [SB("lv_supA", [128, 4, HPG * 128], BF16),
                  SB("lv_supB", [128, 4, HPG * 128], BF16)]
        rs8 = SB("rs8g", [8, TSUP], F32)
        rs8b = SB("rs8gb", [8, TSUP], BF16)
        rope_sc = SB("rope_sc", [64, 4, TSUP], BF16)
        es_c = SB("es_c", [128, 4, 129], BF16)
        es_l = SB("es_l", [128, 4, 256], BF16)
        nm_c = SB("nm_c", [128, 4], F32)
        nm_l = SB("nm_l", [128, 4], F32)
        dn_c = SB("dn_c", [128, 4], F32)
        dn_l = SB("dn_l", [128, 4], F32)
        rc_c = SB("rc_c", [128, 4], F32)
        rc_l = SB("rc_l", [128, 4], F32)
        p_bf = SB("p_bf", [128, 4, 385], BF16)
        pT_sb = SB("pT_sb", [128, 4, 512], BF16)
        ctx_st = SB("ctx_st", [128, 2, HPG, 128], BF16)

        self.ctx_ds = []

        for g in range(NG):
            gsl = slice(g * 512, (g + 1) * 512)
            wload = [("dG", self.WQKV_DG), ("t", sch.now("t")),
                     ("v", sch.now("v"))]
            sch.dma("sync", lambda e, gsl=gsl: e.dma_start(
                out=wq_g[:], in_=self.d_wqb.ap()[:, gsl].rearrange(
                    "(c p) m -> p c m", p=128)), waits=wload)
            sch.dma("sync", lambda e, gsl=gsl: e.dma_start(
                out=wlk_g[:], in_=self.d_wlkb.ap()[:, gsl].rearrange(
                    "(c p) m -> p c m", p=128)))
            sch.dma("sync", lambda e, gsl=gsl: e.dma_start(
                out=wlv_g[:], in_=self.d_wlvb.ap()[:, gsl].rearrange(
                    "(c p) m -> p c m", p=128)))
            WG = sch.now("dS")
            sch.emit("vector", lambda e: e.memset(
                lk_sup[1][:, :, TSUP - 128:TSUP], 0.0),
                waits=[("t", sch.now("t"))])
            sch.emit("vector", lambda e: e.memset(lv_sup[1][:, 3, :], 0.0),
                     inc="v")

            for sup in range(NSUP):
                par = sup % 2
                tprev = sch.now("t")
                for hc in range(NHC):
                    out = self.xfm[:, hc, :]
                    in_ = self.d_xbf.ap()[sup * TSUP:(sup + 1) * TSUP,
                                          hc * 128:(hc + 1) * 128]
                    sch.dma("sync", lambda e, o=out, i=in_:
                            e.dma_start_transpose(out=o, in_=i),
                            waits=[("t", tprev)] if hc == 0 else ())
                XFM = sch.now("dS")

                vprev = sch.now("v")

                def fn_q(e):
                    last = None
                    for h in range(HPG):
                        for hc in range(NHC):
                            last = e.matmul(
                                self.ps_big[:, h, :],
                                wq_g[:, hc, h * 128:(h + 1) * 128],
                                self.xfm[:, hc, :], start=(hc == 0),
                                stop=(hc == NHC - 1))
                    return last
                sch.emit("tensor", fn_q,
                         waits=[("dS", max(XFM, WG)), ("v", vprev),
                                ("s", sch.now("s"))], inc="t")
                tq = sch.now("t")
                for h in range(HPG):
                    hg = g * HPG + h
                    sch.emit("vector", lambda e, h=h, hg=hg: e.tensor_scalar(
                        out=qn_sb[:, h, :], in0=self.ps_big[:, h, :],
                        scalar1=self.bq_c[:, hg:hg + 1], scalar2=None,
                        op0=ALU.add), waits=[("t", tq)], inc="v")
                    sch.emit("scalar", lambda e, h=h: e.activation(
                        out=sq_bf[:, h, :], in_=qn_sb[:, h, :],
                        func=ACT.Square), waits=W("v"), inc="s")
                vq_evac = sch.now("v")

                def fn_lk(e):
                    last = None
                    for h in range(HPG):
                        for hc in range(NHC):
                            last = e.matmul(
                                self.ps_big[:, h, :],
                                wlk_g[:, hc, h * 128:(h + 1) * 128],
                                self.xfm[:, hc, :], start=(hc == 0),
                                stop=(hc == NHC - 1))
                    return last
                sch.emit("tensor", fn_lk, waits=[("v", vq_evac)], inc="t")
                tlk = sch.now("t")
                for h in range(HPG):
                    hg = g * HPG + h
                    sch.emit("vector", lambda e, h=h, hg=hg: e.tensor_scalar(
                        out=kn_sb[:, h, :], in0=self.ps_big[:, h, :],
                        scalar1=self.blk_c[:, hg:hg + 1], scalar2=None,
                        op0=ALU.add), waits=[("t", tlk)], inc="v")
                    sch.emit("scalar", lambda e, h=h: e.activation(
                        out=sq_bf[:, HPG + h, :], in_=kn_sb[:, h, :],
                        func=ACT.Square), waits=W("v"), inc="s")
                vlk_evac = sch.now("v")
                sq_all = sch.now("s")

                def fn_e(e):
                    last = None
                    for r in range(8):
                        last = e.matmul(self.pb4[0:8, 0:TSUP],
                                        self.emat[:, r * 8:(r + 1) * 8],
                                        sq_bf[:, r, :], start=(r == 0),
                                        stop=(r == 7))
                    return last
                sch.emit("tensor", fn_e,
                         waits=[("s", sq_all), ("v", sch.now("v"))], inc="t")
                te = sch.now("t")

                def fn_lv(e, g=g):
                    last = None
                    for t4 in range(4):
                        for hc in range(NHC):
                            e.matmul(self.ps_big[:, t4, :],
                                     self.xfm[:, hc, t4 * 128:(t4 + 1) * 128],
                                     wlv_g[:, hc, :], start=(hc == 0),
                                     stop=False)
                        last = e.matmul(self.ps_big[:, t4, :],
                                        self.ones_row[0:1, 0:128],
                                        self.rows_b[0:1, g * 512:(g + 1) * 512],
                                        start=False, stop=True)
                    return last
                sch.emit("tensor", fn_lv, waits=[("v", vlk_evac)], inc="t")
                tlv = sch.now("t")

                sch.emit("vector", lambda e: e.tensor_scalar(
                    out=rs8[0:8, :], in0=self.pb4[0:8, 0:TSUP], scalar1=C128,
                    scalar2=EPS_RMS, op0=ALU.mult, op1=ALU.add),
                    waits=[("t", te)], inc="v")
                sch.emit("scalar", lambda e: e.activation(
                    out=rs8[0:8, :], in_=rs8[0:8, :], func=ACT.Sqrt),
                    waits=W("v"), inc="s")
                sch.emit("vector", lambda e: e.reciprocal(
                    out=rs8[0:8, :], in_=rs8[0:8, :]), waits=W("s"))
                sch.emit("vector", lambda e: e.tensor_copy(
                    out=rs8b[0:8, :], in_=rs8[0:8, :]), inc="v", gap=1)
                vrs = sch.now("v")

                for t4 in range(4):
                    sch.emit("vector", lambda e, t4=t4, par=par: e.tensor_copy(
                        out=lv_sup[par][:, t4, :], in_=self.ps_big[:, t4, :]),
                        waits=[("t", tlv)], inc="v")
                vlv = sch.now("v")

                cs_sl = self.cosd[:, sup * TSUP:(sup + 1) * TSUP]
                sn_sl = self.sins[:, sup * TSUP:(sup + 1) * TSUP]
                for r in range(8):
                    bank = [self.pb5, self.pb7][r % 2]
                    is_q = r < HPG
                    h = r % HPG
                    dst = q_sup if is_q else lk_sup[par]
                    src = qn_sb if is_q else kn_sb
                    wcol_ = self.qnw_c if is_q else self.knw_c
                    sch.emit("tensor", lambda e, r=r, bank=bank: e.matmul(
                        bank[:, 0:TSUP], self.fmat[0:8, r * 128:(r + 1) * 128],
                        rs8b[0:8, :], start=True, stop=True),
                        waits=[("v", sch.now("v"))], inc="t")
                    tb = sch.now("t")
                    sch.emit("vector", lambda e, h=h, dst=dst, src=src,
                             wcol_=wcol_, bank=bank: e.scalar_tensor_tensor(
                        out=dst[:, h, :], in0=src[:, h, :], scalar=wcol_[:],
                        in1=bank[:, 0:TSUP], op0=ALU.mult, op1=ALU.mult),
                        waits=[("t", tb)], inc="v")
                    vstt = sch.now("v")
                    obank = [self.pb5, self.pb7][(r + 1) % 2]
                    sch.emit("tensor", lambda e, h=h, dst=dst, obank=obank:
                             e.matmul(obank[0:64, 0:TSUP], self.swapm[:],
                                      dst[0:64, h, :], start=True, stop=True),
                             waits=[("v", vstt)], inc="t")
                    tsw = sch.now("t")
                    qsh = rope_sc[0:64, 2, :]
                    sa = rope_sc[0:64, 0, :]
                    sb_ = rope_sc[0:64, 1, :]
                    sch.emit("vector", lambda e, qsh=qsh, obank=obank:
                             e.tensor_copy(out=qsh, in_=obank[0:64, 0:TSUP]),
                             waits=[("t", tsw)], inc="v")
                    sch.emit("vector", lambda e, qsh=qsh, sa=sa, sn_sl=sn_sl:
                             e.tensor_tensor(out=sa, in0=qsh, in1=sn_sl,
                                             op=ALU.mult), gap=1)
                    sch.emit("vector", lambda e, h=h, dst=dst, sb_=sb_,
                             cs_sl=cs_sl: e.tensor_tensor(
                        out=sb_, in0=dst[0:64, h, :], in1=cs_sl, op=ALU.mult))
                    sch.emit("vector", lambda e, h=h, dst=dst, sa=sa, sb_=sb_:
                             e.tensor_tensor(out=dst[0:64, h, :], in0=sb_,
                                             in1=sa, op=ALU.add), inc="v",
                             gap=1)
                vrope = sch.now("v")

                if self.mkdbg("q", [H, S]) and g == 0:
                    pass

                units = [(t4, h) for t4 in range(4) for h in range(HPG)]
                NU = len(units)
                st_mask_v = {}
                st_ts_v = {}
                st_ptev_v = {}
                st_ctx_v = {}
                st_qk_t = {}
                st_tr_t = {}
                st_pv_t = {}
                st_exp_s = {}
                ctx_dma_ds = {}

                def sc_bank(u):
                    return [self.pb4[:, :], self.pb5[:, :],
                            self.ps_big[:, 2, :], self.ps_big[:, 3, :]][u % 4]

                def em_qk(u):
                    t4, h = units[u]
                    hg = g * HPG + h
                    bank = sc_bank(u)
                    tt = sup * 4 + t4
                    if t4 > 0:
                        lkp = lk_sup[par][:, h, (t4 - 1) * 128:t4 * 128]
                    else:
                        lkp = lk_sup[1 - par][:, h, 3 * 128:4 * 128]
                    lkc = lk_sup[par][:, h, t4 * 128:(t4 + 1) * 128]
                    qs = q_sup[:, h, t4 * 128:(t4 + 1) * 128]
                    if u - 4 >= 0:
                        w = [("s", st_exp_s[u - 4])]
                    else:
                        w = [("v", vrope)]
                    if u == 0:
                        w.append(("v", vlv))

                    def fn(e, qs=qs, bank=bank, hg=hg, lkp=lkp, lkc=lkc):
                        e.matmul(bank[:, 0:129], qs, self.scb[:, hg, :],
                                 start=True, stop=True)
                        e.matmul(bank[:, 129:257], qs, lkp, start=True, stop=True)
                        return e.matmul(bank[:, 257:385], qs, lkc, start=True,
                                        stop=True)
                    sch.emit("tensor", fn, waits=w, inc="t")
                    st_qk_t[u] = sch.now("t")

                def em_mask(u):
                    t4, h = units[u]
                    tt = sup * 4 + t4
                    sl = u % 4
                    bank = sc_bank(u)
                    lm = self.lmask[:, 0:256] if tt == 0 else self.lmask[:, 256:512]
                    cm = self.cmask[:, tt * 129:(tt + 1) * 129]
                    wm = [("t", st_qk_t[u])]
                    if u - 4 in st_exp_s:
                        wm.append(("s", st_exp_s[u - 4]))
                    sch.emit("vector", lambda e, bank=bank, cm=cm:
                             e.scalar_tensor_tensor(
                        out=bank[:, 0:129], in0=bank[:, 0:129], scalar=SCALE,
                        in1=cm, op0=ALU.mult, op1=ALU.add),
                        waits=wm)
                    sch.emit("vector", lambda e, bank=bank, lm=lm:
                             e.scalar_tensor_tensor(
                        out=bank[:, 129:385], in0=bank[:, 129:385], scalar=SCALE,
                        in1=lm, op0=ALU.mult, op1=ALU.add))
                    sch.emit("vector", lambda e, bank=bank, sl=sl: e.tensor_reduce(
                        out=nm_c[:, sl:sl + 1], in_=bank[:, 0:129], axis=AX.X,
                        op=ALU.max, negate=True))
                    sch.emit("vector", lambda e, bank=bank, sl=sl: e.tensor_reduce(
                        out=nm_l[:, sl:sl + 1], in_=bank[:, 129:385], axis=AX.X,
                        op=ALU.max, negate=True), inc="v")
                    st_mask_v[u] = sch.now("v")

                def em_exp(u):
                    sl = u % 4

                    bank = sc_bank(u)

                    def fn(e, sl=sl, bank=bank):
                        e.activation(out=es_c[:, sl, :], in_=bank[:, 0:129],
                                     func=ACT.Exp, bias=nm_c[:, sl:sl + 1],
                                     scale=1.0, accum_out=dn_c[:, sl:sl + 1])
                        return e.activation(out=es_l[:, sl, :], in_=bank[:, 129:385],
                                            func=ACT.Exp, bias=nm_l[:, sl:sl + 1],
                                            scale=1.0,
                                            accum_out=dn_l[:, sl:sl + 1])
                    sch.emit("scalar", fn, waits=[("v", st_mask_v[u])], inc="s")
                    st_exp_s[u] = sch.now("s")

                def em_recips(u):
                    sl = u % 4
                    sch.emit("vector", lambda e, sl=sl: e.reciprocal(
                        out=rc_c[:, sl:sl + 1], in_=dn_c[:, sl:sl + 1]),
                        waits=[("s", st_exp_s[u])])
                    sch.emit("vector", lambda e, sl=sl: e.reciprocal(
                        out=rc_l[:, sl:sl + 1], in_=dn_l[:, sl:sl + 1]))

                def em_ts(u):
                    sl = u % 4
                    sch.emit("vector", lambda e, sl=sl: e.tensor_scalar(
                        out=p_bf[:, sl, 0:129], in0=es_c[:, sl, :],
                        scalar1=rc_c[:, sl:sl + 1], scalar2=0.5, op0=ALU.mult,
                        op1=ALU.mult))
                    sch.emit("vector", lambda e, sl=sl: e.tensor_scalar(
                        out=p_bf[:, sl, 129:385], in0=es_l[:, sl, :],
                        scalar1=rc_l[:, sl:sl + 1], scalar2=0.5, op0=ALU.mult,
                        op1=ALU.mult), inc="v")
                    st_ts_v[u] = sch.now("v")

                def em_tr(u):
                    sl = u % 4
                    rg = (u % 2) * 512
                    w = [("v", st_ts_v[u])]
                    if u - 2 >= 0:
                        w.append(("v", st_ptev_v[u - 2]))

                    def fn(e, sl=sl, rg=rg):
                        e.transpose(self.pt0[:, rg:rg + 128], p_bf[:, sl, 1:129],
                                    self.ident[:])
                        e.transpose(self.pt0[:, rg + 128:rg + 256],
                                    p_bf[:, sl, 129:257], self.ident[:])
                        e.transpose(self.pt0[:, rg + 256:rg + 384],
                                    p_bf[:, sl, 257:385], self.ident[:])
                        return e.transpose(self.pt0[0:1, rg + 384:rg + 512],
                                           p_bf[:, sl, 0:1], self.ident[:])
                    sch.emit("tensor", fn, waits=w, inc="t")
                    st_tr_t[u] = sch.now("t")

                def em_ptev(u):
                    sl = u % 4
                    rg = (u % 2) * 512

                    def fn(e, sl=sl, rg=rg):
                        e.tensor_copy(out=pT_sb[:, sl, 0:384],
                                      in_=self.pt0[:, rg:rg + 384])
                        return e.tensor_copy(out=pT_sb[0:1, sl, 384:512],
                                             in_=self.pt0[0:1, rg + 384:rg + 512])
                    sch.emit("vector", fn, waits=[("t", st_tr_t[u])], inc="v")
                    st_ptev_v[u] = sch.now("v")

                def em_pv(u):
                    t4, h = units[u]
                    hg = g * HPG + h
                    sl = u % 4
                    cb = u % 2
                    if t4 > 0:
                        lvp = lv_sup[par][:, t4 - 1, h * 128:(h + 1) * 128]
                    else:
                        lvp = lv_sup[1 - par][:, 3, h * 128:(h + 1) * 128]
                    lvc = lv_sup[par][:, t4, h * 128:(h + 1) * 128]
                    w = [("v", st_ptev_v[u])]
                    if u - 2 >= 0:
                        w.append(("v", st_ctx_v[u - 2]))

                    def fn(e, sl=sl, cb=cb, hg=hg, lvp=lvp, lvc=lvc):
                        e.matmul(self.ps_big[:, cb, 0:128], self.cvb[:],
                                 pT_sb[:, sl, 0:128], start=True, stop=False)
                        e.matmul(self.ps_big[:, cb, 0:128],
                                 self.rows_b[0:1, 3 * H + hg * 128:3 * H + (hg + 1) * 128],
                                 pT_sb[0:1, sl, 384:512], start=False, stop=False)
                        e.matmul(self.ps_big[:, cb, 0:128], lvp,
                                 pT_sb[:, sl, 128:256], start=False, stop=False)
                        return e.matmul(self.ps_big[:, cb, 0:128], lvc,
                                        pT_sb[:, sl, 256:384], start=False,
                                        stop=True)
                    sch.emit("tensor", fn, waits=w, inc="t")
                    st_pv_t[u] = sch.now("t")

                def em_ctx(u):
                    t4, h = units[u]
                    sl = u % 2
                    w = [("t", st_pv_t[u])]
                    if u - 2 * HPG >= 0:
                        w.append(("dS", ctx_dma_ds[u - 2 * HPG]))

                    sch.emit("vector", lambda e, t4=t4, h=h, sl=sl: e.tensor_copy(
                        out=ctx_st[:, t4 % 2, h, :], in_=self.ps_big[:, sl, 0:128]),
                        waits=w, inc="v")
                    st_ctx_v[u] = sch.now("v")
                    if h == HPG - 1:
                        tt = sup * 4 + t4
                        out = self.d_ctx.ap()[g * 512:(g + 1) * 512,
                                              tt * 128:(tt + 1) * 128].rearrange(
                                                  "(h p) t -> p h t", p=128)
                        sch.dma("sync", lambda e, out=out, t4=t4: e.dma_start(
                            out=out, in_=ctx_st[:, t4 % 2, :, :]),
                            waits=[("v", st_ctx_v[u])])
                        for hh in range(HPG):
                            ctx_dma_ds[u - hh] = sch.now("dS")
                        self.ctx_ds.append(sch.now("dS"))

                for u in range(NU + 1):
                    if u < NU:
                        em_qk(u)
                        em_mask(u)
                        em_exp(u)
                        em_recips(u)
                    if 0 <= u - 1:
                        em_ts(u - 1)
                        em_tr(u - 1)
                        em_ptev(u - 1)
                        em_pv(u - 1)
                        em_ctx(u - 1)
                    if "ffn" in self.phases:
                        self.f0_emit_jobs(1)

    def phase_a4(self, sc):
        sch, W, nc = self.sch, self.W, self.nc
        SB = lambda n, sh, dt: sc.enter_context(nc.sbuf_tensor(n, sh, dt))
        self.xt = [SB("xt_a4", [128, H], F32), SB("xt_b4", [128, H], F32)]
        wo_sb = SB("wo_sb", [128, NHC, H], BF16)
        ctx_t = SB("ctx_t", [128, 2, NHC, 128], BF16)
        r1_sb = SB("r1_sb", [128, H], F32)
        sq_t = SB("sq_t", [128, H], F32)
        mq = SB("mq", [128, 64], F32)
        s2q = SB("s2q", [128, 64], F32)
        rrq = SB("rrq", [128, 64], F32)
        xq8 = SB("xq8", [128, H], FP8)
        xq_bf = SB("xq_bf", [128, H], BF16)

        sch.dma("sync", lambda e: e.dma_start(
            out=wo_sb[:], in_=self.d_wob.ap().rearrange("(c p) m -> p c m", p=128)),
            waits=[("dG", self.WO_DG), ("t", sch.now("t")), ("v", sch.now("v"))])
        WO = sch.now("dS")
        ctx_all = self.ctx_ds[-1] if self.ctx_ds else 0

        dbg_x1 = self.mkdbg("x1", [S, H])

        for tt in range(NT):
            xt = self.xt[tt % 2]
            par = tt % 2
            inctx = self.d_ctx.ap()[:, tt * 128:(tt + 1) * 128].rearrange(
                "(c p) t -> p c t", p=128)
            wv = [("t", sch.now("t"))] if tt >= 2 else [("dS", ctx_all)]
            sch.dma("sync", lambda e, inctx=inctx, par=par: e.dma_start(
                out=ctx_t[:, par, :, :], in_=inctx), waits=wv)
            sch.dma("sync", lambda e, xt=xt, tt=tt: e.dma_start(
                out=xt[:], in_=self.i_x.ap()[tt * 128:(tt + 1) * 128, :]),
                waits=[("v", sch.now("v"))] if tt >= 2 else ())
            dsn = sch.now("dS")

            def fn(e, par=par):
                for hc in range(NHC):
                    for j in range(4):
                        e.matmul(self.ps_big[:, j, :], ctx_t[:, par, hc, :],
                                 wo_sb[:, hc, j * 512:(j + 1) * 512],
                                 start=(hc == 0), stop=False)
                last = None
                for j in range(4):
                    last = e.matmul(self.ps_big[:, j, :],
                                    self.ones_row[0:1, 0:128],
                                    self.rows_b[0:1, H + j * 512:H + (j + 1) * 512],
                                    start=False, stop=True)
                return last
            sch.emit("tensor", fn,
                     waits=[("dS", max(dsn, WO)), ("v", sch.now("v"))],
                     inc="t")
            tn = sch.now("t")

            sch.emit("vector", lambda e, xt=xt: e.tensor_tensor(
                out=r1_sb[:], in0=self.ps_big[:, :, :], in1=xt[:],
                op=ALU.add), waits=[("t", tn)], inc="v")
            sch.emit("vector", lambda e: e.tensor_reduce(
                out=self.cols[:, 0:1], in_=r1_sb[:], axis=AX.X, op=ALU.add),
                inc="v", gap=1)
            sch.emit("scalar", lambda e: e.activation(
                out=sq_t[:], in_=r1_sb[:], func=ACT.Square,
                accum_out=self.cols[:, 1:2]), waits=W("v"), inc="s")
            sn = sch.now("s")
            sch.emit("vector", lambda e: e.tensor_scalar(
                out=self.cols[:, 2:3], in0=self.cols[:, 0:1], scalar1=CH,
                scalar2=None, op0=ALU.mult), waits=[("s", sn)], gap=1)
            sch.emit("vector", lambda e: e.tensor_scalar(
                out=self.cols[:, 4:5], in0=self.cols[:, 1:2], scalar1=CH,
                scalar2=None, op0=ALU.mult))
            sch.emit("vector", lambda e: e.tensor_tensor(
                out=self.cols[:, 3:4], in0=self.cols[:, 2:3],
                in1=self.cols[:, 2:3], op=ALU.mult))
            sch.emit("vector", lambda e: e.tensor_tensor(
                out=self.cols[:, 5:6], in0=self.cols[:, 4:5],
                in1=self.cols[:, 3:4], op=ALU.subtract), gap=1)
            sch.emit("vector", lambda e: e.tensor_scalar(
                out=self.cols[:, 6:7], in0=self.cols[:, 5:6], scalar1=EPS_LN,
                scalar2=None, op0=ALU.add), inc="v", gap=1)
            sch.emit("scalar", lambda e: e.activation(
                out=self.cols[:, 6:7], in_=self.cols[:, 6:7], func=ACT.Sqrt),
                waits=W("v"), inc="s")
            sn = sch.now("s")
            sch.emit("vector", lambda e: e.reciprocal(
                out=self.cols[:, 7:8], in_=self.cols[:, 6:7]), waits=[("s", sn)])
            self.gap(3)
            sch.emit("vector", lambda e: e.tensor_scalar(
                out=self.x1_t[:], in0=r1_sb[:], scalar1=self.cols[:, 2:3],
                scalar2=self.cols[:, 7:8], op0=ALU.subtract, op1=ALU.mult),
                inc="v")
            x13 = self.x1_t[:].rearrange("p (b k) -> p b k", k=32)
            sch.emit("vector", lambda e, x13=x13: e.tensor_reduce(
                out=mq[:], in_=x13, axis=AX.X, op=ALU.max,
                apply_absolute_value=True), gap=1)
            sch.emit("vector", lambda e: e.tensor_scalar(
                out=s2q[:], in0=mq[:], scalar1=C224, scalar2=2e-30,
                op0=ALU.mult, op1=ALU.max), gap=1)
            sch.emit("vector", lambda e: e.reciprocal(out=rrq[:],
                                                      in_=s2q[:]), gap=1)
            self.gap(3)
            rrb = rrq[:].unsqueeze(2).broadcast_to([128, 64, 32])
            s2b = s2q[:].unsqueeze(2).broadcast_to([128, 64, 32])
            sch.emit("vector", lambda e, x13=x13, rrb=rrb: e.tensor_tensor(
                out=xq8[:], in0=x13, in1=rrb, op=ALU.mult))
            sch.emit("vector", lambda e, s2b=s2b: e.tensor_tensor(
                out=xq_bf[:], in0=xq8[:], in1=s2b, op=ALU.mult),
                inc="v", gap=1)
            vq = sch.now("v")
            sch.dma("sync", lambda e, tt=tt: e.dma_start(
                out=self.d_x1.ap()[tt * 128:(tt + 1) * 128, :], in_=self.x1_t[:]),
                waits=[("v", vq)])
            sch.dma("sync", lambda e, tt=tt: e.dma_start(
                out=self.d_xq.ap()[tt * 128:(tt + 1) * 128, :], in_=xq_bf[:]),
                waits=[("v", vq)])
            if dbg_x1:
                sch.dma("sync", lambda e, tt=tt: e.dma_start(
                    out=self.dbg["x1"].ap()[tt * 128:(tt + 1) * 128, :],
                    in_=self.x1_t[:]), waits=[("v", vq)])
            self.a4_store_ds.append(sch.now("dS"))
            if "ffn" not in self.phases:
                sch.dma("sync", lambda e, tt=tt: e.dma_start(
                    out=self.o_y.ap()[tt * 128:(tt + 1) * 128, :],
                    in_=self.x1_t[:]), waits=[("v", vq)])

    def phase_f(self, sc):
        sch, W, nc = self.sch, self.W, self.nc
        SB = lambda n, sh, dt: sc.enter_context(nc.sbuf_tensor(n, sh, dt))
        self.xt = [SB("xt_af", [128, H], F32), SB("xt_bf", [128, H], F32)]
        xfm_q = self.xfm
        wbig = SB("wbig", [128, 64, 512], BF16)
        hqT = SB("hqT", [128, 64, TSUP], BF16)
        sil = SB("sil", [128, 512], F32)
        h_sb = SB("h_sb", [128, 512], F32)
        mh = SB("mh", [128, 16], F32)
        s2h = SB("s2h", [128, 16], F32)
        rrh = SB("rrh", [128, 16], F32)
        h8 = SB("h8", [128, 512], FP8)
        hq = SB("hq", [128, 2, 512], BF16)
        r2c = SB("r2c", [128, 2, 512], F32)
        sqf = SB("sqf", [128, 512], F32)

        for sup in range(NSUP):
            a4ds = self.a4_store_ds[sup * 4 + 3]
            tprev = sch.now("t")
            for hc in range(NHC):
                out = xfm_q[:, hc, :]
                in_ = self.d_xq.ap()[sup * TSUP:(sup + 1) * TSUP,
                                     hc * 128:(hc + 1) * 128]
                w = [("dS", a4ds)]
                if sup > 0 and hc == 0:
                    w.append(("t", tprev))
                sch.dma("sync", lambda e, o=out, i=in_: e.dma_start_transpose(
                    out=o, in_=i), waits=w)
            XQ = sch.now("dS")

            mm_done = {}
            for ic in range(16):
                q = (ic % 2) * 32
                w = [self.F0_DG]
                if ic - 2 in mm_done:
                    w.append(("t", mm_done[ic - 2]))
                if sup == 0 and ic == 0:
                    w.append(("v", sch.now("v")))
                    w.append(("s", sch.now("s")))
                elif sup > 0 and ic == 0:
                    w.append(("t", tprev))
                sch.dma("sync", lambda e, ic=ic, q=q: e.dma_start(
                    out=wbig[:, q:q + 16, :],
                    in_=self.d_gw.ap()[:, ic * 512:(ic + 1) * 512].rearrange(
                        "(c p) m -> p c m", p=128)), waits=w)
                sch.dma("sync", lambda e, ic=ic, q=q: e.dma_start(
                    out=wbig[:, q + 16:q + 32, :],
                    in_=self.d_uw.ap()[:, ic * 512:(ic + 1) * 512].rearrange(
                        "(c p) m -> p c m", p=128)))
                GW = sch.now("dS")
                for t4 in range(4):
                    pg = self.ps_big[:, (t4 % 2) * 2, :]
                    pu = self.ps_big[:, (t4 % 2) * 2 + 1, :]
                    wmm = [("dS", max(GW, XQ)), ("v", sch.now("v")),
                           ("s", sch.now("s"))]

                    def fn(e, t4=t4, pg=pg, pu=pu, q=q):
                        last = None
                        for hc in range(NHC):
                            lh = xfm_q[:, hc, t4 * 128:(t4 + 1) * 128]
                            e.matmul(pg, lh, wbig[:, q + hc, :],
                                     start=(hc == 0), stop=(hc == NHC - 1))
                            last = e.matmul(pu, lh, wbig[:, q + 16 + hc, :],
                                            start=(hc == 0),
                                            stop=(hc == NHC - 1))
                        return last
                    sch.emit("tensor", fn, waits=wmm, inc="t")
                    tgu = sch.now("t")
                    sch.emit("scalar", lambda e, pg=pg: e.activation(
                        out=sil[:], in_=pg, func=ACT.Silu),
                        waits=[("t", tgu), ("v", sch.now("v"))], inc="s")
                    sn = sch.now("s")
                    sch.emit("vector", lambda e, pu=pu: e.tensor_tensor(
                        out=h_sb[:], in0=sil[:], in1=pu, op=ALU.mult),
                        waits=[("s", sn)], inc="v")
                    h3 = h_sb[:].rearrange("p (b k) -> p b k", k=32)
                    sch.emit("vector", lambda e, h3=h3: e.tensor_reduce(
                        out=mh[:], in_=h3, axis=AX.X, op=ALU.max,
                        apply_absolute_value=True), gap=1)
                    sch.emit("vector", lambda e: e.tensor_scalar(
                        out=s2h[:], in0=mh[:], scalar1=C224, scalar2=2e-30,
                        op0=ALU.mult, op1=ALU.max), gap=1)
                    sch.emit("vector", lambda e: e.reciprocal(out=rrh[:],
                                                              in_=s2h[:]),
                             gap=1)
                    self.gap(3)
                    rrb = rrh[:].unsqueeze(2).broadcast_to([128, 16, 32])
                    s2b = s2h[:].unsqueeze(2).broadcast_to([128, 16, 32])
                    sch.emit("vector", lambda e, h3=h3, rrb=rrb: e.tensor_tensor(
                        out=h8[:], in0=h3, in1=rrb, op=ALU.mult))
                    sch.emit("vector", lambda e, s2b=s2b, t4=t4: e.tensor_tensor(
                        out=hq[:, t4 % 2, :], in0=h8[:], in1=s2b, op=ALU.mult),
                        inc="v", gap=1)
                    vh = sch.now("v")
                    wtr = [("v", vh), ("s", sch.now("s"))]

                    def ftr(e, t4=t4):
                        last = None
                        for isub in range(4):
                            last = e.transpose(
                                self.pt0[:, isub * 128:(isub + 1) * 128],
                                hq[:, t4 % 2, isub * 128:(isub + 1) * 128],
                                self.ident[:])
                        return last
                    sch.emit("tensor", ftr, waits=wtr, inc="t")
                    ttr = sch.now("t")

                    def fev(e, ic=ic, t4=t4):
                        last = None
                        for isub in range(4):
                            last = e.copy(
                                out=hqT[:, ic * 4 + isub,
                                        t4 * 128:(t4 + 1) * 128],
                                in_=self.pt0[:, isub * 128:(isub + 1) * 128])
                        return last
                    sch.emit("scalar", fev, waits=[("t", ttr)], inc="s")
                mm_done[ic] = sch.now("t")

            SEVAC = sch.now("s")
            TGU_ALL = sch.now("t")
            dl_done = {}
            r2_store_ds = {}
            for hcd in range(4):
                for hf in range(2):
                    w = [self.F0_DG]
                    key = hcd * 2 + hf
                    if key == 0:
                        w.append(("t", TGU_ALL))
                    elif key - 2 in dl_done:
                        w.append(("t", dl_done[key - 2]))
                    rows = slice(hf * 4096, (hf + 1) * 4096)
                    sch.dma("sync", lambda e, hcd=hcd, hf=hf, rows=rows: e.dma_start(
                        out=wbig[:, hf * 32:(hf + 1) * 32, :],
                        in_=self.d_dw.ap()[rows, hcd * 512:(hcd + 1) * 512]
                        .rearrange("(c p) m -> p c m", p=128)), waits=w)
                    dlw = sch.now("dS")
                    for t4 in range(4):
                        tt = sup * 4 + t4
                        xt = self.xt[(hcd * 4 + t4) % 2]
                        if hf == 1:
                            sch.dma("sync", lambda e, tt=tt, xt=xt, hcd=hcd:
                                    e.dma_start(
                                out=xt[:, hcd * 512:(hcd + 1) * 512],
                                in_=self.d_x1.ap()[tt * 128:(tt + 1) * 128,
                                                   hcd * 512:(hcd + 1) * 512]),
                                waits=[("v", sch.now("v"))])
                        X1 = sch.now("dS")
                        pbank = self.ps_big[:, t4, :]
                        wmm = [("dS", dlw),
                               ("s", SEVAC if (key == 0 and t4 == 0)
                                else sch.now("s")),
                               ("v", sch.now("v"))]

                        def fn(e, t4=t4, pbank=pbank, hf=hf):
                            last = None
                            for i32 in range(32):
                                icd = hf * 32 + i32
                                last = e.matmul(
                                    pbank,
                                    hqT[:, icd, t4 * 128:(t4 + 1) * 128],
                                    wbig[:, hf * 32 + i32, :],
                                    start=(icd == 0), stop=(icd == 63))
                            return last
                        sch.emit("tensor", fn, waits=wmm, inc="t")
                        tdn = sch.now("t")
                        if hf == 1:
                            sl = t4 % 2
                            wv = [("t", tdn), ("dS", X1)]
                            if (hcd, t4 - 2) in r2_store_ds:
                                wv.append(("dS", r2_store_ds[(hcd, t4 - 2)]))
                            elif hcd > 0 and (hcd - 1, t4 + 2) in r2_store_ds:
                                wv.append(("dS", r2_store_ds[(hcd - 1, t4 + 2)]))
                            sch.emit("vector", lambda e, pbank=pbank, sl=sl,
                                     xt=xt, hcd=hcd: e.tensor_tensor(
                                out=r2c[:, sl, :], in0=pbank,
                                in1=xt[:, hcd * 512:(hcd + 1) * 512],
                                op=ALU.add),
                                waits=wv, inc="v")
                            sch.emit("vector", lambda e, sl=sl: e.tensor_reduce(
                                out=self.cols[:, 8:9], in_=r2c[:, sl, :],
                                axis=AX.X, op=ALU.add), gap=1)
                            sch.emit("vector", lambda e, tt=tt: e.tensor_tensor(
                                out=self.ln_s[:, tt:tt + 1],
                                in0=self.ln_s[:, tt:tt + 1],
                                in1=self.cols[:, 8:9], op=ALU.add), gap=1)
                            sch.emit("scalar", lambda e, sl=sl: e.activation(
                                out=sqf[:], in_=r2c[:, sl, :],
                                func=ACT.Square, accum_out=self.cols[:, 9:10]),
                                waits=W("v"), inc="s")
                            sn2 = sch.now("s")
                            sch.emit("vector", lambda e, tt=tt: e.tensor_tensor(
                                out=self.ln_q[:, tt:tt + 1],
                                in0=self.ln_q[:, tt:tt + 1],
                                in1=self.cols[:, 9:10], op=ALU.add),
                                waits=[("s", sn2)], inc="v")
                            vr2 = sch.now("v")
                            sch.dma("sync", lambda e, tt=tt, sl=sl, hcd=hcd:
                                    e.dma_start(
                                out=self.d_r2.ap()[tt * 128:(tt + 1) * 128,
                                                   hcd * 512:(hcd + 1) * 512],
                                in_=r2c[:, sl, :]), waits=[("v", vr2)])
                            r2_store_ds[(hcd, t4)] = sch.now("dS")
                    dl_done[key] = sch.now("t")
            R2D = sch.now("dS")

            for t4 in range(4):
                tt = sup * 4 + t4
                xt = self.xt[t4 % 2]
                sch.dma("sync", lambda e, tt=tt, xt=xt: e.dma_start(
                    out=xt[:], in_=self.d_r2.ap()[tt * 128:(tt + 1) * 128, :]),
                    waits=[("dS", R2D), ("v", sch.now("v"))])
                dsn = sch.now("dS")
                sch.emit("vector", lambda e, tt=tt: e.tensor_scalar(
                    out=self.cols[:, 2:3], in0=self.ln_s[:, tt:tt + 1],
                    scalar1=CH, scalar2=None, op0=ALU.mult),
                    waits=[("dS", dsn)])
                sch.emit("vector", lambda e, tt=tt: e.tensor_scalar(
                    out=self.cols[:, 4:5], in0=self.ln_q[:, tt:tt + 1],
                    scalar1=CH, scalar2=None, op0=ALU.mult))
                sch.emit("vector", lambda e: e.tensor_tensor(
                    out=self.cols[:, 3:4], in0=self.cols[:, 2:3],
                    in1=self.cols[:, 2:3], op=ALU.mult))
                sch.emit("vector", lambda e: e.tensor_tensor(
                    out=self.cols[:, 5:6], in0=self.cols[:, 4:5],
                    in1=self.cols[:, 3:4], op=ALU.subtract), gap=1)
                sch.emit("vector", lambda e: e.tensor_scalar(
                    out=self.cols[:, 6:7], in0=self.cols[:, 5:6],
                    scalar1=EPS_LN, scalar2=None, op0=ALU.add), inc="v", gap=1)
                sch.emit("scalar", lambda e: e.activation(
                    out=self.cols[:, 6:7], in_=self.cols[:, 6:7],
                    func=ACT.Sqrt), waits=W("v"), inc="s")
                sn = sch.now("s")
                sch.emit("vector", lambda e: e.reciprocal(
                    out=self.cols[:, 7:8], in_=self.cols[:, 6:7]),
                    waits=[("s", sn)])
                self.gap(3)
                sch.emit("vector", lambda e, xt=xt: e.tensor_scalar(
                    out=self.x1_t[:], in0=xt[:], scalar1=self.cols[:, 2:3],
                    scalar2=self.cols[:, 7:8], op0=ALU.subtract, op1=ALU.mult),
                    inc="v")
                vy = sch.now("v")
                sch.dma("sync", lambda e, tt=tt: e.dma_start(
                    out=self.o_y.ap()[tt * 128:(tt + 1) * 128, :],
                    in_=self.x1_t[:]), waits=[("v", vy)])


_NC_CACHE = {}


def _get_nc(debug=(), phases=("attn", "ffn"), paranoid=False):
    key = (tuple(sorted(debug)), tuple(sorted(phases)), paranoid)
    if key not in _NC_CACHE:
        kb = KB(debug=debug, phases=phases, paranoid=paranoid)
        _NC_CACHE[key] = (kb.build(), dict(kb.dbg), kb.cs)
    return _NC_CACHE[key]


def kernel(**inputs):
    nc, dbg, cs = _get_nc(paranoid=False)
    x_full = np.ascontiguousarray(inputs["L0_attn_hidden_states"])
    base = {
        "i_wq": np.ascontiguousarray(inputs["L0_attn_Wq"]),
        "i_wlk": np.ascontiguousarray(inputs["L0_attn_Wlk"]),
        "i_wlv": np.ascontiguousarray(inputs["L0_attn_Wlv"]),
        "i_wo": np.ascontiguousarray(inputs["L0_attn_Wo"]),
        "i_wk": np.ascontiguousarray(inputs["L0_attn_Wk"]),
        "i_wv": np.ascontiguousarray(inputs["L0_attn_Wv"]),
        "i_wc": np.ascontiguousarray(
            inputs["L0_attn_Wc"].reshape(NHC, 128).T),
        "i_bq": np.ascontiguousarray(
            inputs["L0_attn_bq"].reshape(NHC, 128).T),
        "i_blk": np.ascontiguousarray(
            inputs["L0_attn_blk"].reshape(NHC, 128).T),
        "i_blv": np.ascontiguousarray(inputs["L0_attn_blv"]),
        "i_bo": np.ascontiguousarray(inputs["L0_attn_bo"]),
        "i_bk": np.ascontiguousarray(inputs["L0_attn_bk"]),
        "i_bv": np.ascontiguousarray(inputs["L0_attn_bv"]),
        "i_bc": np.ascontiguousarray(inputs["L0_attn_bc"]),
        "i_qnw": np.ascontiguousarray(inputs["L0_attn_qn_w"]),
        "i_knw": np.ascontiguousarray(inputs["L0_attn_kn_w"]),
        "i_sk": np.ascontiguousarray(inputs["L0_attn_sink_k"].T),
        "i_sv": np.ascontiguousarray(inputs["L0_attn_sink_v"]),
        "i_gw": np.ascontiguousarray(inputs["L0_ffn_gate_w"]),
        "i_uw": np.ascontiguousarray(inputs["L0_ffn_up_w"]),
        "i_dw": np.ascontiguousarray(inputs["L0_ffn_down_w"]),
    }
    base.update(cs)
    in_maps = []
    for c in range(B):
        m = dict(base)
        m["i_x"] = np.ascontiguousarray(x_full[c])
        in_maps.append(m)
    res = run_bass_kernel_spmd(nc, in_maps, core_ids=list(range(B)))
    return np.stack([r["o_y"] for r in res.results], axis=0)
